# version 36
# baseline (speedup 1.0000x reference)
import numpy as np
import concourse.bass as bass
import concourse.mybir as mybir
import concourse.bass_utils as _bu
from concourse.bass_utils import run_bass_kernel_spmd

if not getattr(_bu, "_ldwopt_patched", False):
    _orig_run_command = _bu.run_command

    def _run_command_ldwopt(argv, **kwargs):
        argv = ["--enable-ldw-opt=true" if a == "--enable-ldw-opt=false" else a
                for a in argv]
        return _orig_run_command(argv, **kwargs)

    _bu.run_command = _run_command_ldwopt
    _bu._ldwopt_patched = True

F32 = mybir.dt.float32
BF16 = mybir.dt.bfloat16
AF = mybir.ActivationFunctionType
ALU = mybir.AluOpType
AXX = mybir.AxisListType.X

D = 1024
HEADS = 4
HDK = 256
HDV = 512
R = 16
C = 128
EPS = 1e-5
LN16 = float(np.log(np.float32(16.0)))


class Prog:
    ENGINES = ("tensor", "vector", "scalar", "gpsimd", "sync")

    def __init__(self, nc):
        self.nc = nc
        self.sem = {e: nc.alloc_semaphore(f"S{e}") for e in self.ENGINES}
        self.stream_sems = {}
        self.cnt = {id(s): 0 for s in self.sem.values()}
        self.sem_by_id = {id(s): s for s in self.sem.values()}
        self.ops = {e: [] for e in self.ENGINES}
        self.res = {}
        self.waited = {e: {} for e in self.ENGINES}
        self.n_ops = 0
        self.n_waits = 0

    def _stream_sem(self, stream):
        s = self.stream_sems.get(stream)
        if s is None:
            s = self.nc.alloc_semaphore(f"D_{stream}")
            self.stream_sems[stream] = s
            self.cnt[id(s)] = 0
            self.sem_by_id[id(s)] = s
        return s

    def _deps(self, engine, reads, writes):
        deps = {}

        def add(tag):
            if tag is None:
                return
            sid, val, is_dma = tag
            if is_dma:
                val = self.cnt[sid]
            if deps.get(sid, 0) < val:
                deps[sid] = val

        for r in reads:
            e = self.res.get(r)
            if e:
                add(e["w"])
        for w in writes:
            e = self.res.get(w)
            if e:
                add(e["w"])
                for t in e["r"]:
                    add(t)
        waits = []
        wt = self.waited[engine]
        for sid, val in deps.items():
            if wt.get(sid, 0) < val:
                waits.append((sid, val))
                wt[sid] = val
        return waits

    def op(self, engine, fn, reads=(), writes=(), dma=False, stream=None):
        if engine == "scalar":
            reads = list(reads) + ["cbias"]
        waits = self._deps(engine, reads, writes)
        self.n_waits += len(waits)
        if dma:
            s = self._stream_sem(stream or ("dma_" + engine))
            self.cnt[id(s)] += 16
            tag = (id(s), self.cnt[id(s)], True)
            inc_sem, inc_by = s, 16
        else:
            s = self.sem[engine]
            self.cnt[id(s)] += 1
            tag = (id(s), self.cnt[id(s)], False)
            inc_sem, inc_by = s, 1
        for w in writes:
            self.res[w] = {"w": tag, "r": []}
        for r in reads:
            e = self.res.setdefault(r, {"w": None, "r": []})
            e["r"].append(tag)
        self.ops[engine].append((waits, fn, inc_sem, inc_by))
        self.n_ops += 1
        return tag

    def wait_all_streams(self, engine, streams=None):
        names = list(self.stream_sems) if streams is None else list(streams)
        targets = [(id(self.stream_sems[n]), self.cnt[id(self.stream_sems[n])])
                   for n in names if n in self.stream_sems]

        def fn(eng):
            for sid, val in targets:
                if val > 0:
                    eng.wait_ge(self.sem_by_id[sid], val)
            return eng.nop()

        self.ops[engine].append(([], fn, self.sem[engine], 1))
        self.cnt[id(self.sem[engine])] += 1
        self.n_ops += 1

    def build(self):
        nc = self.nc
        with nc.Block() as block:
            for engine in self.ENGINES:
                ops = self.ops[engine]
                if not ops:
                    continue
                dec = getattr(block, engine)

                def make(ops):
                    def body(eng):
                        for waits, fn, inc_sem, inc_by in ops:
                            for sid, val in waits:
                                eng.wait_ge(self.sem_by_id[sid], val)
                            inst = fn(eng)
                            if inst is not None:
                                inst.then_inc(inc_sem, inc_by)
                    return body

                dec(make(list(ops)))
        return nc


def build_program(n_tok=2048, tile_tok=512, n_heads=HEADS):
    assert tile_tok % C == 0 and n_tok % tile_tok == 0
    n_tiles = n_tok // tile_tok
    JC = tile_tok // C
    NH2 = tile_tok // 256

    nc = bass.Bass(trn_type="TRN2", target_bir_lowering=False, debug=False)
    din = lambda name, shp: nc.dram_tensor(name, shp, F32, kind="ExternalInput").ap()
    x_hbm = din("x", [n_tok, D])
    wq_l = din("wq_l", [8, 128, 1024])
    wk_l = din("wk_l", [8, 128, 1024])
    wv_l = din("wv_l", [8, 128, 2048])
    wg_l = din("wg_l", [8, 128, 2048])
    wo_l = din("wo_l", [8, 128, 2048])
    w1_l = din("w1_l", [16, 128, 2048])
    w2_l = din("w2_l", [16, 128, 2048])
    wgk1_l = din("wgk1_l", [128, 8 * R])
    dscr = lambda name, shp: nc.dram_tensor(name, shp, BF16, kind="Internal").ap()
    wq_hi = dscr("wq_hi", [8, 128, 1024]); wq_lo = dscr("wq_lo", [8, 128, 1024])
    wk_hi = dscr("wk_hi", [8, 128, 1024]); wk_lo = dscr("wk_lo", [8, 128, 1024])
    wv_hi = dscr("wv_hi", [8, 128, 2048]); wv_lo = dscr("wv_lo", [8, 128, 2048])
    wg_hi = dscr("wg_hi", [8, 128, 2048]); wg_lo = dscr("wg_lo", [8, 128, 2048])
    wo_hi = dscr("wo_hi", [8, 128, 2048]); wo_lo = dscr("wo_lo", [8, 128, 2048])
    w1_hi = dscr("w1_hi", [16, 128, 2048]); w1_lo = dscr("w1_lo", [16, 128, 2048])
    w2_hi = dscr("w2_hi", [16, 128, 2048]); w2_lo = dscr("w2_lo", [16, 128, 2048])
    wgk2_h = din("wgk2", [R, 1024])
    bgk2_l = din("bgk2_l", [128, 8])
    b1_l = din("b1_l", [128, 32])
    b2_l = din("b2_l", [1, D])
    gnw_l = din("gnw_l", [128, 4])
    out_hbm = nc.dram_tensor("out", [n_tok, D], F32, kind="ExternalOutput").ap()

    S_ = lambda name, shp, d=F32: nc.alloc_sbuf_tensor(name, shp, d)

    ident = S_("c_ident", [128, 128])
    triu = S_("c_triu", [128, 128])
    ones128 = S_("c_ones", [128, 128])
    onesrow = S_("c_onesrow", [1, 128])
    wgk2 = S_("r_wgk2", [16, 1024])
    bgk2n = S_("r_bgk2n", [128, 8])
    b1f = S_("r_b1f", [128, 32])
    b2row = S_("r_b2row", [1, D])
    b2bc = S_("r_b2bc", [128, D])
    gnwf = S_("r_gnwf", [128, 4])
    Sst = [[S_(f"st_S_{h}_{c2}", [128, HDV]) for c2 in range(2)] for h in range(n_heads)]

    x_c = [S_(f"t_x{i}", [128, D]) for i in range(2)]
    xnf_h = [S_(f"t_xnfh{k}", [128, tile_tok], BF16) for k in range(8)]
    xnf_l = [S_(f"t_xnfl{k}", [128, tile_tok], BF16) for k in range(8)]
    attn_c = [S_(f"t_attn{j}", [128, D]) for j in range(JC)]
    qf = S_("h_qf", [128, 2 * tile_tok])
    kf = S_("h_kf", [128, 2 * tile_tok])
    spf = S_("h_spf", [128, 2 * tile_tok])
    vth = [S_(f"h_v{j}", [128, HDV]) for j in range(JC)]
    gth = [S_(f"h_g{j}", [128, HDV]) for j in range(JC)]
    ogh_h = [S_(f"h_ogh{j}", [128, HDV], BF16) for j in range(JC)]
    ogh_l = [S_(f"h_ogl{j}", [128, HDV], BF16) for j in range(JC)]
    u_sb = S_("h_u", [16, tile_tok])
    at = [S_(f"k_at{i}", [128, 2 * C]) for i in range(2)]
    qt = [S_(f"k_qt{i}", [128, 2 * C]) for i in range(2)]
    kt = [S_(f"k_kt{i}", [128, 2 * C]) for i in range(2)]
    khf = [S_(f"k_khf{i}", [128, 2 * C]) for i in range(2)]
    khtm = S_("k_khtm", [128, 2 * C])
    alast = [S_(f"k_alast{i}", [128, 2]) for i in range(2)]
    p_sm = S_("k_p", [128, C])
    o_tm = [S_(f"k_otm{i}", [128, HDV]) for i in range(2)]
    sil_s = S_("k_sils", [128, HDV])
    gg = S_("k_gg", [128, HDV])
    bscr = S_("k_bscr", [128, D], BF16)
    xfull = S_("a_xfull", [128, 512])
    negml = S_("a_negml", [128, 2])
    ln_sx = [S_(f"k_sx{i}", [128, 1]) for i in range(2)]
    ln_sxx = [S_(f"k_sxx{i}", [128, 1]) for i in range(2)]
    ln_m = [S_(f"k_m{i}", [128, 1]) for i in range(2)]
    ln_v = [S_(f"k_v{i}", [128, 1]) for i in range(2)]
    ln_inv = [S_(f"k_inv{i}", [128, 1]) for i in range(2)]
    xn_tm = S_("k_xntm", [128, D])
    ssum = S_("k_ssum", [128, 1])
    srms = S_("k_srms", [128, 1])
    wqh_cs = [S_(f"w_wqh{i}", [128, 1024], BF16) for i in range(2)]
    wql_cs = [S_(f"w_wql{i}", [128, 1024], BF16) for i in range(2)]
    wkh_cs = [S_(f"w_wkh{i}", [128, 1024], BF16) for i in range(2)]
    wkl_cs = [S_(f"w_wkl{i}", [128, 1024], BF16) for i in range(2)]
    wvh_cs = [S_(f"w_wvh{i}", [128, 2048], BF16) for i in range(2)]
    wvl_cs = [S_(f"w_wvl{i}", [128, 2048], BF16) for i in range(2)]
    wgh_cs = [S_(f"w_wgh{i}", [128, 2048], BF16) for i in range(2)]
    wgl_cs = [S_(f"w_wgl{i}", [128, 2048], BF16) for i in range(2)]
    woh_cs = [S_(f"w_woh{i}", [128, 2048], BF16) for i in range(2)]
    wol_cs = [S_(f"w_wol{i}", [128, 2048], BF16) for i in range(2)]
    wgk1h = S_("r_wgk1h", [128, 8 * R], BF16)
    wgk1l = S_("r_wgk1l", [128, 8 * R], BF16)
    h1h = S_("f_h1h", [128, 4 * tile_tok], BF16)
    h1l = S_("f_h1l", [128, 4 * tile_tok], BF16)

    pb = [nc.alloc_psum_tensor(f"pb{i}", [128, 512], F32) for i in range(8)]
    PB = [f"pb{i}" for i in range(8)]

    p = Prog(nc)
    OP = p.op

    cbias_vals = [0.0, 1.0, EPS, -LN16]
    cbias = S_("c_bias2", [128, len(cbias_vals)])
    for i, v in enumerate(cbias_vals):
        OP("gpsimd", lambda g, i=i, v=v: g.memset(cbias[:, i:i + 1], float(v)),
           writes=["cbias"])
        nc.const_aps.register(F32, float(v), cbias[:, i:i + 1])

    OP("gpsimd", lambda g: g.memset(ident[:], 1.0), writes=["ident"])
    OP("gpsimd", lambda g: g.affine_select(out=ident[:], in_=ident[:],
        compare_op=ALU.is_equal, fill=0.0, base=0, pattern=[[1, 128]],
        channel_multiplier=-1), reads=["ident"], writes=["ident"])
    OP("gpsimd", lambda g: g.memset(triu[:], 1.0), writes=["triu"])
    OP("gpsimd", lambda g: g.affine_select(out=triu[:], in_=triu[:],
        compare_op=ALU.is_ge, fill=0.0, base=0, pattern=[[1, 128]],
        channel_multiplier=-1), reads=["triu"], writes=["triu"])
    OP("gpsimd", lambda g: g.memset(ones128[:], 1.0), writes=["ones128"])
    OP("gpsimd", lambda g: g.memset(onesrow[:], 1.0), writes=["onesrow"])
    for h in range(n_heads):
        for c2 in range(2):
            OP("gpsimd", lambda g, h=h, c2=c2: g.memset(Sst[h][c2][:], 0.0),
               writes=[f"S{h}{c2}"])
    OP("sync", lambda s: s.dma_start(out=wgk2[:], in_=wgk2_h),
       writes=["wgk2"], dma=True, stream="cst_wgk2")
    OP("sync", lambda s: s.dma_start(out=bgk2n[:], in_=bgk2_l),
       writes=["bgk2n"], dma=True, stream="cst_bgk2n")
    OP("vector", lambda v: v.tensor_scalar(out=bgk2n[:], in0=bgk2n[:],
       scalar1=-1.0, scalar2=None, op0=ALU.mult), reads=["bgk2n"], writes=["bgk2n"])
    OP("sync", lambda s: s.dma_start(out=b1f[:], in_=b1_l),
       writes=["b1f"], dma=True, stream="cst_b1f")
    OP("sync", lambda s: s.dma_start(out=b2row[:], in_=b2_l),
       writes=["b2row"], dma=True, stream="cst_b2row")
    OP("sync", lambda s: s.dma_start(out=gnwf[:], in_=gnw_l),
       writes=["gnwf"], dma=True, stream="cst_gnwf")

    def bcast_row(dst, dkey, srcrow, skey, width, bank):
        for g0 in range(0, width, 512):
            w = min(512, width - g0)

            def mmg(t, g0=g0, w=w):
                last = None
                for c0 in range(0, w, 256):
                    last = t.matmul(pb[bank][:, c0:c0 + 256], onesrow[0:1, :],
                                    srcrow[0:1, g0 + c0:g0 + c0 + 256],
                                    start=True, stop=True)
                return last
            OP("tensor", mmg, reads=[skey, "onesrow"], writes=[PB[bank]])
            OP("vector", lambda v, g0=g0, w=w: v.tensor_copy(
               dst[:, g0:g0 + w], pb[bank][:, 0:w]),
               reads=[PB[bank]], writes=[dkey])

    bcast_row(b2bc, "b2bc", b2row, "b2row", D, 6)

    split_jobs = []
    PWS = 512
    def add_split(matname, mat_fp32, hi, lo, nslab, W):
        for s in range(nslab):
            for c0 in range(0, W, PWS):
                dc = (c0 // 512) if matname == "wo" else None
                split_jobs.append((matname, mat_fp32[s, :, c0:c0 + PWS],
                                   hi[s, :, c0:c0 + PWS], lo[s, :, c0:c0 + PWS],
                                   dc))
    add_split("wq", wq_l, wq_hi, wq_lo, 8, 1024)
    add_split("wk", wk_l, wk_hi, wk_lo, 8, 1024)
    add_split("wv", wv_l, wv_hi, wv_lo, 8, 2048)
    add_split("wg", wg_l, wg_hi, wg_lo, 8, 2048)
    add_split("wo", wo_l, wo_hi, wo_lo, 8, 2048)
    add_split("w1", w1_l, w1_hi, w1_lo, 16, 2048)
    add_split("w2", w2_l, w2_hi, w2_lo, 16, 2048)
    _mat_order = ["wq", "wk", "wv", "wg", "wo", "w1", "w2"]
    SPLIT_PIECE_KEYS = [f"splh{i}" for i in range(12)] + [f"spll{i}" for i in range(12)]
    SCRK = lambda mn: ([f"scr_{mn}_h{s}" for s in range(12)]
                       + [f"scr_{mn}_l{s}" for s in range(12)])
    split_ctr = [0]

    def emit_split_until(matname):
        n = 0
        for i in range(split_ctr[0], len(split_jobs)):
            if split_jobs[i][0] == matname or _mat_order.index(split_jobs[i][0]) < _mat_order.index(matname):
                n += 1
            else:
                break
        emit_split(n)

    def emit_split(n):
        i0 = split_ctr[0]
        n_land = 8
        def landing(li):
            if li < 4:
                return vth[li][:], f"v{li}"
            return gth[li - 4][:], f"g{li - 4}"
        hi_all = [wvh_cs[0], wvh_cs[1], wgh_cs[0]]
        lo_all = [wvl_cs[0], wvl_cs[1], wgl_cs[0]]
        for i in range(i0, min(i0 + n, len(split_jobs))):
            mn, srcp, hip, lop, wodc = split_jobs[i]
            lb, lbk = landing(i % n_land)
            sslot = i % 12
            hwr = [f"scr_{mn}_h{sslot}"]
            lwr = [f"scr_{mn}_l{sslot}"]
            if mn in ("w1", "w2"):
                sw = i % 4
                hcol = h1h[:, sw * 512:(sw + 1) * 512]
                lcol = h1l[:, sw * 512:(sw + 1) * 512]
                hkey = f"h1h_{sw}"; lkey = f"h1_{sw}"
                OP("gpsimd", lambda g, srcp=srcp, lb=lb: g.dma_start(
                   out=lb, in_=srcp),
                   writes=[lbk], dma=True, stream=f"spl12_{i % n_land}")
                OP("gpsimd", lambda g, lb=lb, hcol=hcol: g.tensor_copy(
                   out=hcol, in_=lb),
                   reads=[lbk], writes=[hkey])
                OP("gpsimd", lambda g, lb=lb, hcol=hcol, lcol=lcol:
                   g.tensor_tensor(out=lcol, in0=lb, in1=hcol,
                   op=ALU.subtract),
                   reads=[lbk, hkey], writes=[lkey])
                OP("gpsimd", lambda g, hip=hip, hcol=hcol: g.dma_start(
                   out=hip, in_=hcol),
                   reads=[hkey], writes=hwr, dma=True, stream=f"s12h{sw}")
                OP("gpsimd", lambda g, lop=lop, lcol=lcol: g.dma_start(
                   out=lop, in_=lcol),
                   reads=[lkey], writes=lwr, dma=True, stream=f"s12l{sw}")
                continue
            hcol = hi_all[sslot // 4][:, (sslot % 4) * 512:(sslot % 4) * 512 + 512]
            lcol = lo_all[sslot // 4][:, (sslot % 4) * 512:(sslot % 4) * 512 + 512]
            hkey = f"splh{sslot}"; lkey = f"spll{sslot}"
            OP("sync", lambda s, srcp=srcp, lb=lb: s.dma_start(out=lb, in_=srcp),
               writes=[lbk], dma=True, stream=f"spl{i % n_land}")
            scl = 1.0 if wodc is None else gnwf[:, wodc:wodc + 1]
            srd = [] if wodc is None else ["gnwf"]
            OP("scalar", lambda sc, lb=lb, hcol=hcol, scl=scl: sc.activation(
               out=hcol, in_=lb, func=AF.Copy, bias=0.0, scale=scl),
               reads=[lbk] + srd, writes=[hkey])
            OP("vector", lambda v, lb=lb, hcol=hcol, lcol=lcol, scl=scl:
               v.scalar_tensor_tensor(out=lcol, in0=lb, scalar=scl, in1=hcol,
               op0=ALU.mult, op1=ALU.subtract),
               reads=[lbk, hkey] + srd, writes=[lkey])
            OP("scalar", lambda sc, hip=hip, hcol=hcol: sc.dma_start(
               out=hip, in_=hcol),
               reads=[hkey], writes=hwr, dma=True, stream=f"sch{sslot}")
            OP("gpsimd", lambda g, lop=lop, lcol=lcol: g.dma_start(
               out=lop, in_=lcol),
               reads=[lkey], writes=lwr, dma=True, stream=f"scl{sslot}")
        split_ctr[0] = min(i0 + n, len(split_jobs))

    def emit_wgk1_split():
        OP("sync", lambda s: s.dma_start(out=vth[0][:, 0:128], in_=wgk1_l),
           writes=["v0"], dma=True, stream="splgk")
        OP("scalar", lambda sc: sc.activation(out=wgk1h[:], in_=vth[0][:, 0:128],
           func=AF.Copy, bias=0.0, scale=1.0), reads=["v0"], writes=["wgk1h"])
        OP("vector", lambda v: v.scalar_tensor_tensor(out=wgk1l[:],
           in0=vth[0][:, 0:128], scalar=1.0, in1=wgk1h[:],
           op0=ALU.mult, op1=ALU.subtract), reads=["v0", "wgk1h"],
           writes=["wgk1l"])

    def load_qk(mat, h, c2):
        bh, bl, hi, lo, key = {
            "wq": (wqh_cs, wql_cs, wq_hi, wq_lo, "wq"),
            "wk": (wkh_cs, wkl_cs, wk_hi, wk_lo, "wk")}[mat]
        s = 2 * h + c2
        OP("sync", lambda ss, s=s: ss.dma_start(out=bh[c2][:], in_=hi[s]),
           reads=SCRK(key), writes=[f"{key}hcs{c2}"], dma=True, stream=f"{key}{c2}")
        OP("sync", lambda ss, s=s: ss.dma_start(out=bl[c2][:], in_=lo[s]),
           reads=SCRK(key), writes=[f"{key}lcs{c2}"], dma=True, stream=f"{key}{c2}")

    def load_vg(mat, h, half, ext=()):
        bh, bl, hi, lo, key = {
            "wv": (wvh_cs, wvl_cs, wv_hi, wv_lo, "wv"),
            "wg": (wgh_cs, wgl_cs, wg_hi, wg_lo, "wg")}[mat]
        s = 2 * h + half
        OP("sync", lambda ss, s=s: ss.dma_start(out=bh[half][:], in_=hi[s]),
           reads=SCRK(key), writes=[f"{key}hcs{half}"] + list(ext), dma=True,
           stream=f"{key}{half}")
        OP("sync", lambda ss, s=s: ss.dma_start(out=bl[half][:], in_=lo[s]),
           reads=SCRK(key), writes=[f"{key}lcs{half}"] + list(ext), dma=True,
           stream=f"{key}{half}")

    def load_w12(mat, jp):
        bh, bl, hi, lo, key, strm = {
            "w1": (wvh_cs, wvl_cs, w1_hi, w1_lo, "wv", "wv"),
            "w2": (wgh_cs, wgl_cs, w2_hi, w2_lo, "wg", "wg")}[mat]
        b = jp % 2
        OP("sync", lambda ss, jp=jp, b=b: ss.dma_start(out=bh[b][:], in_=hi[jp]),
           reads=SCRK(mat), writes=[f"{key}hcs{b}"], dma=True, stream=f"{strm}{b}")
        OP("sync", lambda ss, jp=jp, b=b: ss.dma_start(out=bl[b][:], in_=lo[jp]),
           reads=SCRK(mat), writes=[f"{key}lcs{b}"], dma=True, stream=f"{strm}{b}")

    def load_wo(h):
        for dpart in range(2):
            OP("sync", lambda s, h=h, dpart=dpart: s.dma_start(
               out=woh_cs[dpart][:], in_=wo_hi[2 * h + dpart]),
               reads=SCRK("wo"), writes=[f"wohcs{dpart}"], dma=True,
               stream=f"wo{dpart}")
            OP("sync", lambda s, h=h, dpart=dpart: s.dma_start(
               out=wol_cs[dpart][:], in_=wo_lo[2 * h + dpart]),
               reads=SCRK("wo"), writes=[f"wolcs{dpart}"], dma=True,
               stream=f"wo{dpart}")

    def emit_A0(a0tile):
        a0t0 = a0tile * tile_tok
        for j in range(JC):
            r0 = a0t0 + j * C
            xb = j % 2
            OP("sync", lambda s, xb=xb, r0=r0: s.dma_start(
               out=x_c[xb][:], in_=x_hbm[r0:r0 + C, :]),
               writes=[f"x{xb}"], dma=True, stream=f"x{xb}")
            OP("scalar", lambda sc, xb=xb: sc.activation(out=bscr[:], in_=x_c[xb][:],
               func=AF.Copy, bias=0.0, scale=1.0, accum_out=ln_sx[xb][:]),
               reads=[f"x{xb}", "cbias"], writes=["bscr", f"lsx{xb}"])
            OP("scalar", lambda sc, xb=xb: sc.activation(out=bscr[:], in_=x_c[xb][:],
               func=AF.Square, accum_out=ln_sxx[xb][:]),
               reads=[f"x{xb}", "cbias"], writes=["bscr", f"lsxx{xb}"])
            OP("gpsimd", lambda g, xb=xb: g.tensor_scalar(out=ln_m[xb][:],
               in0=ln_sx[xb][:], scalar1=1.0 / D, scalar2=None, op0=ALU.mult),
               reads=[f"lsx{xb}"], writes=[f"lm{xb}"])
            OP("gpsimd", lambda g, xb=xb: g.tensor_tensor(out=ln_v[xb][:],
               in0=ln_m[xb][:], in1=ln_m[xb][:], op=ALU.mult),
               reads=[f"lm{xb}"], writes=[f"lv{xb}"])
            OP("gpsimd", lambda g, xb=xb: g.tensor_scalar(out=ln_sxx[xb][:],
               in0=ln_sxx[xb][:], scalar1=1.0 / D, scalar2=None, op0=ALU.mult),
               reads=[f"lsxx{xb}"], writes=[f"lsxx{xb}"])
            OP("gpsimd", lambda g, xb=xb: g.tensor_tensor(out=ln_v[xb][:],
               in0=ln_sxx[xb][:], in1=ln_v[xb][:], op=ALU.subtract),
               reads=[f"lsxx{xb}", f"lv{xb}"], writes=[f"lv{xb}"])
            OP("scalar", lambda sc, xb=xb: sc.activation(out=ln_inv[xb][:],
               in_=ln_v[xb][:], func=AF.Ln, bias=EPS, scale=1.0),
               reads=[f"lv{xb}", "cbias"], writes=[f"linv{xb}"])
            OP("scalar", lambda sc, xb=xb: sc.activation(out=ln_inv[xb][:],
               in_=ln_inv[xb][:], func=AF.Exp, bias=0.0, scale=-0.5),
               reads=[f"linv{xb}", "cbias"], writes=[f"linv{xb}"])
            OP("gpsimd", lambda g, xb=xb: g.tensor_tensor(
               out=negml[:, xb:xb + 1], in0=ln_m[xb][:], in1=ln_inv[xb][:],
               op=ALU.mult),
               reads=[f"lm{xb}", f"linv{xb}"], writes=[f"negml{xb}"])
            OP("gpsimd", lambda g, xb=xb: g.tensor_scalar(
               out=negml[:, xb:xb + 1], in0=negml[:, xb:xb + 1],
               scalar1=-1.0, scalar2=None, op0=ALU.mult),
               reads=[f"negml{xb}"], writes=[f"negml{xb}"])
            OP("scalar", lambda sc, xb=xb: sc.activation(out=xn_tm[:],
               in_=x_c[xb][:], func=AF.Identity, bias=negml[:, xb:xb + 1],
               scale=ln_inv[xb][:]),
               reads=[f"x{xb}", f"negml{xb}", f"linv{xb}"], writes=["xntm"])
            for k in range(8):
                bank = k % 2
                OP("tensor", lambda t, k=k, bank=bank: t.transpose(
                   pb[bank][:, 0:128], xn_tm[:, k * 128:(k + 1) * 128], ident[:]),
                   reads=["xntm", "ident"], writes=[PB[bank]])
                k4 = k % 4
                OP("scalar", lambda sc, k4=k4, bank=bank: sc.activation(
                   out=xfull[:, k4 * 128:k4 * 128 + 128],
                   in_=pb[bank][:, 0:128],
                   func=AF.Copy, bias=0.0, scale=1.0),
                   reads=[PB[bank]], writes=[f"xfull{k4}"])
                OP("scalar", lambda sc, k=k, j=j, k4=k4: sc.activation(
                   out=xnf_h[k][:, j * C:(j + 1) * C],
                   in_=xfull[:, k4 * 128:k4 * 128 + 128],
                   func=AF.Copy, bias=0.0, scale=1.0),
                   reads=[f"xfull{k4}"], writes=[f"xnfh{k}_{j}"])
                OP("vector", lambda v, k=k, j=j, k4=k4: v.scalar_tensor_tensor(
                   out=xnf_l[k][:, j * C:(j + 1) * C],
                   in0=xfull[:, k4 * 128:k4 * 128 + 128],
                   scalar=1.0, in1=xnf_h[k][:, j * C:(j + 1) * C],
                   op0=ALU.mult, op1=ALU.subtract),
                   reads=[f"xfull{k4}", f"xnfh{k}_{j}"], writes=[f"xnf{k}_{j}"])

    emit_A0(0)
    emit_wgk1_split()
    emit_split(10 ** 6)

    for tile in range(n_tiles):
        t0 = tile * tile_tok
        if tile > 0:
            emit_A0(tile)

        xnf_tile = [f"xnf{k}_{j}" for k in range(8) for j in range(JC)]

        PW = min(512, tile_tok)
        for p5 in range(tile_tok // PW):
            bank = 4 + (p5 % 2)
            sl = slice(p5 * PW, p5 * PW + PW)

            def u_one(t, sl=sl, bank=bank):
                last = None
                first = True
                for k in range(8):
                    for (wgt, mov) in ((wgk1h, xnf_h), (wgk1h, xnf_l), (wgk1l, xnf_h)):
                        last = t.matmul(pb[bank][0:16, 0:PW],
                                        wgt[:, k * R:(k + 1) * R], mov[k][:, sl],
                                        start=first, stop=(k == 7 and wgt is wgk1l))
                        first = False
                return last
            OP("tensor", u_one, reads=["wgk1h", "wgk1l"] + xnf_tile,
               writes=[PB[bank]])
            OP("vector", lambda v, sl=sl, bank=bank: v.tensor_copy(
               u_sb[:, sl], pb[bank][0:16, 0:PW]),
               reads=[PB[bank]], writes=["u"])

        for h in range(n_heads):
            buf = h % 2
            if tile == 0 and h == 0:
                for c2 in range(2):
                    load_qk("wq", 0, c2)
                    load_qk("wk", 0, c2)
                for half in range(2):
                    load_vg("wv", 0, half, ext=SPLIT_PIECE_KEYS)
                    load_vg("wg", 0, half, ext=SPLIT_PIECE_KEYS)
                load_wo(0)

            for c2 in range(2):
                for (mat, wh, wl, dstf, dkey) in (
                        ("wq", wqh_cs[c2], wql_cs[c2], qf, "qf"),
                        ("wk", wkh_cs[c2], wkl_cs[c2], kf, "kf")):
                    whk = f"{mat}hcs{c2}"; wlk = f"{mat}lcs{c2}"
                    for p5 in range(tile_tok // PW):
                        bk = (0 if mat == "wq" else 2) + (p5 % 2)
                        sl = slice(p5 * PW, p5 * PW + PW)

                        def qk_one(t, wh=wh, wl=wl, sl=sl, bk=bk):
                            last = None
                            first = True
                            for k in range(8):
                                for (wgt, mov) in ((wh, xnf_h), (wh, xnf_l),
                                                   (wl, xnf_h)):
                                    last = t.matmul(
                                        pb[bk][:, 0:PW],
                                        wgt[:, k * 128:(k + 1) * 128],
                                        mov[k][:, sl],
                                        start=first,
                                        stop=(k == 7 and wgt is wl))
                                    first = False
                            return last
                        OP("tensor", qk_one, reads=[whk, wlk] + xnf_tile,
                           writes=[PB[bk]])
                        OP("vector", lambda v, dstf=dstf, c2=c2, sl=sl, bk=bk:
                           v.tensor_copy(
                           dstf[:, c2 * tile_tok + sl.start:
                                c2 * tile_tok + sl.stop], pb[bk][:, 0:PW]),
                           reads=[PB[bk]], writes=[f"{dkey}{c2}"])
                    if h + 1 < n_heads:
                        load_qk(mat, h + 1, c2)
                    elif tile + 1 < n_tiles:
                        load_qk(mat, 0, c2)

            for c2 in range(2):
                d8 = h * 2 + c2

                def z_all(t, d8=d8):
                    last = None
                    for half in range(NH2):
                        last = t.matmul(
                            pb[4][:, (half % 2) * 256:(half % 2) * 256 + 256],
                            wgk2[:, d8 * 128:(d8 + 1) * 128],
                            u_sb[:, half * 256:(half + 1) * 256],
                            start=True, stop=True)
                    return last
                OP("tensor", z_all, reads=["wgk2", "u"], writes=[PB[4]])
                for half in range(NH2):
                    hb = half % 2
                    OP("scalar", lambda sc, c2=c2, d8=d8, half=half, hb=hb:
                       sc.activation(
                       out=spf[:, c2 * tile_tok + half * 256:
                               c2 * tile_tok + half * 256 + 256],
                       in_=pb[4][:, hb * 256:hb * 256 + 256], func=AF.Exp,
                       bias=bgk2n[:, d8:d8 + 1], scale=-1.0),
                       reads=[PB[4], "bgk2n"], writes=[f"sp{c2}"])
                    OP("scalar", lambda sc, c2=c2, half=half: sc.activation(
                       out=spf[:, c2 * tile_tok + half * 256:
                               c2 * tile_tok + half * 256 + 256],
                       in_=spf[:, c2 * tile_tok + half * 256:
                               c2 * tile_tok + half * 256 + 256],
                       func=AF.Ln, bias=1.0, scale=1.0),
                       reads=[f"sp{c2}", "cbias"], writes=[f"sp{c2}"])

            def emit_gates(j):
                jt = j * C
                x = j % 2
                for c2 in range(2):
                    OP("vector", lambda v, c2=c2, jt=jt, x=x: v.tensor_tensor_scan(
                       out=at[x][:, c2 * C:(c2 + 1) * C], data0=ones128[:],
                       data1=spf[:, c2 * tile_tok + jt:c2 * tile_tok + jt + C],
                       initial=0.0, op0=ALU.mult, op1=ALU.add),
                       reads=[f"sp{c2}", "ones128"], writes=[f"at{c2}_{x}"])
                    OP("scalar", lambda sc, c2=c2, x=x: sc.activation(
                       out=qt[x][:, c2 * C:(c2 + 1) * C],
                       in_=at[x][:, c2 * C:(c2 + 1) * C],
                       func=AF.Exp, bias=-LN16, scale=-1.0 / 16.0),
                       reads=[f"at{c2}_{x}", "cbias"], writes=[f"qt{c2}_{x}"])
                    OP("scalar", lambda sc, c2=c2, x=x: sc.activation(
                       out=kt[x][:, c2 * C:(c2 + 1) * C],
                       in_=at[x][:, c2 * C:(c2 + 1) * C],
                       func=AF.Exp, bias=0.0, scale=1.0 / 16.0),
                       reads=[f"at{c2}_{x}", "cbias"], writes=[f"kt{c2}_{x}"])
                    OP("scalar", lambda sc, c2=c2, x=x: sc.activation(
                       out=alast[x][:, c2:c2 + 1],
                       in_=at[x][:, (c2 + 1) * C - 1:(c2 + 1) * C],
                       func=AF.Exp, bias=0.0, scale=-1.0 / 16.0),
                       reads=[f"at{c2}_{x}", "cbias"], writes=[f"alast{c2}_{x}"])
                    OP("vector", lambda v, c2=c2, jt=jt, x=x: v.tensor_tensor(
                       out=qt[x][:, c2 * C:(c2 + 1) * C],
                       in0=qt[x][:, c2 * C:(c2 + 1) * C],
                       in1=qf[:, c2 * tile_tok + jt:c2 * tile_tok + jt + C],
                       op=ALU.mult),
                       reads=[f"qf{c2}", f"qt{c2}_{x}"], writes=[f"qt{c2}_{x}"])
                    OP("vector", lambda v, c2=c2, jt=jt, x=x: v.tensor_tensor(
                       out=kt[x][:, c2 * C:(c2 + 1) * C],
                       in0=kt[x][:, c2 * C:(c2 + 1) * C],
                       in1=kf[:, c2 * tile_tok + jt:c2 * tile_tok + jt + C],
                       op=ALU.mult),
                       reads=[f"kf{c2}", f"kt{c2}_{x}"], writes=[f"kt{c2}_{x}"])
                    OP("scalar", lambda sc, c2=c2, x=x: sc.activation(
                       out=khf[x][:, c2 * C:(c2 + 1) * C],
                       in_=kt[x][:, c2 * C:(c2 + 1) * C],
                       func=AF.Copy, bias=0.0, scale=alast[x][:, c2:c2 + 1]),
                       reads=[f"kt{c2}_{x}", f"alast{c2}_{x}"], writes=[f"khf{c2}_{x}"])

            emit_gates(0)

            for half in range(2):
                for j in range(JC):
                    bv = 2 + (j % 2)
                    bg = 4 + (j % 2)

                    def vg_one(t, j=j, half=half, bv=bv, bg=bg):
                        last = None
                        for k in range(8):
                            ksl = slice(k * 256, (k + 1) * 256)
                            xh = xnf_h[k][:, j * C:(j + 1) * C]
                            xl = xnf_l[k][:, j * C:(j + 1) * C]
                            last = t.matmul(pb[bv][:, 0:256], xh,
                                            wvh_cs[half][:, ksl],
                                            start=(k == 0), stop=False)
                            last = t.matmul(pb[bg][:, 0:256], xh,
                                            wgh_cs[half][:, ksl],
                                            start=(k == 0), stop=False)
                            last = t.matmul(pb[bv][:, 0:256], xh,
                                            wvl_cs[half][:, ksl],
                                            start=False, stop=False)
                            last = t.matmul(pb[bg][:, 0:256], xh,
                                            wgl_cs[half][:, ksl],
                                            start=False, stop=False)
                            last = t.matmul(pb[bv][:, 0:256], xl,
                                            wvh_cs[half][:, ksl],
                                            start=False, stop=(k == 7))
                            last = t.matmul(pb[bg][:, 0:256], xl,
                                            wgh_cs[half][:, ksl],
                                            start=False, stop=(k == 7))
                        return last
                    OP("tensor", vg_one,
                       reads=[f"wvhcs{half}", f"wvlcs{half}", f"wghcs{half}",
                              f"wglcs{half}"] + xnf_tile,
                       writes=[PB[bv], PB[bg]])
                    OP("vector", lambda v, j=j, half=half, bv=bv: v.tensor_copy(
                       vth[j][:, half * 256:half * 256 + 256], pb[bv][:, 0:256]),
                       reads=[PB[bv]], writes=[f"v{j}"])
                    OP("vector", lambda v, j=j, half=half, bg=bg: v.tensor_copy(
                       gth[j][:, half * 256:half * 256 + 256], pb[bg][:, 0:256]),
                       reads=[PB[bg]], writes=[f"g{j}"])
                if h + 1 < n_heads:
                    load_vg("wv", h + 1, half)
                    load_vg("wg", h + 1, half)
                elif half == 1:
                    for jp in range(2):
                        load_w12("w1", jp)
                        load_w12("w2", jp)

            def emit_otm_evac(jp):
                OP("scalar", lambda sc, jp=jp: sc.activation(
                   out=o_tm[jp % 2][:, 0:256], in_=pb[5][:, 0:256],
                   func=AF.Copy, bias=0.0, scale=1.0),
                   reads=[PB[5]], writes=[f"otm{jp % 2}"])
                OP("scalar", lambda sc, jp=jp: sc.activation(
                   out=o_tm[jp % 2][:, 256:512], in_=pb[6][:, 0:256],
                   func=AF.Copy, bias=0.0, scale=1.0),
                   reads=[PB[6]], writes=[f"otm{jp % 2}"])

            def emit_epilogue(jp):
                ot = o_tm[jp % 2]
                otk = f"otm{jp % 2}"
                OP("scalar", lambda sc, ot=ot: sc.activation(out=bscr[:, 0:HDV],
                   in_=ot[:], func=AF.Square, accum_out=ssum[:]),
                   reads=[otk], writes=["bscr", "ssum"])
                OP("scalar", lambda sc: sc.activation(out=srms[:], in_=ssum[:],
                   func=AF.Ln, bias=EPS, scale=1.0 / HDV),
                   reads=["ssum", "cbias"], writes=["srms"])
                OP("scalar", lambda sc: sc.activation(out=srms[:], in_=srms[:],
                   func=AF.Exp, bias=0.0, scale=-0.5),
                   reads=["srms", "cbias"], writes=["srms"])
                OP("scalar", lambda sc, jp=jp: sc.activation(out=sil_s[:],
                   in_=gth[jp][:], func=AF.Exp, bias=0.0, scale=-1.0),
                   reads=[f"g{jp}", "cbias"], writes=["sils"])
                OP("scalar", lambda sc: sc.activation(out=sil_s[:], in_=sil_s[:],
                   func=AF.Copy, bias=1.0, scale=1.0),
                   reads=["sils"], writes=["sils"])
                OP("vector", lambda v: v.reciprocal(sil_s[:], sil_s[:]),
                   reads=["sils"], writes=["sils"])
                OP("vector", lambda v, jp=jp: v.tensor_tensor(out=gg[:],
                   in0=gth[jp][:], in1=sil_s[:], op=ALU.mult),
                   reads=[f"g{jp}", "sils"], writes=["gg"])
                OP("vector", lambda v, ot=ot: v.scalar_tensor_tensor(out=ot[:],
                   in0=ot[:], scalar=srms[:], in1=gg[:],
                   op0=ALU.mult, op1=ALU.mult),
                   reads=[otk, "srms", "gg"], writes=[otk])

            def emit_og(jp):
                ot = o_tm[jp % 2]
                otk = f"otm{jp % 2}"

                def og_tr(t, ot=ot):
                    last = None
                    for dc in range(4):
                        last = t.transpose(pb[2][:, dc * 128:dc * 128 + 128],
                                           ot[:, dc * 128:(dc + 1) * 128], ident[:])
                    return last
                OP("tensor", og_tr, reads=[otk, "ident"], writes=[PB[2]])
                OP("scalar", lambda sc, jp=jp: sc.activation(
                   out=ogh_h[jp][:], in_=pb[2][:, 0:512],
                   func=AF.Copy, bias=0.0, scale=1.0),
                   reads=[PB[2]], writes=[f"ogh{jp}"])
                OP("vector", lambda v, jp=jp: v.scalar_tensor_tensor(
                   out=ogh_l[jp][:], in0=pb[2][:, 0:512], scalar=1.0,
                   in1=ogh_h[jp][:], op0=ALU.mult, op1=ALU.subtract),
                   reads=[PB[2], f"ogh{jp}"], writes=[f"og{jp}"])

            for j in range(JC):
                x = j % 2
                if j + 1 < JC:
                    emit_gates(j + 1)
                if j > 0:
                    emit_otm_evac(j - 1)

                def kh_tr(t, x=x):
                    last = None
                    for c2 in range(2):
                        last = t.transpose(pb[4][:, c2 * 128:c2 * 128 + 128],
                                           khf[x][:, c2 * C:(c2 + 1) * C], ident[:])
                    return last
                OP("tensor", kh_tr, reads=[f"khf0_{x}", f"khf1_{x}", "ident"],
                   writes=[PB[4]])
                OP("scalar", lambda sc: sc.activation(out=khtm[:, 0:256],
                   in_=pb[4][:, 0:256], func=AF.Copy, bias=0.0, scale=1.0),
                   reads=[PB[4]], writes=["khtm0", "khtm1"])

                def p_mms(t, x=x):
                    last = None
                    for c2 in range(2):
                        last = t.matmul(pb[7][:, 0:128],
                                        kt[x][:, c2 * C:(c2 + 1) * C],
                                        qt[x][:, c2 * C:(c2 + 1) * C],
                                        start=(c2 == 0), stop=(c2 == 1))
                    return last
                OP("tensor", p_mms,
                   reads=[f"qt0_{x}", f"qt1_{x}", f"kt0_{x}", f"kt1_{x}"],
                   writes=[PB[7]])
                OP("vector", lambda v: v.tensor_tensor(out=p_sm[:],
                   in0=pb[7][:, 0:128], in1=triu[:], op=ALU.mult),
                   reads=[PB[7], "triu"], writes=["psm"])

                def o_mms(t, j=j, h=h, x=x):
                    last = None
                    for half in range(2):
                        sl = slice(half * 256, half * 256 + 256)
                        last = t.matmul(pb[5 + half][:, 0:256], p_sm[:],
                                        vth[j][:, sl], start=True, stop=False)
                    for c2 in range(2):
                        for half in range(2):
                            sl = slice(half * 256, half * 256 + 256)
                            last = t.matmul(pb[5 + half][:, 0:256],
                                            qt[x][:, c2 * C:(c2 + 1) * C],
                                            Sst[h][c2][:, sl],
                                            start=False, stop=(c2 == 1))
                    return last
                OP("tensor", o_mms,
                   reads=["psm", f"v{j}", f"qt0_{x}", f"qt1_{x}",
                          f"S{h}0", f"S{h}1"],
                   writes=[PB[5], PB[6]])

                for c2 in range(2):
                    def s_mms(t, c2=c2, j=j):
                        last = None
                        for half in range(2):
                            sl = slice(half * 256, half * 256 + 256)
                            last = t.matmul(pb[7][:, sl],
                                            khtm[:, c2 * C:(c2 + 1) * C],
                                            vth[j][:, sl], start=True, stop=True)
                        return last
                    OP("tensor", s_mms, reads=[f"khtm{c2}", f"v{j}"],
                       writes=[PB[7]])
                    OP("vector", lambda v, h=h, c2=c2, x=x: v.scalar_tensor_tensor(
                       out=Sst[h][c2][:], in0=Sst[h][c2][:],
                       scalar=alast[x][:, c2:c2 + 1], in1=pb[7][:, 0:HDV],
                       op0=ALU.mult, op1=ALU.add),
                       reads=[f"S{h}{c2}", f"alast{c2}_{x}", PB[7]],
                       writes=[f"S{h}{c2}"])

                if j > 0:
                    emit_epilogue(j - 1)
                if j > 1:
                    emit_og(j - 2)

            emit_otm_evac(JC - 1)
            emit_epilogue(JC - 1)
            emit_og(JC - 2)
            emit_og(JC - 1)

            for j in range(JC):
                def wo_mms(t, j=j):
                    last = None
                    first = [True, True]
                    for dc in range(4):
                        oh = ogh_h[j][:, dc * 128:(dc + 1) * 128]
                        ol = ogh_l[j][:, dc * 128:(dc + 1) * 128]
                        for dpart in range(2):
                            bk = 2 + dpart
                            dsl = slice(dc * 512, (dc + 1) * 512)
                            last = t.matmul(pb[bk][:, 0:512], oh,
                                            woh_cs[dpart][:, dsl],
                                            start=first[dpart], stop=False)
                            first[dpart] = False
                            last = t.matmul(pb[bk][:, 0:512], oh,
                                            wol_cs[dpart][:, dsl],
                                            start=False, stop=False)
                            last = t.matmul(pb[bk][:, 0:512], ol,
                                            woh_cs[dpart][:, dsl],
                                            start=False, stop=(dc == 3))
                    return last
                OP("tensor", wo_mms,
                   reads=["wohcs0", "wolcs0", "wohcs1", "wolcs1",
                          f"ogh{j}", f"og{j}"],
                   writes=[PB[2], PB[3]])
                for dpart in range(2):
                    sl0 = dpart * 512
                    bank = 2 + dpart
                    if h == 0:
                        OP("vector", lambda v, j=j, sl0=sl0, bank=bank:
                           v.scalar_tensor_tensor(
                           out=attn_c[j][:, sl0:sl0 + 512],
                           in0=pb[bank][:, 0:512], scalar=1.0,
                           in1=b2bc[:, sl0:sl0 + 512],
                           op0=ALU.mult, op1=ALU.add),
                           reads=[PB[bank], "b2bc"], writes=[f"attn{j}"])
                    else:
                        OP("vector", lambda v, j=j, sl0=sl0, bank=bank:
                           v.scalar_tensor_tensor(
                           out=attn_c[j][:, sl0:sl0 + 512],
                           in0=pb[bank][:, 0:512], scalar=1.0,
                           in1=attn_c[j][:, sl0:sl0 + 512],
                           op0=ALU.mult, op1=ALU.add),
                           reads=[PB[bank], f"attn{j}"], writes=[f"attn{j}"])
            if h + 1 < n_heads:
                load_wo(h + 1)
            elif tile + 1 < n_tiles:
                load_wo(0)

        for grp in range(8):
            jp0 = grp * 2
            for jpi in range(2):
                jp = jp0 + jpi
                wb = jpi
                for b in range(2):
                    jw = 2 * jp + b
                    slot = jw % 4
                    bank = slot % 2

                    def f1_all(t, b=b, wb=wb, bank=bank):
                        last = None
                        first = True
                        for k in range(8):
                            co = k * 256 + b * 128
                            for (wgt, mov) in ((wvh_cs[wb], xnf_h),
                                               (wvh_cs[wb], xnf_l),
                                               (wvl_cs[wb], xnf_h)):
                                lo_pass = wgt is wvl_cs[wb]
                                w_ap = (wvl_cs[wb] if lo_pass else wvh_cs[wb])[:, co:co + 128]
                                last = t.matmul(pb[bank][:, 0:tile_tok], w_ap,
                                                mov[k][:, 0:tile_tok],
                                                start=first,
                                                stop=(k == 7 and lo_pass))
                                first = False
                        return last
                    OP("tensor", f1_all,
                       reads=[f"wvhcs{wb}", f"wvlcs{wb}"] + xnf_tile,
                       writes=[PB[bank]])
                    OP("vector", lambda v, jw=jw, bank=bank: v.tensor_scalar(
                       out=gg[:, 0:tile_tok], in0=pb[bank][:, 0:tile_tok],
                       scalar1=b1f[:, jw:jw + 1],
                       scalar2=0.0, op0=ALU.add, op1=ALU.max),
                       reads=[PB[bank], "b1f"], writes=["gg"])
                    OP("scalar", lambda sc, slot=slot: sc.activation(
                       out=h1h[:, slot * tile_tok:(slot + 1) * tile_tok],
                       in_=gg[:, 0:tile_tok],
                       func=AF.Copy, bias=0.0, scale=1.0),
                       reads=["gg"], writes=[f"h1h_{slot}"])
                    OP("vector", lambda v, slot=slot: v.scalar_tensor_tensor(
                       out=h1l[:, slot * tile_tok:(slot + 1) * tile_tok],
                       in0=gg[:, 0:tile_tok], scalar=1.0,
                       in1=h1h[:, slot * tile_tok:(slot + 1) * tile_tok],
                       op0=ALU.mult, op1=ALU.subtract),
                       reads=["gg", f"h1h_{slot}"], writes=[f"h1_{slot}"])
            for jpi in range(2):
                jp = jp0 + jpi
                if jp + 2 < 16:
                    load_w12("w1", jp + 2)
                elif tile + 1 < n_tiles:
                    load_vg("wv", 0, jp % 2)
            for slot in range(4):
                def f2_slot(t, slot=slot):
                    last = None
                    jpi, b = slot // 2, slot % 2
                    for j in range(JC):
                        hh = h1h[:, slot * tile_tok + j * C:slot * tile_tok + (j + 1) * C]
                        hl = h1l[:, slot * tile_tok + j * C:slot * tile_tok + (j + 1) * C]
                        for dh in range(2):
                            bk = 2 * j + dh
                            wsl = slice(b * 1024 + dh * 512, b * 1024 + dh * 512 + 512)
                            last = t.matmul(pb[bk][:, 0:512], hh,
                                            wgh_cs[jpi][:, wsl],
                                            start=(slot == 0), stop=False)
                            last = t.matmul(pb[bk][:, 0:512], hh,
                                            wgl_cs[jpi][:, wsl],
                                            start=False, stop=False)
                            last = t.matmul(pb[bk][:, 0:512], hl,
                                            wgh_cs[jpi][:, wsl],
                                            start=False, stop=(slot == 3))
                    return last
                OP("tensor", f2_slot,
                   reads=[f"wghcs{slot // 2}", f"wglcs{slot // 2}",
                          f"h1h_{slot}", f"h1_{slot}"],
                   writes=PB[0:8])
            for jpi in range(2):
                jp = jp0 + 2 + jpi
                if jp < 16:
                    load_w12("w2", jp)
                elif tile + 1 < n_tiles:
                    load_vg("wg", 0, jp % 2)
            if grp < 7:
                for j in range(JC):
                    for dh in range(2):
                        OP("vector", lambda v, j=j, dh=dh: v.scalar_tensor_tensor(
                           out=attn_c[j][:, dh * 512:dh * 512 + 512],
                           in0=pb[2 * j + dh][:, 0:512], scalar=1.0,
                           in1=attn_c[j][:, dh * 512:dh * 512 + 512],
                           op0=ALU.mult, op1=ALU.add),
                           reads=[PB[2 * j + dh], f"attn{j}"], writes=[f"attn{j}"])
            else:
                for j in range(JC):
                    ob = j % 2
                    r0 = t0 + j * C
                    OP("sync", lambda s, ob=ob, r0=r0: s.dma_start(
                       out=x_c[ob][:], in_=x_hbm[r0:r0 + C, :]),
                       writes=[f"x{ob}"], dma=True, stream=f"x{ob}")
                    for dh in range(2):
                        OP("vector", lambda v, j=j, dh=dh:
                           v.scalar_tensor_tensor(
                           out=attn_c[j][:, dh * 512:dh * 512 + 512],
                           in0=pb[2 * j + dh][:, 0:512], scalar=1.0,
                           in1=attn_c[j][:, dh * 512:dh * 512 + 512],
                           op0=ALU.mult, op1=ALU.add),
                           reads=[PB[2 * j + dh], f"attn{j}"], writes=[f"attn{j}"])
                    OP("vector", lambda v, j=j, ob=ob: v.tensor_tensor(
                       out=attn_c[j][:], in0=attn_c[j][:], in1=x_c[ob][:],
                       op=ALU.add),
                       reads=[f"attn{j}", f"x{ob}"], writes=[f"attn{j}"])
                    OP("sync", lambda s, j=j, r0=r0: s.dma_start(
                       out=out_hbm[r0:r0 + C, :], in_=attn_c[j][:]),
                       reads=[f"attn{j}"], dma=True, stream="out")
    p.wait_all_streams("sync")
    p.build()
    return nc, p


_CACHE = {}


def _get_program(n_tok):
    if n_tok not in _CACHE:
        _CACHE[n_tok] = build_program(n_tok)
    return _CACHE[n_tok]


def _layout_weights(Wq, Wk, Wv, Wg, Wgk1, Wgk2, bgk2, Wo, gnw, w1, b1, w2, b2):
    c = np.ascontiguousarray
    return {
        "wq_l": c(Wq.reshape(8, 128, 8, 128).transpose(2, 1, 0, 3).reshape(8, 128, 1024)),
        "wk_l": c(Wk.reshape(8, 128, 8, 128).transpose(2, 1, 0, 3).reshape(8, 128, 1024)),
        "wv_l": c(Wv.reshape(8, 128, 8, 256).transpose(2, 1, 0, 3).reshape(8, 128, 2048)),
        "wg_l": c(Wg.reshape(8, 128, 8, 256).transpose(2, 1, 0, 3).reshape(8, 128, 2048)),
        "wo_l": c(Wo.reshape(4, 4, 128, 2, 512).transpose(0, 3, 2, 1, 4).reshape(8, 128, 2048)),
        "w1_l": c(w1.reshape(8, 128, 16, 256).transpose(2, 1, 0, 3).reshape(16, 128, 2048)),
        "w2_l": c(w2.reshape(16, 2, 128, 1024).transpose(0, 2, 1, 3).reshape(16, 128, 2048)),
        "wgk1_l": c(Wgk1.reshape(8, 128, R).transpose(1, 0, 2).reshape(128, 8 * R)),
        "wgk2": c(Wgk2),
        "bgk2_l": c(bgk2.reshape(8, 128).T),
        "b1_l": c(b1.reshape(32, 128).T),
        "b2_l": c(b2.reshape(1, D)),
        "gnw_l": c(gnw.reshape(4, 128).T),
    }


def kernel(L0_attn_x, L0_attn_Wq, L0_attn_Wk, L0_attn_Wv, L0_attn_Wg,
           L0_attn_Wgk1, L0_attn_Wgk2, L0_attn_bgk2, L0_attn_Wo,
           L0_attn_g_norm_weight, L0_ffn_w1, L0_ffn_b1, L0_ffn_w2, L0_ffn_b2):
    x = np.ascontiguousarray(np.asarray(L0_attn_x, dtype=np.float32))
    B, N, _ = x.shape
    wmap = _layout_weights(
        np.asarray(L0_attn_Wq, np.float32), np.asarray(L0_attn_Wk, np.float32),
        np.asarray(L0_attn_Wv, np.float32), np.asarray(L0_attn_Wg, np.float32),
        np.asarray(L0_attn_Wgk1, np.float32), np.asarray(L0_attn_Wgk2, np.float32),
        np.asarray(L0_attn_bgk2, np.float32), np.asarray(L0_attn_Wo, np.float32),
        np.asarray(L0_attn_g_norm_weight, np.float32),
        np.asarray(L0_ffn_w1, np.float32), np.asarray(L0_ffn_b1, np.float32),
        np.asarray(L0_ffn_w2, np.float32), np.asarray(L0_ffn_b2, np.float32))
    nc, _ = _get_program(N)
    in_maps = [dict(wmap, x=x[b]) for b in range(B)]
    res = run_bass_kernel_spmd(nc, in_maps, core_ids=list(range(B)))
    out = np.stack([res.results[b]["out"] for b in range(B)], axis=0)
    return out


# version 37
# speedup vs baseline: 1.1558x; 1.1558x over previous
import numpy as np
import concourse.bass as bass
import concourse.mybir as mybir
import concourse.bass_utils as _bu
from concourse.bass_utils import run_bass_kernel_spmd

if not getattr(_bu, "_ldwopt_patched", False):
    _orig_run_command = _bu.run_command

    def _run_command_ldwopt(argv, **kwargs):
        argv = ["--enable-ldw-opt=true" if a == "--enable-ldw-opt=false" else a
                for a in argv]
        return _orig_run_command(argv, **kwargs)

    _bu.run_command = _run_command_ldwopt
    _bu._ldwopt_patched = True

F32 = mybir.dt.float32
BF16 = mybir.dt.bfloat16
AF = mybir.ActivationFunctionType
ALU = mybir.AluOpType
AXX = mybir.AxisListType.X

D = 1024
HEADS = 4
HDK = 256
HDV = 512
R = 16
C = 128
EPS = 1e-5
LN16 = float(np.log(np.float32(16.0)))


class Prog:
    ENGINES = ("tensor", "vector", "scalar", "gpsimd", "sync")

    def __init__(self, nc):
        self.nc = nc
        self.sem = {e: nc.alloc_semaphore(f"S{e}") for e in self.ENGINES}
        self.stream_sems = {}
        self.cnt = {id(s): 0 for s in self.sem.values()}
        self.sem_by_id = {id(s): s for s in self.sem.values()}
        self.ops = {e: [] for e in self.ENGINES}
        self.res = {}
        self.waited = {e: {} for e in self.ENGINES}
        self.n_ops = 0
        self.n_waits = 0

    def _stream_sem(self, stream):
        s = self.stream_sems.get(stream)
        if s is None:
            s = self.nc.alloc_semaphore(f"D_{stream}")
            self.stream_sems[stream] = s
            self.cnt[id(s)] = 0
            self.sem_by_id[id(s)] = s
        return s

    def _deps(self, engine, reads, writes):
        deps = {}

        def add(tag):
            if tag is None:
                return
            sid, val, is_dma = tag
            if is_dma:
                val = self.cnt[sid]
            if deps.get(sid, 0) < val:
                deps[sid] = val

        for r in reads:
            e = self.res.get(r)
            if e:
                add(e["w"])
        for w in writes:
            e = self.res.get(w)
            if e:
                add(e["w"])
                for t in e["r"]:
                    add(t)
        waits = []
        wt = self.waited[engine]
        for sid, val in deps.items():
            if wt.get(sid, 0) < val:
                waits.append((sid, val))
                wt[sid] = val
        return waits

    def op(self, engine, fn, reads=(), writes=(), dma=False, stream=None):
        if engine == "scalar":
            reads = list(reads) + ["cbias"]
        waits = self._deps(engine, reads, writes)
        self.n_waits += len(waits)
        if dma:
            s = self._stream_sem(stream or ("dma_" + engine))
            self.cnt[id(s)] += 16
            tag = (id(s), self.cnt[id(s)], True)
            inc_sem, inc_by = s, 16
        else:
            s = self.sem[engine]
            self.cnt[id(s)] += 1
            tag = (id(s), self.cnt[id(s)], False)
            inc_sem, inc_by = s, 1
        for w in writes:
            self.res[w] = {"w": tag, "r": []}
        for r in reads:
            e = self.res.setdefault(r, {"w": None, "r": []})
            e["r"].append(tag)
        self.ops[engine].append((waits, fn, inc_sem, inc_by))
        self.n_ops += 1
        return tag

    def wait_all_streams(self, engine, streams=None):
        names = list(self.stream_sems) if streams is None else list(streams)
        targets = [(id(self.stream_sems[n]), self.cnt[id(self.stream_sems[n])])
                   for n in names if n in self.stream_sems]

        def fn(eng):
            for sid, val in targets:
                if val > 0:
                    eng.wait_ge(self.sem_by_id[sid], val)
            return eng.nop()

        self.ops[engine].append(([], fn, self.sem[engine], 1))
        self.cnt[id(self.sem[engine])] += 1
        self.n_ops += 1

    def build(self):
        nc = self.nc
        with nc.Block() as block:
            for engine in self.ENGINES:
                ops = self.ops[engine]
                if not ops:
                    continue
                dec = getattr(block, engine)

                def make(ops):
                    def body(eng):
                        for waits, fn, inc_sem, inc_by in ops:
                            for sid, val in waits:
                                eng.wait_ge(self.sem_by_id[sid], val)
                            inst = fn(eng)
                            if inst is not None:
                                inst.then_inc(inc_sem, inc_by)
                    return body

                dec(make(list(ops)))
        return nc


def build_program(n_tok=2048, tile_tok=512, n_heads=HEADS):
    assert tile_tok % C == 0 and n_tok % tile_tok == 0
    n_tiles = n_tok // tile_tok
    JC = tile_tok // C
    NH2 = tile_tok // 256

    nc = bass.Bass(trn_type="TRN2", target_bir_lowering=False, debug=False)
    din = lambda name, shp: nc.dram_tensor(name, shp, F32, kind="ExternalInput").ap()
    x_hbm = din("x", [n_tok, D])
    wq_l = din("wq_l", [8, 128, 1024])
    wk_l = din("wk_l", [8, 128, 1024])
    wv_l = din("wv_l", [8, 128, 2048])
    wg_l = din("wg_l", [8, 128, 2048])
    wo_l = din("wo_l", [8, 128, 2048])
    w1_l = din("w1_l", [16, 128, 2048])
    w2_l = din("w2_l", [16, 128, 2048])
    wgk1_l = din("wgk1_l", [128, 8 * R])
    dscr = lambda name, shp: nc.dram_tensor(name, shp, BF16, kind="Internal").ap()
    wq_hi = dscr("wq_hi", [8, 128, 1024]); wq_lo = dscr("wq_lo", [8, 128, 1024])
    wk_hi = dscr("wk_hi", [8, 128, 1024]); wk_lo = dscr("wk_lo", [8, 128, 1024])
    wv_hi = dscr("wv_hi", [8, 128, 2048]); wv_lo = dscr("wv_lo", [8, 128, 2048])
    wg_hi = dscr("wg_hi", [8, 128, 2048]); wg_lo = dscr("wg_lo", [8, 128, 2048])
    wo_hi = dscr("wo_hi", [8, 128, 2048]); wo_lo = dscr("wo_lo", [8, 128, 2048])
    w1_hi = dscr("w1_hi", [16, 128, 2048]); w1_lo = dscr("w1_lo", [16, 128, 2048])
    w2_hi = dscr("w2_hi", [16, 128, 2048]); w2_lo = dscr("w2_lo", [16, 128, 2048])
    wgk2_h = din("wgk2", [R, 1024])
    bgk2_l = din("bgk2_l", [128, 8])
    b1_l = din("b1_l", [128, 32])
    b2_l = din("b2_l", [1, D])
    gnw_l = din("gnw_l", [128, 4])
    out_hbm = nc.dram_tensor("out", [n_tok, D], F32, kind="ExternalOutput").ap()

    S_ = lambda name, shp, d=F32: nc.alloc_sbuf_tensor(name, shp, d)

    ident = S_("c_ident", [128, 128])
    triu = S_("c_triu", [128, 128])
    ones128 = S_("c_ones", [128, 128])
    onesrow = S_("c_onesrow", [1, 128])
    wgk2 = S_("r_wgk2", [16, 1024])
    bgk2n = S_("r_bgk2n", [128, 8])
    b1f = S_("r_b1f", [128, 32])
    b2row = S_("r_b2row", [1, D])
    b2bc = S_("r_b2bc", [128, D])
    gnwf = S_("r_gnwf", [128, 4])
    Sst = [[S_(f"st_S_{h}_{c2}", [128, HDV]) for c2 in range(2)] for h in range(n_heads)]

    x_c = [S_(f"t_x{i}", [128, D]) for i in range(2)]
    xnf_h = [S_(f"t_xnfh{k}", [128, tile_tok], BF16) for k in range(8)]
    xnf_l = [S_(f"t_xnfl{k}", [128, tile_tok], BF16) for k in range(8)]
    attn_c = [S_(f"t_attn{j}", [128, D]) for j in range(JC)]
    qf = S_("h_qf", [128, 2 * tile_tok])
    kf = S_("h_kf", [128, 2 * tile_tok])
    spf = S_("h_spf", [128, 2 * tile_tok])
    vth = [S_(f"h_v{j}", [128, HDV]) for j in range(JC)]
    gth = [S_(f"h_g{j}", [128, HDV]) for j in range(JC)]
    ogh_h = [S_(f"h_ogh{j}", [128, HDV], BF16) for j in range(JC)]
    ogh_l = [S_(f"h_ogl{j}", [128, HDV], BF16) for j in range(JC)]
    u_sb = S_("h_u", [16, tile_tok])
    at = [S_(f"k_at{i}", [128, 2 * C]) for i in range(2)]
    qt = [S_(f"k_qt{i}", [128, 2 * C]) for i in range(2)]
    kt = [S_(f"k_kt{i}", [128, 2 * C]) for i in range(2)]
    khf = [S_(f"k_khf{i}", [128, 2 * C]) for i in range(2)]
    khtm = S_("k_khtm", [128, 2 * C])
    alast = [S_(f"k_alast{i}", [128, 2]) for i in range(2)]
    p_sm = S_("k_p", [128, C])
    o_tm = [S_(f"k_otm{i}", [128, HDV]) for i in range(2)]
    sil_s = S_("k_sils", [128, HDV])
    gg = S_("k_gg", [128, HDV])
    bscr = S_("k_bscr", [128, D], BF16)
    xfull = S_("a_xfull", [128, 512])
    negml = S_("a_negml", [128, 2])
    ln_sx = [S_(f"k_sx{i}", [128, 1]) for i in range(2)]
    ln_sxx = [S_(f"k_sxx{i}", [128, 1]) for i in range(2)]
    ln_m = [S_(f"k_m{i}", [128, 1]) for i in range(2)]
    ln_v = [S_(f"k_v{i}", [128, 1]) for i in range(2)]
    ln_inv = [S_(f"k_inv{i}", [128, 1]) for i in range(2)]
    xn_tm = S_("k_xntm", [128, D])
    ssum = S_("k_ssum", [128, 1])
    srms = S_("k_srms", [128, 1])
    wqh_cs = [S_(f"w_wqh{i}", [128, 1024], BF16) for i in range(2)]
    wql_cs = [S_(f"w_wql{i}", [128, 1024], BF16) for i in range(2)]
    wkh_cs = [S_(f"w_wkh{i}", [128, 1024], BF16) for i in range(2)]
    wkl_cs = [S_(f"w_wkl{i}", [128, 1024], BF16) for i in range(2)]
    wvh_cs = [S_(f"w_wvh{i}", [128, 2048], BF16) for i in range(2)]
    wvl_cs = [S_(f"w_wvl{i}", [128, 2048], BF16) for i in range(2)]
    wgh_cs = [S_(f"w_wgh{i}", [128, 2048], BF16) for i in range(2)]
    wgl_cs = [S_(f"w_wgl{i}", [128, 2048], BF16) for i in range(2)]
    woh_cs = [S_(f"w_woh{i}", [128, 2048], BF16) for i in range(2)]
    wol_cs = [S_(f"w_wol{i}", [128, 2048], BF16) for i in range(2)]
    wgk1h = S_("r_wgk1h", [128, 8 * R], BF16)
    wgk1l = S_("r_wgk1l", [128, 8 * R], BF16)
    h1h = S_("f_h1h", [128, 4 * tile_tok], BF16)
    h1l = S_("f_h1l", [128, 4 * tile_tok], BF16)

    pb = [nc.alloc_psum_tensor(f"pb{i}", [128, 512], F32) for i in range(8)]
    PB = [f"pb{i}" for i in range(8)]

    p = Prog(nc)
    OP = p.op

    cbias_vals = [0.0, 1.0, EPS, -LN16]
    cbias = S_("c_bias2", [128, len(cbias_vals)])
    for i, v in enumerate(cbias_vals):
        OP("gpsimd", lambda g, i=i, v=v: g.memset(cbias[:, i:i + 1], float(v)),
           writes=["cbias"])
        nc.const_aps.register(F32, float(v), cbias[:, i:i + 1])

    OP("gpsimd", lambda g: g.memset(ident[:], 1.0), writes=["ident"])
    OP("gpsimd", lambda g: g.affine_select(out=ident[:], in_=ident[:],
        compare_op=ALU.is_equal, fill=0.0, base=0, pattern=[[1, 128]],
        channel_multiplier=-1), reads=["ident"], writes=["ident"])
    OP("gpsimd", lambda g: g.memset(triu[:], 1.0), writes=["triu"])
    OP("gpsimd", lambda g: g.affine_select(out=triu[:], in_=triu[:],
        compare_op=ALU.is_ge, fill=0.0, base=0, pattern=[[1, 128]],
        channel_multiplier=-1), reads=["triu"], writes=["triu"])
    OP("gpsimd", lambda g: g.memset(ones128[:], 1.0), writes=["ones128"])
    OP("gpsimd", lambda g: g.memset(onesrow[:], 1.0), writes=["onesrow"])
    for h in range(n_heads):
        for c2 in range(2):
            OP("gpsimd", lambda g, h=h, c2=c2: g.memset(Sst[h][c2][:], 0.0),
               writes=[f"S{h}{c2}"])
    OP("sync", lambda s: s.dma_start(out=wgk2[:], in_=wgk2_h),
       writes=["wgk2"], dma=True, stream="cst_wgk2")
    OP("sync", lambda s: s.dma_start(out=bgk2n[:], in_=bgk2_l),
       writes=["bgk2n"], dma=True, stream="cst_bgk2n")
    OP("vector", lambda v: v.tensor_scalar(out=bgk2n[:], in0=bgk2n[:],
       scalar1=-1.0, scalar2=None, op0=ALU.mult), reads=["bgk2n"], writes=["bgk2n"])
    OP("sync", lambda s: s.dma_start(out=b1f[:], in_=b1_l),
       writes=["b1f"], dma=True, stream="cst_b1f")
    OP("sync", lambda s: s.dma_start(out=b2row[:], in_=b2_l),
       writes=["b2row"], dma=True, stream="cst_b2row")
    OP("sync", lambda s: s.dma_start(out=gnwf[:], in_=gnw_l),
       writes=["gnwf"], dma=True, stream="cst_gnwf")

    def bcast_row(dst, dkey, srcrow, skey, width, bank):
        for g0 in range(0, width, 512):
            w = min(512, width - g0)

            def mmg(t, g0=g0, w=w):
                last = None
                for c0 in range(0, w, 256):
                    last = t.matmul(pb[bank][:, c0:c0 + 256], onesrow[0:1, :],
                                    srcrow[0:1, g0 + c0:g0 + c0 + 256],
                                    start=True, stop=True)
                return last
            OP("tensor", mmg, reads=[skey, "onesrow"], writes=[PB[bank]])
            OP("vector", lambda v, g0=g0, w=w: v.tensor_copy(
               dst[:, g0:g0 + w], pb[bank][:, 0:w]),
               reads=[PB[bank]], writes=[dkey])

    bcast_row(b2bc, "b2bc", b2row, "b2row", D, 6)

    split_jobs = []
    PWS = 512
    def add_split(matname, mat_fp32, hi, lo, nslab, W):
        for s in range(nslab):
            for c0 in range(0, W, PWS):
                dc = (c0 // 512) if matname == "wo" else None
                split_jobs.append((matname, mat_fp32[s, :, c0:c0 + PWS],
                                   hi[s, :, c0:c0 + PWS], lo[s, :, c0:c0 + PWS],
                                   dc))
    add_split("wq", wq_l, wq_hi, wq_lo, 8, 1024)
    add_split("wk", wk_l, wk_hi, wk_lo, 8, 1024)
    add_split("wv", wv_l, wv_hi, wv_lo, 8, 2048)
    add_split("wg", wg_l, wg_hi, wg_lo, 8, 2048)
    add_split("wo", wo_l, wo_hi, wo_lo, 8, 2048)
    add_split("w1", w1_l, w1_hi, w1_lo, 16, 2048)
    add_split("w2", w2_l, w2_hi, w2_lo, 16, 2048)
    _mat_order = ["wq", "wk", "wv", "wg", "wo", "w1", "w2"]
    SPLIT_PIECE_KEYS = [f"splh{i}" for i in range(12)] + [f"spll{i}" for i in range(12)]
    SCRK = lambda mn: ([f"scr_{mn}_h{s}" for s in range(12)]
                       + [f"scr_{mn}_l{s}" for s in range(12)])
    split_ctr = [0]

    def emit_split_until(matname):
        n = 0
        for i in range(split_ctr[0], len(split_jobs)):
            if split_jobs[i][0] == matname or _mat_order.index(split_jobs[i][0]) < _mat_order.index(matname):
                n += 1
            else:
                break
        emit_split(n)

    def emit_split(n):
        i0 = split_ctr[0]
        n_land = 8
        def landing(li):
            if li < 4:
                return vth[li][:], f"v{li}"
            return gth[li - 4][:], f"g{li - 4}"
        hi_all = [wvh_cs[0], wvh_cs[1], wgh_cs[0]]
        lo_all = [wvl_cs[0], wvl_cs[1], wgl_cs[0]]
        for i in range(i0, min(i0 + n, len(split_jobs))):
            mn, srcp, hip, lop, wodc = split_jobs[i]
            lb, lbk = landing(i % n_land)
            sslot = i % 12
            hcol = hi_all[sslot // 4][:, (sslot % 4) * 512:(sslot % 4) * 512 + 512]
            lcol = lo_all[sslot // 4][:, (sslot % 4) * 512:(sslot % 4) * 512 + 512]
            hkey = f"splh{sslot}"; lkey = f"spll{sslot}"
            lastp = (i == len(split_jobs) - 1)
            OP("sync", lambda s, srcp=srcp, lb=lb: s.dma_start(out=lb, in_=srcp),
               writes=[lbk], dma=True, stream=f"spl{i % n_land}")
            scl = 1.0 if wodc is None else gnwf[:, wodc:wodc + 1]
            srd = [] if wodc is None else ["gnwf"]
            OP("scalar", lambda sc, lb=lb, hcol=hcol, scl=scl: sc.activation(
               out=hcol, in_=lb, func=AF.Copy, bias=0.0, scale=scl),
               reads=[lbk] + srd, writes=[hkey])
            OP("vector", lambda v, lb=lb, hcol=hcol, lcol=lcol, scl=scl:
               v.scalar_tensor_tensor(out=lcol, in0=lb, scalar=scl, in1=hcol,
               op0=ALU.mult, op1=ALU.subtract),
               reads=[lbk, hkey] + srd, writes=[lkey])
            hwr = [f"scr_{mn}_h{sslot}"]
            lwr = [f"scr_{mn}_l{sslot}"]
            OP("scalar", lambda sc, hip=hip, hcol=hcol: sc.dma_start(
               out=hip, in_=hcol),
               reads=[hkey], writes=hwr, dma=True, stream=f"sch{sslot}")
            OP("gpsimd", lambda g, lop=lop, lcol=lcol: g.dma_start(
               out=lop, in_=lcol),
               reads=[lkey], writes=lwr, dma=True, stream=f"scl{sslot}")
        split_ctr[0] = min(i0 + n, len(split_jobs))

    def emit_wgk1_split():
        OP("sync", lambda s: s.dma_start(out=vth[0][:, 0:128], in_=wgk1_l),
           writes=["v0"], dma=True, stream="splgk")
        OP("scalar", lambda sc: sc.activation(out=wgk1h[:], in_=vth[0][:, 0:128],
           func=AF.Copy, bias=0.0, scale=1.0), reads=["v0"], writes=["wgk1h"])
        OP("vector", lambda v: v.scalar_tensor_tensor(out=wgk1l[:],
           in0=vth[0][:, 0:128], scalar=1.0, in1=wgk1h[:],
           op0=ALU.mult, op1=ALU.subtract), reads=["v0", "wgk1h"],
           writes=["wgk1l"])

    def load_qk(mat, h, c2):
        bh, bl, hi, lo, key = {
            "wq": (wqh_cs, wql_cs, wq_hi, wq_lo, "wq"),
            "wk": (wkh_cs, wkl_cs, wk_hi, wk_lo, "wk")}[mat]
        s = 2 * h + c2
        OP("sync", lambda ss, s=s: ss.dma_start(out=bh[c2][:], in_=hi[s]),
           reads=SCRK(key), writes=[f"{key}hcs{c2}"], dma=True, stream=f"{key}{c2}")
        OP("sync", lambda ss, s=s: ss.dma_start(out=bl[c2][:], in_=lo[s]),
           reads=SCRK(key), writes=[f"{key}lcs{c2}"], dma=True, stream=f"{key}{c2}")

    def load_vg(mat, h, half, ext=()):
        bh, bl, hi, lo, key = {
            "wv": (wvh_cs, wvl_cs, wv_hi, wv_lo, "wv"),
            "wg": (wgh_cs, wgl_cs, wg_hi, wg_lo, "wg")}[mat]
        s = 2 * h + half
        OP("gpsimd", lambda g, s=s: g.dma_start(out=bh[half][:], in_=hi[s]),
           reads=SCRK(key), writes=[f"{key}hcs{half}"] + list(ext), dma=True,
           stream=f"{key}{half}")
        OP("gpsimd", lambda g, s=s: g.dma_start(out=bl[half][:], in_=lo[s]),
           reads=SCRK(key), writes=[f"{key}lcs{half}"] + list(ext), dma=True,
           stream=f"{key}{half}")

    def load_w12(mat, jp):
        bh, bl, hi, lo, key, strm = {
            "w1": (wvh_cs, wvl_cs, w1_hi, w1_lo, "wv", "wv"),
            "w2": (wgh_cs, wgl_cs, w2_hi, w2_lo, "wg", "wg")}[mat]
        b = jp % 2
        OP("gpsimd", lambda g, jp=jp, b=b: g.dma_start(out=bh[b][:], in_=hi[jp]),
           reads=SCRK(mat), writes=[f"{key}hcs{b}"], dma=True, stream=f"{strm}{b}")
        OP("gpsimd", lambda g, jp=jp, b=b: g.dma_start(out=bl[b][:], in_=lo[jp]),
           reads=SCRK(mat), writes=[f"{key}lcs{b}"], dma=True, stream=f"{strm}{b}")

    def load_wo(h):
        for dpart in range(2):
            OP("sync", lambda s, h=h, dpart=dpart: s.dma_start(
               out=woh_cs[dpart][:], in_=wo_hi[2 * h + dpart]),
               reads=SCRK("wo"), writes=[f"wohcs{dpart}"], dma=True,
               stream=f"wo{dpart}")
            OP("sync", lambda s, h=h, dpart=dpart: s.dma_start(
               out=wol_cs[dpart][:], in_=wo_lo[2 * h + dpart]),
               reads=SCRK("wo"), writes=[f"wolcs{dpart}"], dma=True,
               stream=f"wo{dpart}")

    def emit_A0(a0tile):
        a0t0 = a0tile * tile_tok
        for j in range(JC):
            r0 = a0t0 + j * C
            xb = j % 2
            OP("sync", lambda s, xb=xb, r0=r0: s.dma_start(
               out=x_c[xb][:], in_=x_hbm[r0:r0 + C, :]),
               writes=[f"x{xb}"], dma=True, stream=f"x{xb}")
            OP("scalar", lambda sc, xb=xb: sc.activation(out=bscr[:], in_=x_c[xb][:],
               func=AF.Copy, bias=0.0, scale=1.0, accum_out=ln_sx[xb][:]),
               reads=[f"x{xb}", "cbias"], writes=["bscr", f"lsx{xb}"])
            OP("scalar", lambda sc, xb=xb: sc.activation(out=bscr[:], in_=x_c[xb][:],
               func=AF.Square, accum_out=ln_sxx[xb][:]),
               reads=[f"x{xb}", "cbias"], writes=["bscr", f"lsxx{xb}"])
            OP("gpsimd", lambda g, xb=xb: g.tensor_scalar(out=ln_m[xb][:],
               in0=ln_sx[xb][:], scalar1=1.0 / D, scalar2=None, op0=ALU.mult),
               reads=[f"lsx{xb}"], writes=[f"lm{xb}"])
            OP("gpsimd", lambda g, xb=xb: g.tensor_tensor(out=ln_v[xb][:],
               in0=ln_m[xb][:], in1=ln_m[xb][:], op=ALU.mult),
               reads=[f"lm{xb}"], writes=[f"lv{xb}"])
            OP("gpsimd", lambda g, xb=xb: g.tensor_scalar(out=ln_sxx[xb][:],
               in0=ln_sxx[xb][:], scalar1=1.0 / D, scalar2=None, op0=ALU.mult),
               reads=[f"lsxx{xb}"], writes=[f"lsxx{xb}"])
            OP("gpsimd", lambda g, xb=xb: g.tensor_tensor(out=ln_v[xb][:],
               in0=ln_sxx[xb][:], in1=ln_v[xb][:], op=ALU.subtract),
               reads=[f"lsxx{xb}", f"lv{xb}"], writes=[f"lv{xb}"])
            OP("scalar", lambda sc, xb=xb: sc.activation(out=ln_inv[xb][:],
               in_=ln_v[xb][:], func=AF.Ln, bias=EPS, scale=1.0),
               reads=[f"lv{xb}", "cbias"], writes=[f"linv{xb}"])
            OP("scalar", lambda sc, xb=xb: sc.activation(out=ln_inv[xb][:],
               in_=ln_inv[xb][:], func=AF.Exp, bias=0.0, scale=-0.5),
               reads=[f"linv{xb}", "cbias"], writes=[f"linv{xb}"])
            OP("gpsimd", lambda g, xb=xb: g.tensor_tensor(
               out=negml[:, xb:xb + 1], in0=ln_m[xb][:], in1=ln_inv[xb][:],
               op=ALU.mult),
               reads=[f"lm{xb}", f"linv{xb}"], writes=[f"negml{xb}"])
            OP("gpsimd", lambda g, xb=xb: g.tensor_scalar(
               out=negml[:, xb:xb + 1], in0=negml[:, xb:xb + 1],
               scalar1=-1.0, scalar2=None, op0=ALU.mult),
               reads=[f"negml{xb}"], writes=[f"negml{xb}"])
            OP("scalar", lambda sc, xb=xb: sc.activation(out=xn_tm[:],
               in_=x_c[xb][:], func=AF.Identity, bias=negml[:, xb:xb + 1],
               scale=ln_inv[xb][:]),
               reads=[f"x{xb}", f"negml{xb}", f"linv{xb}"], writes=["xntm"])
            for k in range(8):
                bank = k % 2
                OP("tensor", lambda t, k=k, bank=bank: t.transpose(
                   pb[bank][:, 0:128], xn_tm[:, k * 128:(k + 1) * 128], ident[:]),
                   reads=["xntm", "ident"], writes=[PB[bank]])
                k4 = k % 4
                OP("scalar", lambda sc, k4=k4, bank=bank: sc.activation(
                   out=xfull[:, k4 * 128:k4 * 128 + 128],
                   in_=pb[bank][:, 0:128],
                   func=AF.Copy, bias=0.0, scale=1.0),
                   reads=[PB[bank]], writes=[f"xfull{k4}"])
                OP("scalar", lambda sc, k=k, j=j, k4=k4: sc.activation(
                   out=xnf_h[k][:, j * C:(j + 1) * C],
                   in_=xfull[:, k4 * 128:k4 * 128 + 128],
                   func=AF.Copy, bias=0.0, scale=1.0),
                   reads=[f"xfull{k4}"], writes=[f"xnfh{k}_{j}"])
                OP("vector", lambda v, k=k, j=j, k4=k4: v.scalar_tensor_tensor(
                   out=xnf_l[k][:, j * C:(j + 1) * C],
                   in0=xfull[:, k4 * 128:k4 * 128 + 128],
                   scalar=1.0, in1=xnf_h[k][:, j * C:(j + 1) * C],
                   op0=ALU.mult, op1=ALU.subtract),
                   reads=[f"xfull{k4}", f"xnfh{k}_{j}"], writes=[f"xnf{k}_{j}"])

    emit_A0(0)
    emit_wgk1_split()
    emit_split(10 ** 6)

    for tile in range(n_tiles):
        t0 = tile * tile_tok
        if tile > 0:
            emit_A0(tile)

        xnf_tile = [f"xnf{k}_{j}" for k in range(8) for j in range(JC)]

        PW = min(512, tile_tok)
        for p5 in range(tile_tok // PW):
            bank = 4 + (p5 % 2)
            sl = slice(p5 * PW, p5 * PW + PW)

            def u_one(t, sl=sl, bank=bank):
                last = None
                first = True
                for k in range(8):
                    for (wgt, mov) in ((wgk1h, xnf_h), (wgk1h, xnf_l), (wgk1l, xnf_h)):
                        last = t.matmul(pb[bank][0:16, 0:PW],
                                        wgt[:, k * R:(k + 1) * R], mov[k][:, sl],
                                        start=first, stop=(k == 7 and wgt is wgk1l))
                        first = False
                return last
            OP("tensor", u_one, reads=["wgk1h", "wgk1l"] + xnf_tile,
               writes=[PB[bank]])
            OP("vector", lambda v, sl=sl, bank=bank: v.tensor_copy(
               u_sb[:, sl], pb[bank][0:16, 0:PW]),
               reads=[PB[bank]], writes=["u"])

        for h in range(n_heads):
            buf = h % 2
            if tile == 0 and h == 0:
                for c2 in range(2):
                    load_qk("wq", 0, c2)
                    load_qk("wk", 0, c2)
                for half in range(2):
                    load_vg("wv", 0, half, ext=SPLIT_PIECE_KEYS)
                    load_vg("wg", 0, half, ext=SPLIT_PIECE_KEYS)
                load_wo(0)

            for c2 in range(2):
                for (mat, wh, wl, dstf, dkey) in (
                        ("wq", wqh_cs[c2], wql_cs[c2], qf, "qf"),
                        ("wk", wkh_cs[c2], wkl_cs[c2], kf, "kf")):
                    whk = f"{mat}hcs{c2}"; wlk = f"{mat}lcs{c2}"
                    for p5 in range(tile_tok // PW):
                        bk = (0 if mat == "wq" else 2) + (p5 % 2)
                        sl = slice(p5 * PW, p5 * PW + PW)

                        def qk_one(t, wh=wh, wl=wl, sl=sl, bk=bk):
                            last = None
                            first = True
                            for k in range(8):
                                for (wgt, mov) in ((wh, xnf_h), (wh, xnf_l),
                                                   (wl, xnf_h)):
                                    last = t.matmul(
                                        pb[bk][:, 0:PW],
                                        wgt[:, k * 128:(k + 1) * 128],
                                        mov[k][:, sl],
                                        start=first,
                                        stop=(k == 7 and wgt is wl))
                                    first = False
                            return last
                        OP("tensor", qk_one, reads=[whk, wlk] + xnf_tile,
                           writes=[PB[bk]])
                        OP("vector", lambda v, dstf=dstf, c2=c2, sl=sl, bk=bk:
                           v.tensor_copy(
                           dstf[:, c2 * tile_tok + sl.start:
                                c2 * tile_tok + sl.stop], pb[bk][:, 0:PW]),
                           reads=[PB[bk]], writes=[f"{dkey}{c2}"])
                    if h + 1 < n_heads:
                        load_qk(mat, h + 1, c2)
                    elif tile + 1 < n_tiles:
                        load_qk(mat, 0, c2)

            for c2 in range(2):
                d8 = h * 2 + c2

                def z_all(t, d8=d8):
                    last = None
                    for half in range(NH2):
                        last = t.matmul(
                            pb[4][:, (half % 2) * 256:(half % 2) * 256 + 256],
                            wgk2[:, d8 * 128:(d8 + 1) * 128],
                            u_sb[:, half * 256:(half + 1) * 256],
                            start=True, stop=True)
                    return last
                OP("tensor", z_all, reads=["wgk2", "u"], writes=[PB[4]])
                for half in range(NH2):
                    hb = half % 2
                    OP("scalar", lambda sc, c2=c2, d8=d8, half=half, hb=hb:
                       sc.activation(
                       out=spf[:, c2 * tile_tok + half * 256:
                               c2 * tile_tok + half * 256 + 256],
                       in_=pb[4][:, hb * 256:hb * 256 + 256], func=AF.Exp,
                       bias=bgk2n[:, d8:d8 + 1], scale=-1.0),
                       reads=[PB[4], "bgk2n"], writes=[f"sp{c2}"])
                    OP("scalar", lambda sc, c2=c2, half=half: sc.activation(
                       out=spf[:, c2 * tile_tok + half * 256:
                               c2 * tile_tok + half * 256 + 256],
                       in_=spf[:, c2 * tile_tok + half * 256:
                               c2 * tile_tok + half * 256 + 256],
                       func=AF.Ln, bias=1.0, scale=1.0),
                       reads=[f"sp{c2}", "cbias"], writes=[f"sp{c2}"])

            def emit_gates(j):
                jt = j * C
                x = j % 2
                for c2 in range(2):
                    OP("vector", lambda v, c2=c2, jt=jt, x=x: v.tensor_tensor_scan(
                       out=at[x][:, c2 * C:(c2 + 1) * C], data0=ones128[:],
                       data1=spf[:, c2 * tile_tok + jt:c2 * tile_tok + jt + C],
                       initial=0.0, op0=ALU.mult, op1=ALU.add),
                       reads=[f"sp{c2}", "ones128"], writes=[f"at{c2}_{x}"])
                    OP("scalar", lambda sc, c2=c2, x=x: sc.activation(
                       out=qt[x][:, c2 * C:(c2 + 1) * C],
                       in_=at[x][:, c2 * C:(c2 + 1) * C],
                       func=AF.Exp, bias=-LN16, scale=-1.0 / 16.0),
                       reads=[f"at{c2}_{x}", "cbias"], writes=[f"qt{c2}_{x}"])
                    OP("scalar", lambda sc, c2=c2, x=x: sc.activation(
                       out=kt[x][:, c2 * C:(c2 + 1) * C],
                       in_=at[x][:, c2 * C:(c2 + 1) * C],
                       func=AF.Exp, bias=0.0, scale=1.0 / 16.0),
                       reads=[f"at{c2}_{x}", "cbias"], writes=[f"kt{c2}_{x}"])
                    OP("scalar", lambda sc, c2=c2, x=x: sc.activation(
                       out=alast[x][:, c2:c2 + 1],
                       in_=at[x][:, (c2 + 1) * C - 1:(c2 + 1) * C],
                       func=AF.Exp, bias=0.0, scale=-1.0 / 16.0),
                       reads=[f"at{c2}_{x}", "cbias"], writes=[f"alast{c2}_{x}"])
                    OP("vector", lambda v, c2=c2, jt=jt, x=x: v.tensor_tensor(
                       out=qt[x][:, c2 * C:(c2 + 1) * C],
                       in0=qt[x][:, c2 * C:(c2 + 1) * C],
                       in1=qf[:, c2 * tile_tok + jt:c2 * tile_tok + jt + C],
                       op=ALU.mult),
                       reads=[f"qf{c2}", f"qt{c2}_{x}"], writes=[f"qt{c2}_{x}"])
                    OP("vector", lambda v, c2=c2, jt=jt, x=x: v.tensor_tensor(
                       out=kt[x][:, c2 * C:(c2 + 1) * C],
                       in0=kt[x][:, c2 * C:(c2 + 1) * C],
                       in1=kf[:, c2 * tile_tok + jt:c2 * tile_tok + jt + C],
                       op=ALU.mult),
                       reads=[f"kf{c2}", f"kt{c2}_{x}"], writes=[f"kt{c2}_{x}"])
                    OP("scalar", lambda sc, c2=c2, x=x: sc.activation(
                       out=khf[x][:, c2 * C:(c2 + 1) * C],
                       in_=kt[x][:, c2 * C:(c2 + 1) * C],
                       func=AF.Copy, bias=0.0, scale=alast[x][:, c2:c2 + 1]),
                       reads=[f"kt{c2}_{x}", f"alast{c2}_{x}"], writes=[f"khf{c2}_{x}"])

            emit_gates(0)

            for half in range(2):
                for j in range(JC):
                    bv = 2 + (j % 2)
                    bg = 4 + (j % 2)

                    def vg_one(t, j=j, half=half, bv=bv, bg=bg):
                        last = None
                        for k in range(8):
                            ksl = slice(k * 256, (k + 1) * 256)
                            xh = xnf_h[k][:, j * C:(j + 1) * C]
                            xl = xnf_l[k][:, j * C:(j + 1) * C]
                            last = t.matmul(pb[bv][:, 0:256], xh,
                                            wvh_cs[half][:, ksl],
                                            start=(k == 0), stop=False)
                            last = t.matmul(pb[bg][:, 0:256], xh,
                                            wgh_cs[half][:, ksl],
                                            start=(k == 0), stop=False)
                            last = t.matmul(pb[bv][:, 0:256], xh,
                                            wvl_cs[half][:, ksl],
                                            start=False, stop=False)
                            last = t.matmul(pb[bg][:, 0:256], xh,
                                            wgl_cs[half][:, ksl],
                                            start=False, stop=False)
                            last = t.matmul(pb[bv][:, 0:256], xl,
                                            wvh_cs[half][:, ksl],
                                            start=False, stop=(k == 7))
                            last = t.matmul(pb[bg][:, 0:256], xl,
                                            wgh_cs[half][:, ksl],
                                            start=False, stop=(k == 7))
                        return last
                    OP("tensor", vg_one,
                       reads=[f"wvhcs{half}", f"wvlcs{half}", f"wghcs{half}",
                              f"wglcs{half}"] + xnf_tile,
                       writes=[PB[bv], PB[bg]])
                    OP("vector", lambda v, j=j, half=half, bv=bv: v.tensor_copy(
                       vth[j][:, half * 256:half * 256 + 256], pb[bv][:, 0:256]),
                       reads=[PB[bv]], writes=[f"v{j}"])
                    OP("vector", lambda v, j=j, half=half, bg=bg: v.tensor_copy(
                       gth[j][:, half * 256:half * 256 + 256], pb[bg][:, 0:256]),
                       reads=[PB[bg]], writes=[f"g{j}"])
                if h + 1 < n_heads:
                    load_vg("wv", h + 1, half)
                    load_vg("wg", h + 1, half)
                elif half == 1:
                    for jp in range(2):
                        load_w12("w1", jp)
                        load_w12("w2", jp)

            def emit_otm_evac(jp):
                OP("scalar", lambda sc, jp=jp: sc.activation(
                   out=o_tm[jp % 2][:, 0:256], in_=pb[5][:, 0:256],
                   func=AF.Copy, bias=0.0, scale=1.0),
                   reads=[PB[5]], writes=[f"otm{jp % 2}"])
                OP("scalar", lambda sc, jp=jp: sc.activation(
                   out=o_tm[jp % 2][:, 256:512], in_=pb[6][:, 0:256],
                   func=AF.Copy, bias=0.0, scale=1.0),
                   reads=[PB[6]], writes=[f"otm{jp % 2}"])

            def emit_epilogue(jp):
                ot = o_tm[jp % 2]
                otk = f"otm{jp % 2}"
                OP("scalar", lambda sc, ot=ot: sc.activation(out=bscr[:, 0:HDV],
                   in_=ot[:], func=AF.Square, accum_out=ssum[:]),
                   reads=[otk], writes=["bscr", "ssum"])
                OP("scalar", lambda sc: sc.activation(out=srms[:], in_=ssum[:],
                   func=AF.Ln, bias=EPS, scale=1.0 / HDV),
                   reads=["ssum", "cbias"], writes=["srms"])
                OP("scalar", lambda sc: sc.activation(out=srms[:], in_=srms[:],
                   func=AF.Exp, bias=0.0, scale=-0.5),
                   reads=["srms", "cbias"], writes=["srms"])
                OP("scalar", lambda sc, jp=jp: sc.activation(out=sil_s[:],
                   in_=gth[jp][:], func=AF.Exp, bias=0.0, scale=-1.0),
                   reads=[f"g{jp}", "cbias"], writes=["sils"])
                OP("scalar", lambda sc: sc.activation(out=sil_s[:], in_=sil_s[:],
                   func=AF.Copy, bias=1.0, scale=1.0),
                   reads=["sils"], writes=["sils"])
                OP("vector", lambda v: v.reciprocal(sil_s[:], sil_s[:]),
                   reads=["sils"], writes=["sils"])
                OP("vector", lambda v, jp=jp: v.tensor_tensor(out=gg[:],
                   in0=gth[jp][:], in1=sil_s[:], op=ALU.mult),
                   reads=[f"g{jp}", "sils"], writes=["gg"])
                OP("vector", lambda v, ot=ot: v.scalar_tensor_tensor(out=ot[:],
                   in0=ot[:], scalar=srms[:], in1=gg[:],
                   op0=ALU.mult, op1=ALU.mult),
                   reads=[otk, "srms", "gg"], writes=[otk])

            def emit_og(jp):
                ot = o_tm[jp % 2]
                otk = f"otm{jp % 2}"

                def og_tr(t, ot=ot):
                    last = None
                    for dc in range(4):
                        last = t.transpose(pb[2][:, dc * 128:dc * 128 + 128],
                                           ot[:, dc * 128:(dc + 1) * 128], ident[:])
                    return last
                OP("tensor", og_tr, reads=[otk, "ident"], writes=[PB[2]])
                OP("scalar", lambda sc, jp=jp: sc.activation(
                   out=ogh_h[jp][:], in_=pb[2][:, 0:512],
                   func=AF.Copy, bias=0.0, scale=1.0),
                   reads=[PB[2]], writes=[f"ogh{jp}"])
                OP("vector", lambda v, jp=jp: v.scalar_tensor_tensor(
                   out=ogh_l[jp][:], in0=pb[2][:, 0:512], scalar=1.0,
                   in1=ogh_h[jp][:], op0=ALU.mult, op1=ALU.subtract),
                   reads=[PB[2], f"ogh{jp}"], writes=[f"og{jp}"])

            for j in range(JC):
                x = j % 2
                if j > 0:
                    emit_otm_evac(j - 1)

                def kh_tr(t, x=x):
                    last = None
                    for c2 in range(2):
                        last = t.transpose(pb[4][:, c2 * 128:c2 * 128 + 128],
                                           khf[x][:, c2 * C:(c2 + 1) * C], ident[:])
                    return last
                OP("tensor", kh_tr, reads=[f"khf0_{x}", f"khf1_{x}", "ident"],
                   writes=[PB[4]])
                OP("scalar", lambda sc: sc.activation(out=khtm[:, 0:256],
                   in_=pb[4][:, 0:256], func=AF.Copy, bias=0.0, scale=1.0),
                   reads=[PB[4]], writes=["khtm0", "khtm1"])

                def p_mms(t, x=x):
                    last = None
                    for c2 in range(2):
                        last = t.matmul(pb[7][:, 0:128],
                                        kt[x][:, c2 * C:(c2 + 1) * C],
                                        qt[x][:, c2 * C:(c2 + 1) * C],
                                        start=(c2 == 0), stop=(c2 == 1))
                    return last
                OP("tensor", p_mms,
                   reads=[f"qt0_{x}", f"qt1_{x}", f"kt0_{x}", f"kt1_{x}"],
                   writes=[PB[7]])
                OP("vector", lambda v: v.tensor_tensor(out=p_sm[:],
                   in0=pb[7][:, 0:128], in1=triu[:], op=ALU.mult),
                   reads=[PB[7], "triu"], writes=["psm"])

                def o_mms(t, j=j, h=h, x=x):
                    last = None
                    for half in range(2):
                        sl = slice(half * 256, half * 256 + 256)
                        last = t.matmul(pb[5 + half][:, 0:256], p_sm[:],
                                        vth[j][:, sl], start=True, stop=False)
                    for c2 in range(2):
                        for half in range(2):
                            sl = slice(half * 256, half * 256 + 256)
                            last = t.matmul(pb[5 + half][:, 0:256],
                                            qt[x][:, c2 * C:(c2 + 1) * C],
                                            Sst[h][c2][:, sl],
                                            start=False, stop=(c2 == 1))
                    return last
                OP("tensor", o_mms,
                   reads=["psm", f"v{j}", f"qt0_{x}", f"qt1_{x}",
                          f"S{h}0", f"S{h}1"],
                   writes=[PB[5], PB[6]])

                for c2 in range(2):
                    def s_mms(t, c2=c2, j=j):
                        last = None
                        for half in range(2):
                            sl = slice(half * 256, half * 256 + 256)
                            last = t.matmul(pb[7][:, sl],
                                            khtm[:, c2 * C:(c2 + 1) * C],
                                            vth[j][:, sl], start=True, stop=True)
                        return last
                    OP("tensor", s_mms, reads=[f"khtm{c2}", f"v{j}"],
                       writes=[PB[7]])
                    OP("vector", lambda v, h=h, c2=c2, x=x: v.scalar_tensor_tensor(
                       out=Sst[h][c2][:], in0=Sst[h][c2][:],
                       scalar=alast[x][:, c2:c2 + 1], in1=pb[7][:, 0:HDV],
                       op0=ALU.mult, op1=ALU.add),
                       reads=[f"S{h}{c2}", f"alast{c2}_{x}", PB[7]],
                       writes=[f"S{h}{c2}"])

                if j + 1 < JC:
                    emit_gates(j + 1)
                if j > 0:
                    emit_epilogue(j - 1)
                if j > 1:
                    emit_og(j - 2)

            emit_otm_evac(JC - 1)
            emit_epilogue(JC - 1)
            emit_og(JC - 2)
            emit_og(JC - 1)

            for j in range(JC):
                def wo_mms(t, j=j):
                    last = None
                    first = [True, True]
                    for dc in range(4):
                        oh = ogh_h[j][:, dc * 128:(dc + 1) * 128]
                        ol = ogh_l[j][:, dc * 128:(dc + 1) * 128]
                        for dpart in range(2):
                            bk = 2 + dpart
                            dsl = slice(dc * 512, (dc + 1) * 512)
                            last = t.matmul(pb[bk][:, 0:512], oh,
                                            woh_cs[dpart][:, dsl],
                                            start=first[dpart], stop=False)
                            first[dpart] = False
                            last = t.matmul(pb[bk][:, 0:512], oh,
                                            wol_cs[dpart][:, dsl],
                                            start=False, stop=False)
                            last = t.matmul(pb[bk][:, 0:512], ol,
                                            woh_cs[dpart][:, dsl],
                                            start=False, stop=(dc == 3))
                    return last
                OP("tensor", wo_mms,
                   reads=["wohcs0", "wolcs0", "wohcs1", "wolcs1",
                          f"ogh{j}", f"og{j}"],
                   writes=[PB[2], PB[3]])
                for dpart in range(2):
                    sl0 = dpart * 512
                    bank = 2 + dpart
                    if h == 0:
                        OP("vector", lambda v, j=j, sl0=sl0, bank=bank:
                           v.scalar_tensor_tensor(
                           out=attn_c[j][:, sl0:sl0 + 512],
                           in0=pb[bank][:, 0:512], scalar=1.0,
                           in1=b2bc[:, sl0:sl0 + 512],
                           op0=ALU.mult, op1=ALU.add),
                           reads=[PB[bank], "b2bc"], writes=[f"attn{j}"])
                    else:
                        OP("vector", lambda v, j=j, sl0=sl0, bank=bank:
                           v.scalar_tensor_tensor(
                           out=attn_c[j][:, sl0:sl0 + 512],
                           in0=pb[bank][:, 0:512], scalar=1.0,
                           in1=attn_c[j][:, sl0:sl0 + 512],
                           op0=ALU.mult, op1=ALU.add),
                           reads=[PB[bank], f"attn{j}"], writes=[f"attn{j}"])
            if h + 1 < n_heads:
                load_wo(h + 1)
            elif tile + 1 < n_tiles:
                load_wo(0)

        for grp in range(8):
            jp0 = grp * 2
            for jpi in range(2):
                jp = jp0 + jpi
                wb = jpi
                for b in range(2):
                    jw = 2 * jp + b
                    slot = jw % 4
                    bank = slot % 2

                    def f1_all(t, b=b, wb=wb, bank=bank):
                        last = None
                        first = True
                        for k in range(8):
                            co = k * 256 + b * 128
                            for (wgt, mov) in ((wvh_cs[wb], xnf_h),
                                               (wvh_cs[wb], xnf_l),
                                               (wvl_cs[wb], xnf_h)):
                                lo_pass = wgt is wvl_cs[wb]
                                w_ap = (wvl_cs[wb] if lo_pass else wvh_cs[wb])[:, co:co + 128]
                                last = t.matmul(pb[bank][:, 0:tile_tok], w_ap,
                                                mov[k][:, 0:tile_tok],
                                                start=first,
                                                stop=(k == 7 and lo_pass))
                                first = False
                        return last
                    OP("tensor", f1_all,
                       reads=[f"wvhcs{wb}", f"wvlcs{wb}"] + xnf_tile,
                       writes=[PB[bank]])
                    OP("vector", lambda v, jw=jw, bank=bank: v.tensor_scalar(
                       out=gg[:, 0:tile_tok], in0=pb[bank][:, 0:tile_tok],
                       scalar1=b1f[:, jw:jw + 1],
                       scalar2=0.0, op0=ALU.add, op1=ALU.max),
                       reads=[PB[bank], "b1f"], writes=["gg"])
                    OP("scalar", lambda sc, slot=slot: sc.activation(
                       out=h1h[:, slot * tile_tok:(slot + 1) * tile_tok],
                       in_=gg[:, 0:tile_tok],
                       func=AF.Copy, bias=0.0, scale=1.0),
                       reads=["gg"], writes=[f"h1h_{slot}"])
                    OP("vector", lambda v, slot=slot: v.scalar_tensor_tensor(
                       out=h1l[:, slot * tile_tok:(slot + 1) * tile_tok],
                       in0=gg[:, 0:tile_tok], scalar=1.0,
                       in1=h1h[:, slot * tile_tok:(slot + 1) * tile_tok],
                       op0=ALU.mult, op1=ALU.subtract),
                       reads=["gg", f"h1h_{slot}"], writes=[f"h1_{slot}"])
            for jpi in range(2):
                jp = jp0 + jpi
                if jp + 2 < 16:
                    load_w12("w1", jp + 2)
                elif tile + 1 < n_tiles:
                    load_vg("wv", 0, jp % 2)
            for slot in range(4):
                def f2_slot(t, slot=slot):
                    last = None
                    jpi, b = slot // 2, slot % 2
                    for j in range(JC):
                        hh = h1h[:, slot * tile_tok + j * C:slot * tile_tok + (j + 1) * C]
                        hl = h1l[:, slot * tile_tok + j * C:slot * tile_tok + (j + 1) * C]
                        for dh in range(2):
                            bk = 2 * j + dh
                            wsl = slice(b * 1024 + dh * 512, b * 1024 + dh * 512 + 512)
                            last = t.matmul(pb[bk][:, 0:512], hh,
                                            wgh_cs[jpi][:, wsl],
                                            start=(slot == 0), stop=False)
                            last = t.matmul(pb[bk][:, 0:512], hh,
                                            wgl_cs[jpi][:, wsl],
                                            start=False, stop=False)
                            last = t.matmul(pb[bk][:, 0:512], hl,
                                            wgh_cs[jpi][:, wsl],
                                            start=False, stop=(slot == 3))
                    return last
                OP("tensor", f2_slot,
                   reads=[f"wghcs{slot // 2}", f"wglcs{slot // 2}",
                          f"h1h_{slot}", f"h1_{slot}"],
                   writes=PB[0:8])
            for jpi in range(2):
                jp = jp0 + 2 + jpi
                if jp < 16:
                    load_w12("w2", jp)
                elif tile + 1 < n_tiles:
                    load_vg("wg", 0, jp % 2)
            if grp < 7:
                for j in range(JC):
                    for dh in range(2):
                        OP("vector", lambda v, j=j, dh=dh: v.scalar_tensor_tensor(
                           out=attn_c[j][:, dh * 512:dh * 512 + 512],
                           in0=pb[2 * j + dh][:, 0:512], scalar=1.0,
                           in1=attn_c[j][:, dh * 512:dh * 512 + 512],
                           op0=ALU.mult, op1=ALU.add),
                           reads=[PB[2 * j + dh], f"attn{j}"], writes=[f"attn{j}"])
            else:
                for j in range(JC):
                    ob = j % 2
                    r0 = t0 + j * C
                    OP("sync", lambda s, ob=ob, r0=r0: s.dma_start(
                       out=x_c[ob][:], in_=x_hbm[r0:r0 + C, :]),
                       writes=[f"x{ob}"], dma=True, stream=f"x{ob}")
                    for dh in range(2):
                        OP("vector", lambda v, j=j, dh=dh:
                           v.scalar_tensor_tensor(
                           out=attn_c[j][:, dh * 512:dh * 512 + 512],
                           in0=pb[2 * j + dh][:, 0:512], scalar=1.0,
                           in1=attn_c[j][:, dh * 512:dh * 512 + 512],
                           op0=ALU.mult, op1=ALU.add),
                           reads=[PB[2 * j + dh], f"attn{j}"], writes=[f"attn{j}"])
                    OP("vector", lambda v, j=j, ob=ob: v.tensor_tensor(
                       out=attn_c[j][:], in0=attn_c[j][:], in1=x_c[ob][:],
                       op=ALU.add),
                       reads=[f"attn{j}", f"x{ob}"], writes=[f"attn{j}"])
                    OP("sync", lambda s, j=j, r0=r0: s.dma_start(
                       out=out_hbm[r0:r0 + C, :], in_=attn_c[j][:]),
                       reads=[f"attn{j}"], dma=True, stream="out")
    p.wait_all_streams("sync")
    p.build()
    return nc, p


_CACHE = {}


def _get_program(n_tok):
    if n_tok not in _CACHE:
        _CACHE[n_tok] = build_program(n_tok)
    return _CACHE[n_tok]


def _layout_weights(Wq, Wk, Wv, Wg, Wgk1, Wgk2, bgk2, Wo, gnw, w1, b1, w2, b2):
    c = np.ascontiguousarray
    return {
        "wq_l": c(Wq.reshape(8, 128, 8, 128).transpose(2, 1, 0, 3).reshape(8, 128, 1024)),
        "wk_l": c(Wk.reshape(8, 128, 8, 128).transpose(2, 1, 0, 3).reshape(8, 128, 1024)),
        "wv_l": c(Wv.reshape(8, 128, 8, 256).transpose(2, 1, 0, 3).reshape(8, 128, 2048)),
        "wg_l": c(Wg.reshape(8, 128, 8, 256).transpose(2, 1, 0, 3).reshape(8, 128, 2048)),
        "wo_l": c(Wo.reshape(4, 4, 128, 2, 512).transpose(0, 3, 2, 1, 4).reshape(8, 128, 2048)),
        "w1_l": c(w1.reshape(8, 128, 16, 256).transpose(2, 1, 0, 3).reshape(16, 128, 2048)),
        "w2_l": c(w2.reshape(16, 2, 128, 1024).transpose(0, 2, 1, 3).reshape(16, 128, 2048)),
        "wgk1_l": c(Wgk1.reshape(8, 128, R).transpose(1, 0, 2).reshape(128, 8 * R)),
        "wgk2": c(Wgk2),
        "bgk2_l": c(bgk2.reshape(8, 128).T),
        "b1_l": c(b1.reshape(32, 128).T),
        "b2_l": c(b2.reshape(1, D)),
        "gnw_l": c(gnw.reshape(4, 128).T),
    }


def kernel(L0_attn_x, L0_attn_Wq, L0_attn_Wk, L0_attn_Wv, L0_attn_Wg,
           L0_attn_Wgk1, L0_attn_Wgk2, L0_attn_bgk2, L0_attn_Wo,
           L0_attn_g_norm_weight, L0_ffn_w1, L0_ffn_b1, L0_ffn_w2, L0_ffn_b2):
    x = np.ascontiguousarray(np.asarray(L0_attn_x, dtype=np.float32))
    B, N, _ = x.shape
    wmap = _layout_weights(
        np.asarray(L0_attn_Wq, np.float32), np.asarray(L0_attn_Wk, np.float32),
        np.asarray(L0_attn_Wv, np.float32), np.asarray(L0_attn_Wg, np.float32),
        np.asarray(L0_attn_Wgk1, np.float32), np.asarray(L0_attn_Wgk2, np.float32),
        np.asarray(L0_attn_bgk2, np.float32), np.asarray(L0_attn_Wo, np.float32),
        np.asarray(L0_attn_g_norm_weight, np.float32),
        np.asarray(L0_ffn_w1, np.float32), np.asarray(L0_ffn_b1, np.float32),
        np.asarray(L0_ffn_w2, np.float32), np.asarray(L0_ffn_b2, np.float32))
    nc, _ = _get_program(N)
    in_maps = [dict(wmap, x=x[b]) for b in range(B)]
    res = run_bass_kernel_spmd(nc, in_maps, core_ids=list(range(B)))
    out = np.stack([res.results[b]["out"] for b in range(B)], axis=0)
    return out


# version 38 (final)
# speedup vs baseline: 1.1940x; 1.0330x over previous
import numpy as np
import concourse.bass as bass
import concourse.mybir as mybir
import concourse.bass_utils as _bu
from concourse.bass_utils import run_bass_kernel_spmd

if not getattr(_bu, "_ldwopt_patched", False):
    _orig_run_command = _bu.run_command

    def _run_command_ldwopt(argv, **kwargs):
        argv = ["--enable-ldw-opt=true" if a == "--enable-ldw-opt=false" else a
                for a in argv]
        return _orig_run_command(argv, **kwargs)

    _bu.run_command = _run_command_ldwopt
    _bu._ldwopt_patched = True

F32 = mybir.dt.float32
BF16 = mybir.dt.bfloat16
AF = mybir.ActivationFunctionType
ALU = mybir.AluOpType
AXX = mybir.AxisListType.X

D = 1024
HEADS = 4
HDK = 256
HDV = 512
R = 16
C = 128
EPS = 1e-5
LN16 = float(np.log(np.float32(16.0)))


class Prog:
    ENGINES = ("tensor", "vector", "scalar", "gpsimd", "sync")

    def __init__(self, nc):
        self.nc = nc
        self.sem = {e: nc.alloc_semaphore(f"S{e}") for e in self.ENGINES}
        self.stream_sems = {}
        self.cnt = {id(s): 0 for s in self.sem.values()}
        self.sem_by_id = {id(s): s for s in self.sem.values()}
        self.ops = {e: [] for e in self.ENGINES}
        self.res = {}
        self.waited = {e: {} for e in self.ENGINES}
        self.n_ops = 0
        self.n_waits = 0

    def _stream_sem(self, stream):
        s = self.stream_sems.get(stream)
        if s is None:
            s = self.nc.alloc_semaphore(f"D_{stream}")
            self.stream_sems[stream] = s
            self.cnt[id(s)] = 0
            self.sem_by_id[id(s)] = s
        return s

    def _deps(self, engine, reads, writes):
        deps = {}

        def add(tag):
            if tag is None:
                return
            sid, val, is_dma = tag
            if is_dma:
                val = self.cnt[sid]
            if deps.get(sid, 0) < val:
                deps[sid] = val

        for r in reads:
            e = self.res.get(r)
            if e:
                add(e["w"])
        for w in writes:
            e = self.res.get(w)
            if e:
                add(e["w"])
                for t in e["r"]:
                    add(t)
        waits = []
        wt = self.waited[engine]
        for sid, val in deps.items():
            if wt.get(sid, 0) < val:
                waits.append((sid, val))
                wt[sid] = val
        return waits

    def op(self, engine, fn, reads=(), writes=(), dma=False, stream=None):
        if engine == "scalar":
            reads = list(reads) + ["cbias"]
        waits = self._deps(engine, reads, writes)
        self.n_waits += len(waits)
        if dma:
            s = self._stream_sem(stream or ("dma_" + engine))
            self.cnt[id(s)] += 16
            tag = (id(s), self.cnt[id(s)], True)
            inc_sem, inc_by = s, 16
        else:
            s = self.sem[engine]
            self.cnt[id(s)] += 1
            tag = (id(s), self.cnt[id(s)], False)
            inc_sem, inc_by = s, 1
        for w in writes:
            self.res[w] = {"w": tag, "r": []}
        for r in reads:
            e = self.res.setdefault(r, {"w": None, "r": []})
            e["r"].append(tag)
        self.ops[engine].append((waits, fn, inc_sem, inc_by))
        self.n_ops += 1
        return tag

    def wait_all_streams(self, engine, streams=None):
        names = list(self.stream_sems) if streams is None else list(streams)
        targets = [(id(self.stream_sems[n]), self.cnt[id(self.stream_sems[n])])
                   for n in names if n in self.stream_sems]

        def fn(eng):
            for sid, val in targets:
                if val > 0:
                    eng.wait_ge(self.sem_by_id[sid], val)
            return eng.nop()

        self.ops[engine].append(([], fn, self.sem[engine], 1))
        self.cnt[id(self.sem[engine])] += 1
        self.n_ops += 1

    def build(self):
        nc = self.nc
        with nc.Block() as block:
            for engine in self.ENGINES:
                ops = self.ops[engine]
                if not ops:
                    continue
                dec = getattr(block, engine)

                def make(ops):
                    def body(eng):
                        for waits, fn, inc_sem, inc_by in ops:
                            for sid, val in waits:
                                eng.wait_ge(self.sem_by_id[sid], val)
                            inst = fn(eng)
                            if inst is not None:
                                inst.then_inc(inc_sem, inc_by)
                    return body

                dec(make(list(ops)))
        return nc


def build_program(n_tok=2048, tile_tok=512, n_heads=HEADS):
    assert tile_tok % C == 0 and n_tok % tile_tok == 0
    n_tiles = n_tok // tile_tok
    JC = tile_tok // C
    NH2 = tile_tok // 256

    nc = bass.Bass(trn_type="TRN2", target_bir_lowering=False, debug=False)
    din = lambda name, shp: nc.dram_tensor(name, shp, F32, kind="ExternalInput").ap()
    x_hbm = din("x", [n_tok, D])
    wq_l = din("wq_l", [8, 128, 1024])
    wk_l = din("wk_l", [8, 128, 1024])
    wv_l = din("wv_l", [8, 128, 2048])
    wg_l = din("wg_l", [8, 128, 2048])
    wo_l = din("wo_l", [8, 128, 2048])
    w1_l = din("w1_l", [16, 128, 2048])
    w2_l = din("w2_l", [16, 128, 2048])
    wgk1_l = din("wgk1_l", [128, 8 * R])
    dscr = lambda name, shp: nc.dram_tensor(name, shp, BF16, kind="Internal").ap()
    wq_hi = dscr("wq_hi", [8, 128, 1024]); wq_lo = dscr("wq_lo", [8, 128, 1024])
    wk_hi = dscr("wk_hi", [8, 128, 1024]); wk_lo = dscr("wk_lo", [8, 128, 1024])
    wv_hi = dscr("wv_hi", [8, 128, 2048]); wv_lo = dscr("wv_lo", [8, 128, 2048])
    wg_hi = dscr("wg_hi", [8, 128, 2048]); wg_lo = dscr("wg_lo", [8, 128, 2048])
    wo_hi = dscr("wo_hi", [8, 128, 2048]); wo_lo = dscr("wo_lo", [8, 128, 2048])
    w1_hi = dscr("w1_hi", [16, 128, 2048]); w1_lo = dscr("w1_lo", [16, 128, 2048])
    w2_hi = dscr("w2_hi", [16, 128, 2048]); w2_lo = dscr("w2_lo", [16, 128, 2048])
    wgk2_h = din("wgk2", [R, 1024])
    bgk2_l = din("bgk2_l", [128, 8])
    b1_l = din("b1_l", [128, 32])
    b2_l = din("b2_l", [1, D])
    gnw_l = din("gnw_l", [128, 4])
    out_hbm = nc.dram_tensor("out", [n_tok, D], F32, kind="ExternalOutput").ap()

    S_ = lambda name, shp, d=F32: nc.alloc_sbuf_tensor(name, shp, d)

    ident = S_("c_ident", [128, 128])
    triu = S_("c_triu", [128, 128])
    ones128 = S_("c_ones", [128, 128])
    onesrow = S_("c_onesrow", [1, 128])
    wgk2 = S_("r_wgk2", [16, 1024])
    bgk2n = S_("r_bgk2n", [128, 8])
    b1f = S_("r_b1f", [128, 32])
    b2row = S_("r_b2row", [1, D])
    b2bc = S_("r_b2bc", [128, D])
    gnwf = S_("r_gnwf", [128, 4])
    Sst = [[S_(f"st_S_{h}_{c2}", [128, HDV]) for c2 in range(2)] for h in range(n_heads)]

    x_c = [S_(f"t_x{i}", [128, D]) for i in range(2)]
    xnf_h = [S_(f"t_xnfh{k}", [128, tile_tok], BF16) for k in range(8)]
    xnf_l = [S_(f"t_xnfl{k}", [128, tile_tok], BF16) for k in range(8)]
    attn_c = [S_(f"t_attn{j}", [128, D]) for j in range(JC)]
    qf = S_("h_qf", [128, 2 * tile_tok])
    kf = S_("h_kf", [128, 2 * tile_tok])
    spf = S_("h_spf", [128, 2 * tile_tok])
    vth = [S_(f"h_v{j}", [128, HDV]) for j in range(JC)]
    gth = [S_(f"h_g{j}", [128, HDV]) for j in range(JC)]
    ogh_h = [S_(f"h_ogh{j}", [128, HDV], BF16) for j in range(JC)]
    ogh_l = [S_(f"h_ogl{j}", [128, HDV], BF16) for j in range(JC)]
    u_sb = S_("h_u", [16, tile_tok])
    at = [S_(f"k_at{i}", [128, 2 * C]) for i in range(2)]
    qt = [S_(f"k_qt{i}", [128, 2 * C]) for i in range(2)]
    kt = [S_(f"k_kt{i}", [128, 2 * C]) for i in range(2)]
    khf = [S_(f"k_khf{i}", [128, 2 * C]) for i in range(2)]
    khtm = S_("k_khtm", [128, 2 * C])
    alast = [S_(f"k_alast{i}", [128, 2]) for i in range(2)]
    p_sm = S_("k_p", [128, C])
    o_tm = [S_(f"k_otm{i}", [128, HDV]) for i in range(2)]
    sil_s = S_("k_sils", [128, HDV])
    gg = S_("k_gg", [128, HDV])
    bscr = S_("k_bscr", [128, D], BF16)
    xfull = S_("a_xfull", [128, 512])
    negml = S_("a_negml", [128, 2])
    ln_sx = [S_(f"k_sx{i}", [128, 1]) for i in range(2)]
    ln_sxx = [S_(f"k_sxx{i}", [128, 1]) for i in range(2)]
    ln_m = [S_(f"k_m{i}", [128, 1]) for i in range(2)]
    ln_v = [S_(f"k_v{i}", [128, 1]) for i in range(2)]
    ln_inv = [S_(f"k_inv{i}", [128, 1]) for i in range(2)]
    xn_tm = S_("k_xntm", [128, D])
    ssum = S_("k_ssum", [128, 1])
    srms = S_("k_srms", [128, 1])
    wqh_cs = [S_(f"w_wqh{i}", [128, 1024], BF16) for i in range(2)]
    wql_cs = [S_(f"w_wql{i}", [128, 1024], BF16) for i in range(2)]
    wkh_cs = [S_(f"w_wkh{i}", [128, 1024], BF16) for i in range(2)]
    wkl_cs = [S_(f"w_wkl{i}", [128, 1024], BF16) for i in range(2)]
    wvh_cs = [S_(f"w_wvh{i}", [128, 2048], BF16) for i in range(2)]
    wvl_cs = [S_(f"w_wvl{i}", [128, 2048], BF16) for i in range(2)]
    wgh_cs = [S_(f"w_wgh{i}", [128, 2048], BF16) for i in range(2)]
    wgl_cs = [S_(f"w_wgl{i}", [128, 2048], BF16) for i in range(2)]
    woh_cs = [S_(f"w_woh{i}", [128, 2048], BF16) for i in range(2)]
    wol_cs = [S_(f"w_wol{i}", [128, 2048], BF16) for i in range(2)]
    wgk1h = S_("r_wgk1h", [128, 8 * R], BF16)
    wgk1l = S_("r_wgk1l", [128, 8 * R], BF16)
    h1h = S_("f_h1h", [128, 4 * tile_tok], BF16)
    h1l = S_("f_h1l", [128, 4 * tile_tok], BF16)

    pb = [nc.alloc_psum_tensor(f"pb{i}", [128, 512], F32) for i in range(8)]
    PB = [f"pb{i}" for i in range(8)]

    p = Prog(nc)
    OP = p.op

    cbias_vals = [0.0, 1.0, EPS, -LN16]
    cbias = S_("c_bias2", [128, len(cbias_vals)])
    for i, v in enumerate(cbias_vals):
        OP("gpsimd", lambda g, i=i, v=v: g.memset(cbias[:, i:i + 1], float(v)),
           writes=["cbias"])
        nc.const_aps.register(F32, float(v), cbias[:, i:i + 1])

    OP("gpsimd", lambda g: g.memset(ident[:], 1.0), writes=["ident"])
    OP("gpsimd", lambda g: g.affine_select(out=ident[:], in_=ident[:],
        compare_op=ALU.is_equal, fill=0.0, base=0, pattern=[[1, 128]],
        channel_multiplier=-1), reads=["ident"], writes=["ident"])
    OP("gpsimd", lambda g: g.memset(triu[:], 1.0), writes=["triu"])
    OP("gpsimd", lambda g: g.affine_select(out=triu[:], in_=triu[:],
        compare_op=ALU.is_ge, fill=0.0, base=0, pattern=[[1, 128]],
        channel_multiplier=-1), reads=["triu"], writes=["triu"])
    OP("gpsimd", lambda g: g.memset(ones128[:], 1.0), writes=["ones128"])
    OP("gpsimd", lambda g: g.memset(onesrow[:], 1.0), writes=["onesrow"])
    for h in range(n_heads):
        for c2 in range(2):
            OP("gpsimd", lambda g, h=h, c2=c2: g.memset(Sst[h][c2][:], 0.0),
               writes=[f"S{h}{c2}"])
    OP("sync", lambda s: s.dma_start(out=wgk2[:], in_=wgk2_h),
       writes=["wgk2"], dma=True, stream="cst_wgk2")
    OP("sync", lambda s: s.dma_start(out=bgk2n[:], in_=bgk2_l),
       writes=["bgk2n"], dma=True, stream="cst_bgk2n")
    OP("vector", lambda v: v.tensor_scalar(out=bgk2n[:], in0=bgk2n[:],
       scalar1=-1.0, scalar2=None, op0=ALU.mult), reads=["bgk2n"], writes=["bgk2n"])
    OP("sync", lambda s: s.dma_start(out=b1f[:], in_=b1_l),
       writes=["b1f"], dma=True, stream="cst_b1f")
    OP("sync", lambda s: s.dma_start(out=b2row[:], in_=b2_l),
       writes=["b2row"], dma=True, stream="cst_b2row")
    OP("sync", lambda s: s.dma_start(out=gnwf[:], in_=gnw_l),
       writes=["gnwf"], dma=True, stream="cst_gnwf")

    def bcast_row(dst, dkey, srcrow, skey, width, bank):
        for g0 in range(0, width, 512):
            w = min(512, width - g0)

            def mmg(t, g0=g0, w=w):
                last = None
                for c0 in range(0, w, 256):
                    last = t.matmul(pb[bank][:, c0:c0 + 256], onesrow[0:1, :],
                                    srcrow[0:1, g0 + c0:g0 + c0 + 256],
                                    start=True, stop=True)
                return last
            OP("tensor", mmg, reads=[skey, "onesrow"], writes=[PB[bank]])
            OP("vector", lambda v, g0=g0, w=w: v.tensor_copy(
               dst[:, g0:g0 + w], pb[bank][:, 0:w]),
               reads=[PB[bank]], writes=[dkey])

    bcast_row(b2bc, "b2bc", b2row, "b2row", D, 6)

    split_jobs = []
    PWS = 512
    def add_split(matname, mat_fp32, hi, lo, nslab, W):
        for s in range(nslab):
            for c0 in range(0, W, PWS):
                dc = (c0 // 512) if matname == "wo" else None
                split_jobs.append((matname, mat_fp32[s, :, c0:c0 + PWS],
                                   hi[s, :, c0:c0 + PWS], lo[s, :, c0:c0 + PWS],
                                   dc))
    add_split("wq", wq_l, wq_hi, wq_lo, 8, 1024)
    add_split("wk", wk_l, wk_hi, wk_lo, 8, 1024)
    add_split("wv", wv_l, wv_hi, wv_lo, 8, 2048)
    add_split("wg", wg_l, wg_hi, wg_lo, 8, 2048)
    add_split("wo", wo_l, wo_hi, wo_lo, 8, 2048)
    add_split("w1", w1_l, w1_hi, w1_lo, 16, 2048)
    add_split("w2", w2_l, w2_hi, w2_lo, 16, 2048)
    _mat_order = ["wq", "wk", "wv", "wg", "wo", "w1", "w2"]
    SPLIT_PIECE_KEYS = [f"splh{i}" for i in range(12)] + [f"spll{i}" for i in range(12)]
    SCRK = lambda mn: ([f"scr_{mn}_h{s}" for s in range(12)]
                       + [f"scr_{mn}_l{s}" for s in range(12)])
    split_ctr = [0]

    def emit_split_until(matname):
        n = 0
        for i in range(split_ctr[0], len(split_jobs)):
            if split_jobs[i][0] == matname or _mat_order.index(split_jobs[i][0]) < _mat_order.index(matname):
                n += 1
            else:
                break
        emit_split(n)

    def emit_split(n):
        i0 = split_ctr[0]
        n_land = 8
        def landing(li):
            if li < 4:
                return vth[li][:], f"v{li}"
            return gth[li - 4][:], f"g{li - 4}"
        hi_all = [wvh_cs[0], wvh_cs[1], wgh_cs[0]]
        lo_all = [wvl_cs[0], wvl_cs[1], wgl_cs[0]]
        for i in range(i0, min(i0 + n, len(split_jobs))):
            mn, srcp, hip, lop, wodc = split_jobs[i]
            lb, lbk = landing(i % n_land)
            sslot = i % 12
            hcol = hi_all[sslot // 4][:, (sslot % 4) * 512:(sslot % 4) * 512 + 512]
            lcol = lo_all[sslot // 4][:, (sslot % 4) * 512:(sslot % 4) * 512 + 512]
            hkey = f"splh{sslot}"; lkey = f"spll{sslot}"
            lastp = (i == len(split_jobs) - 1)
            OP("sync", lambda s, srcp=srcp, lb=lb: s.dma_start(out=lb, in_=srcp),
               writes=[lbk], dma=True, stream=f"spl{i % n_land}")
            scl = 1.0 if wodc is None else gnwf[:, wodc:wodc + 1]
            srd = [] if wodc is None else ["gnwf"]
            OP("scalar", lambda sc, lb=lb, hcol=hcol, scl=scl: sc.activation(
               out=hcol, in_=lb, func=AF.Copy, bias=0.0, scale=scl),
               reads=[lbk] + srd, writes=[hkey])
            OP("vector", lambda v, lb=lb, hcol=hcol, lcol=lcol, scl=scl:
               v.scalar_tensor_tensor(out=lcol, in0=lb, scalar=scl, in1=hcol,
               op0=ALU.mult, op1=ALU.subtract),
               reads=[lbk, hkey] + srd, writes=[lkey])
            hwr = [f"scr_{mn}_h{sslot}"]
            lwr = [f"scr_{mn}_l{sslot}"]
            OP("scalar", lambda sc, hip=hip, hcol=hcol: sc.dma_start(
               out=hip, in_=hcol),
               reads=[hkey], writes=hwr, dma=True, stream=f"sch{sslot}")
            OP("gpsimd", lambda g, lop=lop, lcol=lcol: g.dma_start(
               out=lop, in_=lcol),
               reads=[lkey], writes=lwr, dma=True, stream=f"scl{sslot}")
        split_ctr[0] = min(i0 + n, len(split_jobs))

    def emit_wgk1_split():
        OP("sync", lambda s: s.dma_start(out=vth[0][:, 0:128], in_=wgk1_l),
           writes=["v0"], dma=True, stream="splgk")
        OP("scalar", lambda sc: sc.activation(out=wgk1h[:], in_=vth[0][:, 0:128],
           func=AF.Copy, bias=0.0, scale=1.0), reads=["v0"], writes=["wgk1h"])
        OP("vector", lambda v: v.scalar_tensor_tensor(out=wgk1l[:],
           in0=vth[0][:, 0:128], scalar=1.0, in1=wgk1h[:],
           op0=ALU.mult, op1=ALU.subtract), reads=["v0", "wgk1h"],
           writes=["wgk1l"])

    def load_qk(mat, h, c2):
        bh, bl, hi, lo, key = {
            "wq": (wqh_cs, wql_cs, wq_hi, wq_lo, "wq"),
            "wk": (wkh_cs, wkl_cs, wk_hi, wk_lo, "wk")}[mat]
        s = 2 * h + c2
        OP("sync", lambda ss, s=s: ss.dma_start(out=bh[c2][:], in_=hi[s]),
           reads=SCRK(key), writes=[f"{key}hcs{c2}"], dma=True, stream=f"{key}{c2}")
        OP("sync", lambda ss, s=s: ss.dma_start(out=bl[c2][:], in_=lo[s]),
           reads=SCRK(key), writes=[f"{key}lcs{c2}"], dma=True, stream=f"{key}{c2}")

    def load_vg(mat, h, half, ext=()):
        bh, bl, hi, lo, key = {
            "wv": (wvh_cs, wvl_cs, wv_hi, wv_lo, "wv"),
            "wg": (wgh_cs, wgl_cs, wg_hi, wg_lo, "wg")}[mat]
        s = 2 * h + half
        OP("gpsimd", lambda g, s=s: g.dma_start(out=bh[half][:], in_=hi[s]),
           reads=SCRK(key), writes=[f"{key}hcs{half}"] + list(ext), dma=True,
           stream=f"{key}{half}")
        OP("gpsimd", lambda g, s=s: g.dma_start(out=bl[half][:], in_=lo[s]),
           reads=SCRK(key), writes=[f"{key}lcs{half}"] + list(ext), dma=True,
           stream=f"{key}{half}")

    def load_w12(mat, jp):
        bh, bl, hi, lo, key, strm = {
            "w1": (wvh_cs, wvl_cs, w1_hi, w1_lo, "wv", "wv"),
            "w2": (wgh_cs, wgl_cs, w2_hi, w2_lo, "wg", "wg")}[mat]
        b = jp % 2
        OP("gpsimd", lambda g, jp=jp, b=b: g.dma_start(out=bh[b][:], in_=hi[jp]),
           reads=SCRK(mat), writes=[f"{key}hcs{b}"], dma=True, stream=f"{strm}{b}")
        OP("gpsimd", lambda g, jp=jp, b=b: g.dma_start(out=bl[b][:], in_=lo[jp]),
           reads=SCRK(mat), writes=[f"{key}lcs{b}"], dma=True, stream=f"{strm}{b}")

    def load_wo(h):
        for dpart in range(2):
            OP("sync", lambda s, h=h, dpart=dpart: s.dma_start(
               out=woh_cs[dpart][:], in_=wo_hi[2 * h + dpart]),
               reads=SCRK("wo"), writes=[f"wohcs{dpart}"], dma=True,
               stream=f"wo{dpart}")
            OP("sync", lambda s, h=h, dpart=dpart: s.dma_start(
               out=wol_cs[dpart][:], in_=wo_lo[2 * h + dpart]),
               reads=SCRK("wo"), writes=[f"wolcs{dpart}"], dma=True,
               stream=f"wo{dpart}")

    def emit_A0(a0tile):
        a0t0 = a0tile * tile_tok
        for j in range(JC):
            r0 = a0t0 + j * C
            xb = j % 2
            OP("sync", lambda s, xb=xb, r0=r0: s.dma_start(
               out=x_c[xb][:], in_=x_hbm[r0:r0 + C, :]),
               writes=[f"x{xb}"], dma=True, stream=f"x{xb}")
            OP("scalar", lambda sc, xb=xb: sc.activation(out=bscr[:], in_=x_c[xb][:],
               func=AF.Copy, bias=0.0, scale=1.0, accum_out=ln_sx[xb][:]),
               reads=[f"x{xb}", "cbias"], writes=["bscr", f"lsx{xb}"])
            OP("scalar", lambda sc, xb=xb: sc.activation(out=bscr[:], in_=x_c[xb][:],
               func=AF.Square, accum_out=ln_sxx[xb][:]),
               reads=[f"x{xb}", "cbias"], writes=["bscr", f"lsxx{xb}"])
            OP("gpsimd", lambda g, xb=xb: g.tensor_scalar(out=ln_m[xb][:],
               in0=ln_sx[xb][:], scalar1=1.0 / D, scalar2=None, op0=ALU.mult),
               reads=[f"lsx{xb}"], writes=[f"lm{xb}"])
            OP("gpsimd", lambda g, xb=xb: g.tensor_tensor(out=ln_v[xb][:],
               in0=ln_m[xb][:], in1=ln_m[xb][:], op=ALU.mult),
               reads=[f"lm{xb}"], writes=[f"lv{xb}"])
            OP("gpsimd", lambda g, xb=xb: g.tensor_scalar(out=ln_sxx[xb][:],
               in0=ln_sxx[xb][:], scalar1=1.0 / D, scalar2=None, op0=ALU.mult),
               reads=[f"lsxx{xb}"], writes=[f"lsxx{xb}"])
            OP("gpsimd", lambda g, xb=xb: g.tensor_tensor(out=ln_v[xb][:],
               in0=ln_sxx[xb][:], in1=ln_v[xb][:], op=ALU.subtract),
               reads=[f"lsxx{xb}", f"lv{xb}"], writes=[f"lv{xb}"])
            OP("scalar", lambda sc, xb=xb: sc.activation(out=ln_inv[xb][:],
               in_=ln_v[xb][:], func=AF.Ln, bias=EPS, scale=1.0),
               reads=[f"lv{xb}", "cbias"], writes=[f"linv{xb}"])
            OP("scalar", lambda sc, xb=xb: sc.activation(out=ln_inv[xb][:],
               in_=ln_inv[xb][:], func=AF.Exp, bias=0.0, scale=-0.5),
               reads=[f"linv{xb}", "cbias"], writes=[f"linv{xb}"])
            OP("gpsimd", lambda g, xb=xb: g.tensor_tensor(
               out=negml[:, xb:xb + 1], in0=ln_m[xb][:], in1=ln_inv[xb][:],
               op=ALU.mult),
               reads=[f"lm{xb}", f"linv{xb}"], writes=[f"negml{xb}"])
            OP("gpsimd", lambda g, xb=xb: g.tensor_scalar(
               out=negml[:, xb:xb + 1], in0=negml[:, xb:xb + 1],
               scalar1=-1.0, scalar2=None, op0=ALU.mult),
               reads=[f"negml{xb}"], writes=[f"negml{xb}"])
            OP("scalar", lambda sc, xb=xb: sc.activation(out=xn_tm[:],
               in_=x_c[xb][:], func=AF.Identity, bias=negml[:, xb:xb + 1],
               scale=ln_inv[xb][:]),
               reads=[f"x{xb}", f"negml{xb}", f"linv{xb}"], writes=["xntm"])
            for k in range(8):
                bank = k % 2
                OP("tensor", lambda t, k=k, bank=bank: t.transpose(
                   pb[bank][:, 0:128], xn_tm[:, k * 128:(k + 1) * 128], ident[:]),
                   reads=["xntm", "ident"], writes=[PB[bank]])
                k4 = k % 4
                OP("scalar", lambda sc, k4=k4, bank=bank: sc.activation(
                   out=xfull[:, k4 * 128:k4 * 128 + 128],
                   in_=pb[bank][:, 0:128],
                   func=AF.Copy, bias=0.0, scale=1.0),
                   reads=[PB[bank]], writes=[f"xfull{k4}"])
                OP("scalar", lambda sc, k=k, j=j, k4=k4: sc.activation(
                   out=xnf_h[k][:, j * C:(j + 1) * C],
                   in_=xfull[:, k4 * 128:k4 * 128 + 128],
                   func=AF.Copy, bias=0.0, scale=1.0),
                   reads=[f"xfull{k4}"], writes=[f"xnfh{k}_{j}"])
                OP("vector", lambda v, k=k, j=j, k4=k4: v.scalar_tensor_tensor(
                   out=xnf_l[k][:, j * C:(j + 1) * C],
                   in0=xfull[:, k4 * 128:k4 * 128 + 128],
                   scalar=1.0, in1=xnf_h[k][:, j * C:(j + 1) * C],
                   op0=ALU.mult, op1=ALU.subtract),
                   reads=[f"xfull{k4}", f"xnfh{k}_{j}"], writes=[f"xnf{k}_{j}"])

    emit_A0(0)
    emit_wgk1_split()
    emit_split(10 ** 6)

    for tile in range(n_tiles):
        t0 = tile * tile_tok
        if tile > 0:
            emit_A0(tile)

        xnf_tile = [f"xnf{k}_{j}" for k in range(8) for j in range(JC)]

        PW = min(512, tile_tok)
        for p5 in range(tile_tok // PW):
            bank = 4 + (p5 % 2)
            sl = slice(p5 * PW, p5 * PW + PW)

            def u_one(t, sl=sl, bank=bank):
                last = None
                first = True
                for k in range(8):
                    for (wgt, mov) in ((wgk1h, xnf_h), (wgk1h, xnf_l), (wgk1l, xnf_h)):
                        last = t.matmul(pb[bank][0:16, 0:PW],
                                        wgt[:, k * R:(k + 1) * R], mov[k][:, sl],
                                        start=first, stop=(k == 7 and wgt is wgk1l))
                        first = False
                return last
            OP("tensor", u_one, reads=["wgk1h", "wgk1l"] + xnf_tile,
               writes=[PB[bank]])
            OP("vector", lambda v, sl=sl, bank=bank: v.tensor_copy(
               u_sb[:, sl], pb[bank][0:16, 0:PW]),
               reads=[PB[bank]], writes=["u"])

        for h in range(n_heads):
            buf = h % 2
            if tile == 0 and h == 0:
                for c2 in range(2):
                    load_qk("wq", 0, c2)
                    load_qk("wk", 0, c2)
                for half in range(2):
                    load_vg("wv", 0, half, ext=SPLIT_PIECE_KEYS)
                    load_vg("wg", 0, half, ext=SPLIT_PIECE_KEYS)
                load_wo(0)

            for c2 in range(2):
                for (mat, wh, wl, dstf, dkey) in (
                        ("wq", wqh_cs[c2], wql_cs[c2], qf, "qf"),
                        ("wk", wkh_cs[c2], wkl_cs[c2], kf, "kf")):
                    whk = f"{mat}hcs{c2}"; wlk = f"{mat}lcs{c2}"
                    for p5 in range(tile_tok // PW):
                        bk = (0 if mat == "wq" else 2) + (p5 % 2)
                        sl = slice(p5 * PW, p5 * PW + PW)

                        def qk_one(t, wh=wh, wl=wl, sl=sl, bk=bk):
                            last = None
                            first = True
                            for k in range(8):
                                for (wgt, mov) in ((wh, xnf_h), (wh, xnf_l),
                                                   (wl, xnf_h)):
                                    last = t.matmul(
                                        pb[bk][:, 0:PW],
                                        wgt[:, k * 128:(k + 1) * 128],
                                        mov[k][:, sl],
                                        start=first,
                                        stop=(k == 7 and wgt is wl))
                                    first = False
                            return last
                        OP("tensor", qk_one, reads=[whk, wlk] + xnf_tile,
                           writes=[PB[bk]])
                        OP("vector", lambda v, dstf=dstf, c2=c2, sl=sl, bk=bk:
                           v.tensor_copy(
                           dstf[:, c2 * tile_tok + sl.start:
                                c2 * tile_tok + sl.stop], pb[bk][:, 0:PW]),
                           reads=[PB[bk]], writes=[f"{dkey}{c2}"])
                    if h + 1 < n_heads:
                        load_qk(mat, h + 1, c2)
                    elif tile + 1 < n_tiles:
                        load_qk(mat, 0, c2)

            for c2 in range(2):
                d8 = h * 2 + c2

                def z_all(t, d8=d8):
                    last = None
                    for half in range(NH2):
                        last = t.matmul(
                            pb[4][:, (half % 2) * 256:(half % 2) * 256 + 256],
                            wgk2[:, d8 * 128:(d8 + 1) * 128],
                            u_sb[:, half * 256:(half + 1) * 256],
                            start=True, stop=True)
                    return last
                OP("tensor", z_all, reads=["wgk2", "u"], writes=[PB[4]])
                for half in range(NH2):
                    hb = half % 2
                    OP("scalar", lambda sc, c2=c2, d8=d8, half=half, hb=hb:
                       sc.activation(
                       out=spf[:, c2 * tile_tok + half * 256:
                               c2 * tile_tok + half * 256 + 256],
                       in_=pb[4][:, hb * 256:hb * 256 + 256], func=AF.Exp,
                       bias=bgk2n[:, d8:d8 + 1], scale=-1.0),
                       reads=[PB[4], "bgk2n"], writes=[f"sp{c2}"])
                    OP("scalar", lambda sc, c2=c2, half=half: sc.activation(
                       out=spf[:, c2 * tile_tok + half * 256:
                               c2 * tile_tok + half * 256 + 256],
                       in_=spf[:, c2 * tile_tok + half * 256:
                               c2 * tile_tok + half * 256 + 256],
                       func=AF.Ln, bias=1.0, scale=1.0),
                       reads=[f"sp{c2}", "cbias"], writes=[f"sp{c2}"])

            def emit_gates(j):
                jt = j * C
                x = j % 2
                for c2 in range(2):
                    OP("vector", lambda v, c2=c2, jt=jt, x=x: v.tensor_tensor_scan(
                       out=at[x][:, c2 * C:(c2 + 1) * C], data0=ones128[:],
                       data1=spf[:, c2 * tile_tok + jt:c2 * tile_tok + jt + C],
                       initial=0.0, op0=ALU.mult, op1=ALU.add),
                       reads=[f"sp{c2}", "ones128"], writes=[f"at{c2}_{x}"])
                    OP("scalar", lambda sc, c2=c2, x=x: sc.activation(
                       out=qt[x][:, c2 * C:(c2 + 1) * C],
                       in_=at[x][:, c2 * C:(c2 + 1) * C],
                       func=AF.Exp, bias=-LN16, scale=-1.0 / 16.0),
                       reads=[f"at{c2}_{x}", "cbias"], writes=[f"qt{c2}_{x}"])
                    OP("scalar", lambda sc, c2=c2, x=x: sc.activation(
                       out=kt[x][:, c2 * C:(c2 + 1) * C],
                       in_=at[x][:, c2 * C:(c2 + 1) * C],
                       func=AF.Exp, bias=0.0, scale=1.0 / 16.0),
                       reads=[f"at{c2}_{x}", "cbias"], writes=[f"kt{c2}_{x}"])
                    OP("scalar", lambda sc, c2=c2, x=x: sc.activation(
                       out=alast[x][:, c2:c2 + 1],
                       in_=at[x][:, (c2 + 1) * C - 1:(c2 + 1) * C],
                       func=AF.Exp, bias=0.0, scale=-1.0 / 16.0),
                       reads=[f"at{c2}_{x}", "cbias"], writes=[f"alast{c2}_{x}"])
                    OP("vector", lambda v, c2=c2, jt=jt, x=x: v.tensor_tensor(
                       out=qt[x][:, c2 * C:(c2 + 1) * C],
                       in0=qt[x][:, c2 * C:(c2 + 1) * C],
                       in1=qf[:, c2 * tile_tok + jt:c2 * tile_tok + jt + C],
                       op=ALU.mult),
                       reads=[f"qf{c2}", f"qt{c2}_{x}"], writes=[f"qt{c2}_{x}"])
                    OP("vector", lambda v, c2=c2, jt=jt, x=x: v.tensor_tensor(
                       out=kt[x][:, c2 * C:(c2 + 1) * C],
                       in0=kt[x][:, c2 * C:(c2 + 1) * C],
                       in1=kf[:, c2 * tile_tok + jt:c2 * tile_tok + jt + C],
                       op=ALU.mult),
                       reads=[f"kf{c2}", f"kt{c2}_{x}"], writes=[f"kt{c2}_{x}"])
                    OP("scalar", lambda sc, c2=c2, x=x: sc.activation(
                       out=khf[x][:, c2 * C:(c2 + 1) * C],
                       in_=kt[x][:, c2 * C:(c2 + 1) * C],
                       func=AF.Copy, bias=0.0, scale=alast[x][:, c2:c2 + 1]),
                       reads=[f"kt{c2}_{x}", f"alast{c2}_{x}"], writes=[f"khf{c2}_{x}"])

            emit_gates(0)

            for half in range(2):
                for j in range(JC):
                    bv = 2 + (j % 2)
                    bg = 4 + (j % 2)

                    def vg_one(t, j=j, half=half, bv=bv, bg=bg):
                        last = None
                        for k in range(8):
                            ksl = slice(k * 256, (k + 1) * 256)
                            xh = xnf_h[k][:, j * C:(j + 1) * C]
                            xl = xnf_l[k][:, j * C:(j + 1) * C]
                            last = t.matmul(pb[bv][:, 0:256], xh,
                                            wvh_cs[half][:, ksl],
                                            start=(k == 0), stop=False)
                            last = t.matmul(pb[bg][:, 0:256], xh,
                                            wgh_cs[half][:, ksl],
                                            start=(k == 0), stop=False)
                            last = t.matmul(pb[bv][:, 0:256], xh,
                                            wvl_cs[half][:, ksl],
                                            start=False, stop=False)
                            last = t.matmul(pb[bg][:, 0:256], xh,
                                            wgl_cs[half][:, ksl],
                                            start=False, stop=False)
                            last = t.matmul(pb[bv][:, 0:256], xl,
                                            wvh_cs[half][:, ksl],
                                            start=False, stop=(k == 7))
                            last = t.matmul(pb[bg][:, 0:256], xl,
                                            wgh_cs[half][:, ksl],
                                            start=False, stop=(k == 7))
                        return last
                    OP("tensor", vg_one,
                       reads=[f"wvhcs{half}", f"wvlcs{half}", f"wghcs{half}",
                              f"wglcs{half}"] + xnf_tile,
                       writes=[PB[bv], PB[bg]])
                    OP("vector", lambda v, j=j, half=half, bv=bv: v.tensor_copy(
                       vth[j][:, half * 256:half * 256 + 256], pb[bv][:, 0:256]),
                       reads=[PB[bv]], writes=[f"v{j}"])
                    OP("vector", lambda v, j=j, half=half, bg=bg: v.tensor_copy(
                       gth[j][:, half * 256:half * 256 + 256], pb[bg][:, 0:256]),
                       reads=[PB[bg]], writes=[f"g{j}"])
                if h + 1 < n_heads:
                    load_vg("wv", h + 1, half)
                    load_vg("wg", h + 1, half)
                elif half == 1:
                    for jp in range(2):
                        load_w12("w1", jp)
                        load_w12("w2", jp)

            def emit_otm_evac(jp):
                OP("scalar", lambda sc, jp=jp: sc.activation(
                   out=o_tm[jp % 2][:, 0:256], in_=pb[5][:, 0:256],
                   func=AF.Copy, bias=0.0, scale=1.0),
                   reads=[PB[5]], writes=[f"otm{jp % 2}"])
                OP("scalar", lambda sc, jp=jp: sc.activation(
                   out=o_tm[jp % 2][:, 256:512], in_=pb[6][:, 0:256],
                   func=AF.Copy, bias=0.0, scale=1.0),
                   reads=[PB[6]], writes=[f"otm{jp % 2}"])

            def emit_epilogue(jp):
                ot = o_tm[jp % 2]
                otk = f"otm{jp % 2}"
                OP("scalar", lambda sc, ot=ot: sc.activation(out=bscr[:, 0:HDV],
                   in_=ot[:], func=AF.Square, accum_out=ssum[:]),
                   reads=[otk], writes=["bscr", "ssum"])
                OP("scalar", lambda sc: sc.activation(out=srms[:], in_=ssum[:],
                   func=AF.Ln, bias=EPS, scale=1.0 / HDV),
                   reads=["ssum", "cbias"], writes=["srms"])
                OP("scalar", lambda sc: sc.activation(out=srms[:], in_=srms[:],
                   func=AF.Exp, bias=0.0, scale=-0.5),
                   reads=["srms", "cbias"], writes=["srms"])
                OP("scalar", lambda sc, jp=jp: sc.activation(out=sil_s[:],
                   in_=gth[jp][:], func=AF.Exp, bias=0.0, scale=-1.0),
                   reads=[f"g{jp}", "cbias"], writes=["sils"])
                OP("scalar", lambda sc: sc.activation(out=sil_s[:], in_=sil_s[:],
                   func=AF.Copy, bias=1.0, scale=1.0),
                   reads=["sils"], writes=["sils"])
                OP("vector", lambda v: v.reciprocal(sil_s[:], sil_s[:]),
                   reads=["sils"], writes=["sils"])
                OP("vector", lambda v, jp=jp: v.tensor_tensor(out=gg[:],
                   in0=gth[jp][:], in1=sil_s[:], op=ALU.mult),
                   reads=[f"g{jp}", "sils"], writes=["gg"])
                OP("vector", lambda v, ot=ot: v.scalar_tensor_tensor(out=ot[:],
                   in0=ot[:], scalar=srms[:], in1=gg[:],
                   op0=ALU.mult, op1=ALU.mult),
                   reads=[otk, "srms", "gg"], writes=[otk])

            def emit_og(jp):
                ot = o_tm[jp % 2]
                otk = f"otm{jp % 2}"

                def og_tr(t, ot=ot):
                    last = None
                    for dc in range(4):
                        last = t.transpose(pb[2][:, dc * 128:dc * 128 + 128],
                                           ot[:, dc * 128:(dc + 1) * 128], ident[:])
                    return last
                OP("tensor", og_tr, reads=[otk, "ident"], writes=[PB[2]])
                OP("scalar", lambda sc, jp=jp: sc.activation(
                   out=ogh_h[jp][:], in_=pb[2][:, 0:512],
                   func=AF.Copy, bias=0.0, scale=1.0),
                   reads=[PB[2]], writes=[f"ogh{jp}"])
                OP("vector", lambda v, jp=jp: v.scalar_tensor_tensor(
                   out=ogh_l[jp][:], in0=pb[2][:, 0:512], scalar=1.0,
                   in1=ogh_h[jp][:], op0=ALU.mult, op1=ALU.subtract),
                   reads=[PB[2], f"ogh{jp}"], writes=[f"og{jp}"])

            for j in range(JC):
                x = j % 2
                if j + 1 < JC:
                    emit_gates(j + 1)
                if j > 0:
                    emit_otm_evac(j - 1)

                def kh_tr(t, x=x):
                    last = None
                    for c2 in range(2):
                        last = t.transpose(pb[4][:, c2 * 128:c2 * 128 + 128],
                                           khf[x][:, c2 * C:(c2 + 1) * C], ident[:])
                    return last
                OP("tensor", kh_tr, reads=[f"khf0_{x}", f"khf1_{x}", "ident"],
                   writes=[PB[4]])
                OP("scalar", lambda sc: sc.activation(out=khtm[:, 0:256],
                   in_=pb[4][:, 0:256], func=AF.Copy, bias=0.0, scale=1.0),
                   reads=[PB[4]], writes=["khtm0", "khtm1"])

                def p_mms(t, x=x):
                    last = None
                    for c2 in range(2):
                        last = t.matmul(pb[7][:, 0:128],
                                        kt[x][:, c2 * C:(c2 + 1) * C],
                                        qt[x][:, c2 * C:(c2 + 1) * C],
                                        start=(c2 == 0), stop=(c2 == 1))
                    return last
                OP("tensor", p_mms,
                   reads=[f"qt0_{x}", f"qt1_{x}", f"kt0_{x}", f"kt1_{x}"],
                   writes=[PB[7]])
                OP("vector", lambda v: v.tensor_tensor(out=p_sm[:],
                   in0=pb[7][:, 0:128], in1=triu[:], op=ALU.mult),
                   reads=[PB[7], "triu"], writes=["psm"])

                def o_mms(t, j=j, h=h, x=x):
                    last = None
                    for half in range(2):
                        sl = slice(half * 256, half * 256 + 256)
                        last = t.matmul(pb[5 + half][:, 0:256], p_sm[:],
                                        vth[j][:, sl], start=True, stop=False)
                    for c2 in range(2):
                        for half in range(2):
                            sl = slice(half * 256, half * 256 + 256)
                            last = t.matmul(pb[5 + half][:, 0:256],
                                            qt[x][:, c2 * C:(c2 + 1) * C],
                                            Sst[h][c2][:, sl],
                                            start=False, stop=(c2 == 1))
                    return last
                OP("tensor", o_mms,
                   reads=["psm", f"v{j}", f"qt0_{x}", f"qt1_{x}",
                          f"S{h}0", f"S{h}1"],
                   writes=[PB[5], PB[6]])

                for c2 in range(2):
                    def s_mms(t, c2=c2, j=j):
                        last = None
                        for half in range(2):
                            sl = slice(half * 256, half * 256 + 256)
                            last = t.matmul(pb[7][:, sl],
                                            khtm[:, c2 * C:(c2 + 1) * C],
                                            vth[j][:, sl], start=True, stop=True)
                        return last
                    OP("tensor", s_mms, reads=[f"khtm{c2}", f"v{j}"],
                       writes=[PB[7]])
                    OP("vector", lambda v, h=h, c2=c2, x=x: v.scalar_tensor_tensor(
                       out=Sst[h][c2][:], in0=Sst[h][c2][:],
                       scalar=alast[x][:, c2:c2 + 1], in1=pb[7][:, 0:HDV],
                       op0=ALU.mult, op1=ALU.add),
                       reads=[f"S{h}{c2}", f"alast{c2}_{x}", PB[7]],
                       writes=[f"S{h}{c2}"])

                if j > 0:
                    emit_epilogue(j - 1)
                if j > 1:
                    emit_og(j - 2)

            emit_otm_evac(JC - 1)
            emit_epilogue(JC - 1)
            emit_og(JC - 2)
            emit_og(JC - 1)

            for j in range(JC):
                def wo_mms(t, j=j):
                    last = None
                    first = [True, True]
                    for dc in range(4):
                        oh = ogh_h[j][:, dc * 128:(dc + 1) * 128]
                        ol = ogh_l[j][:, dc * 128:(dc + 1) * 128]
                        for dpart in range(2):
                            bk = 2 + dpart
                            dsl = slice(dc * 512, (dc + 1) * 512)
                            last = t.matmul(pb[bk][:, 0:512], oh,
                                            woh_cs[dpart][:, dsl],
                                            start=first[dpart], stop=False)
                            first[dpart] = False
                            last = t.matmul(pb[bk][:, 0:512], oh,
                                            wol_cs[dpart][:, dsl],
                                            start=False, stop=False)
                            last = t.matmul(pb[bk][:, 0:512], ol,
                                            woh_cs[dpart][:, dsl],
                                            start=False, stop=(dc == 3))
                    return last
                OP("tensor", wo_mms,
                   reads=["wohcs0", "wolcs0", "wohcs1", "wolcs1",
                          f"ogh{j}", f"og{j}"],
                   writes=[PB[2], PB[3]])
                for dpart in range(2):
                    sl0 = dpart * 512
                    bank = 2 + dpart
                    if h == 0:
                        OP("vector", lambda v, j=j, sl0=sl0, bank=bank:
                           v.scalar_tensor_tensor(
                           out=attn_c[j][:, sl0:sl0 + 512],
                           in0=pb[bank][:, 0:512], scalar=1.0,
                           in1=b2bc[:, sl0:sl0 + 512],
                           op0=ALU.mult, op1=ALU.add),
                           reads=[PB[bank], "b2bc"], writes=[f"attn{j}"])
                    else:
                        OP("vector", lambda v, j=j, sl0=sl0, bank=bank:
                           v.scalar_tensor_tensor(
                           out=attn_c[j][:, sl0:sl0 + 512],
                           in0=pb[bank][:, 0:512], scalar=1.0,
                           in1=attn_c[j][:, sl0:sl0 + 512],
                           op0=ALU.mult, op1=ALU.add),
                           reads=[PB[bank], f"attn{j}"], writes=[f"attn{j}"])
            if h + 1 < n_heads:
                load_wo(h + 1)
            elif tile + 1 < n_tiles:
                load_wo(0)

        for grp in range(8):
            jp0 = grp * 2
            for jpi in range(2):
                jp = jp0 + jpi
                wb = jpi
                for b in range(2):
                    jw = 2 * jp + b
                    slot = jw % 4
                    bank = slot % 2

                    def f1_all(t, b=b, wb=wb, bank=bank):
                        last = None
                        first = True
                        for k in range(8):
                            co = k * 256 + b * 128
                            for (wgt, mov) in ((wvh_cs[wb], xnf_h),
                                               (wvh_cs[wb], xnf_l),
                                               (wvl_cs[wb], xnf_h)):
                                lo_pass = wgt is wvl_cs[wb]
                                w_ap = (wvl_cs[wb] if lo_pass else wvh_cs[wb])[:, co:co + 128]
                                last = t.matmul(pb[bank][:, 0:tile_tok], w_ap,
                                                mov[k][:, 0:tile_tok],
                                                start=first,
                                                stop=(k == 7 and lo_pass))
                                first = False
                        return last
                    OP("tensor", f1_all,
                       reads=[f"wvhcs{wb}", f"wvlcs{wb}"] + xnf_tile,
                       writes=[PB[bank]])
                    OP("vector", lambda v, jw=jw, bank=bank: v.tensor_scalar(
                       out=gg[:, 0:tile_tok], in0=pb[bank][:, 0:tile_tok],
                       scalar1=b1f[:, jw:jw + 1],
                       scalar2=0.0, op0=ALU.add, op1=ALU.max),
                       reads=[PB[bank], "b1f"], writes=["gg"])
                    OP("scalar", lambda sc, slot=slot: sc.activation(
                       out=h1h[:, slot * tile_tok:(slot + 1) * tile_tok],
                       in_=gg[:, 0:tile_tok],
                       func=AF.Copy, bias=0.0, scale=1.0),
                       reads=["gg"], writes=[f"h1h_{slot}"])
                    OP("vector", lambda v, slot=slot: v.scalar_tensor_tensor(
                       out=h1l[:, slot * tile_tok:(slot + 1) * tile_tok],
                       in0=gg[:, 0:tile_tok], scalar=1.0,
                       in1=h1h[:, slot * tile_tok:(slot + 1) * tile_tok],
                       op0=ALU.mult, op1=ALU.subtract),
                       reads=["gg", f"h1h_{slot}"], writes=[f"h1_{slot}"])
            for jpi in range(2):
                jp = jp0 + jpi
                if jp + 2 < 16:
                    load_w12("w1", jp + 2)
                elif tile + 1 < n_tiles:
                    load_vg("wv", 0, jp % 2)
            for slot in range(4):
                def f2_slot(t, slot=slot):
                    last = None
                    jpi, b = slot // 2, slot % 2
                    for j in range(JC):
                        hh = h1h[:, slot * tile_tok + j * C:slot * tile_tok + (j + 1) * C]
                        hl = h1l[:, slot * tile_tok + j * C:slot * tile_tok + (j + 1) * C]
                        for dh in range(2):
                            bk = 2 * j + dh
                            wsl = slice(b * 1024 + dh * 512, b * 1024 + dh * 512 + 512)
                            last = t.matmul(pb[bk][:, 0:512], hh,
                                            wgh_cs[jpi][:, wsl],
                                            start=(slot == 0), stop=False)
                            last = t.matmul(pb[bk][:, 0:512], hh,
                                            wgl_cs[jpi][:, wsl],
                                            start=False, stop=False)
                            last = t.matmul(pb[bk][:, 0:512], hl,
                                            wgh_cs[jpi][:, wsl],
                                            start=False, stop=(slot == 3))
                    return last
                OP("tensor", f2_slot,
                   reads=[f"wghcs{slot // 2}", f"wglcs{slot // 2}",
                          f"h1h_{slot}", f"h1_{slot}"],
                   writes=PB[0:8])
            for jpi in range(2):
                jp = jp0 + 2 + jpi
                if jp < 16:
                    load_w12("w2", jp)
                elif tile + 1 < n_tiles:
                    load_vg("wg", 0, jp % 2)
            if grp < 7:
                for j in range(JC):
                    for dh in range(2):
                        OP("vector", lambda v, j=j, dh=dh: v.scalar_tensor_tensor(
                           out=attn_c[j][:, dh * 512:dh * 512 + 512],
                           in0=pb[2 * j + dh][:, 0:512], scalar=1.0,
                           in1=attn_c[j][:, dh * 512:dh * 512 + 512],
                           op0=ALU.mult, op1=ALU.add),
                           reads=[PB[2 * j + dh], f"attn{j}"], writes=[f"attn{j}"])
                if grp == 6:
                    for j in range(JC):
                        ob = j % 2
                        r0 = t0 + j * C
                        OP("sync", lambda s, ob=ob, r0=r0: s.dma_start(
                           out=x_c[ob][:], in_=x_hbm[r0:r0 + C, :]),
                           writes=[f"x{ob}"], dma=True, stream=f"x{ob}")
                        OP("vector", lambda v, j=j, ob=ob: v.tensor_tensor(
                           out=attn_c[j][:], in0=attn_c[j][:], in1=x_c[ob][:],
                           op=ALU.add),
                           reads=[f"attn{j}", f"x{ob}"], writes=[f"attn{j}"])
            else:
                for j in range(JC):
                    r0 = t0 + j * C
                    for dh in range(2):
                        OP("vector", lambda v, j=j, dh=dh:
                           v.scalar_tensor_tensor(
                           out=attn_c[j][:, dh * 512:dh * 512 + 512],
                           in0=pb[2 * j + dh][:, 0:512], scalar=1.0,
                           in1=attn_c[j][:, dh * 512:dh * 512 + 512],
                           op0=ALU.mult, op1=ALU.add),
                           reads=[PB[2 * j + dh], f"attn{j}"], writes=[f"attn{j}"])
                    OP("sync", lambda s, j=j, r0=r0: s.dma_start(
                       out=out_hbm[r0:r0 + C, :], in_=attn_c[j][:]),
                       reads=[f"attn{j}"], dma=True, stream="out")
    p.wait_all_streams("sync")
    p.build()
    return nc, p


_CACHE = {}


def _get_program(n_tok):
    if n_tok not in _CACHE:
        _CACHE[n_tok] = build_program(n_tok)
    return _CACHE[n_tok]


def _layout_weights(Wq, Wk, Wv, Wg, Wgk1, Wgk2, bgk2, Wo, gnw, w1, b1, w2, b2):
    c = np.ascontiguousarray
    return {
        "wq_l": c(Wq.reshape(8, 128, 8, 128).transpose(2, 1, 0, 3).reshape(8, 128, 1024)),
        "wk_l": c(Wk.reshape(8, 128, 8, 128).transpose(2, 1, 0, 3).reshape(8, 128, 1024)),
        "wv_l": c(Wv.reshape(8, 128, 8, 256).transpose(2, 1, 0, 3).reshape(8, 128, 2048)),
        "wg_l": c(Wg.reshape(8, 128, 8, 256).transpose(2, 1, 0, 3).reshape(8, 128, 2048)),
        "wo_l": c(Wo.reshape(4, 4, 128, 2, 512).transpose(0, 3, 2, 1, 4).reshape(8, 128, 2048)),
        "w1_l": c(w1.reshape(8, 128, 16, 256).transpose(2, 1, 0, 3).reshape(16, 128, 2048)),
        "w2_l": c(w2.reshape(16, 2, 128, 1024).transpose(0, 2, 1, 3).reshape(16, 128, 2048)),
        "wgk1_l": c(Wgk1.reshape(8, 128, R).transpose(1, 0, 2).reshape(128, 8 * R)),
        "wgk2": c(Wgk2),
        "bgk2_l": c(bgk2.reshape(8, 128).T),
        "b1_l": c(b1.reshape(32, 128).T),
        "b2_l": c(b2.reshape(1, D)),
        "gnw_l": c(gnw.reshape(4, 128).T),
    }


def kernel(L0_attn_x, L0_attn_Wq, L0_attn_Wk, L0_attn_Wv, L0_attn_Wg,
           L0_attn_Wgk1, L0_attn_Wgk2, L0_attn_bgk2, L0_attn_Wo,
           L0_attn_g_norm_weight, L0_ffn_w1, L0_ffn_b1, L0_ffn_w2, L0_ffn_b2):
    x = np.ascontiguousarray(np.asarray(L0_attn_x, dtype=np.float32))
    B, N, _ = x.shape
    wmap = _layout_weights(
        np.asarray(L0_attn_Wq, np.float32), np.asarray(L0_attn_Wk, np.float32),
        np.asarray(L0_attn_Wv, np.float32), np.asarray(L0_attn_Wg, np.float32),
        np.asarray(L0_attn_Wgk1, np.float32), np.asarray(L0_attn_Wgk2, np.float32),
        np.asarray(L0_attn_bgk2, np.float32), np.asarray(L0_attn_Wo, np.float32),
        np.asarray(L0_attn_g_norm_weight, np.float32),
        np.asarray(L0_ffn_w1, np.float32), np.asarray(L0_ffn_b1, np.float32),
        np.asarray(L0_ffn_w2, np.float32), np.asarray(L0_ffn_b2, np.float32))
    nc, _ = _get_program(N)
    in_maps = [dict(wmap, x=x[b]) for b in range(B)]
    res = run_bass_kernel_spmd(nc, in_maps, core_ids=list(range(B)))
    out = np.stack([res.results[b]["out"] for b in range(B)], axis=0)
    return out


# version 39
# speedup vs baseline: 1.2092x; 1.0128x over previous
import numpy as np
import concourse.bass as bass
import concourse.mybir as mybir
import concourse.bass_utils as _bu
from concourse.bass_utils import run_bass_kernel_spmd

if not getattr(_bu, "_ldwopt_patched", False):
    _orig_run_command = _bu.run_command

    def _run_command_ldwopt(argv, **kwargs):
        argv = ["--enable-ldw-opt=true" if a == "--enable-ldw-opt=false" else a
                for a in argv]
        return _orig_run_command(argv, **kwargs)

    _bu.run_command = _run_command_ldwopt
    _bu._ldwopt_patched = True

F32 = mybir.dt.float32
BF16 = mybir.dt.bfloat16
AF = mybir.ActivationFunctionType
ALU = mybir.AluOpType
AXX = mybir.AxisListType.X

D = 1024
HEADS = 4
HDK = 256
HDV = 512
R = 16
C = 128
EPS = 1e-5
LN16 = float(np.log(np.float32(16.0)))


class Prog:
    ENGINES = ("tensor", "vector", "scalar", "gpsimd", "sync")

    def __init__(self, nc):
        self.nc = nc
        self.sem = {e: nc.alloc_semaphore(f"S{e}") for e in self.ENGINES}
        self.stream_sems = {}
        self.cnt = {id(s): 0 for s in self.sem.values()}
        self.sem_by_id = {id(s): s for s in self.sem.values()}
        self.ops = {e: [] for e in self.ENGINES}
        self.res = {}
        self.waited = {e: {} for e in self.ENGINES}
        self.n_ops = 0
        self.n_waits = 0

    def _stream_sem(self, stream):
        s = self.stream_sems.get(stream)
        if s is None:
            s = self.nc.alloc_semaphore(f"D_{stream}")
            self.stream_sems[stream] = s
            self.cnt[id(s)] = 0
            self.sem_by_id[id(s)] = s
        return s

    def _deps(self, engine, reads, writes):
        deps = {}

        def add(tag):
            if tag is None:
                return
            sid, val, is_dma = tag
            if is_dma:
                val = self.cnt[sid]
            if deps.get(sid, 0) < val:
                deps[sid] = val

        for r in reads:
            e = self.res.get(r)
            if e:
                add(e["w"])
        for w in writes:
            e = self.res.get(w)
            if e:
                add(e["w"])
                for t in e["r"]:
                    add(t)
        waits = []
        wt = self.waited[engine]
        for sid, val in deps.items():
            if wt.get(sid, 0) < val:
                waits.append((sid, val))
                wt[sid] = val
        return waits

    def op(self, engine, fn, reads=(), writes=(), dma=False, stream=None):
        if engine == "scalar":
            reads = list(reads) + ["cbias"]
        waits = self._deps(engine, reads, writes)
        self.n_waits += len(waits)
        if dma:
            s = self._stream_sem(stream or ("dma_" + engine))
            self.cnt[id(s)] += 16
            tag = (id(s), self.cnt[id(s)], True)
            inc_sem, inc_by = s, 16
        else:
            s = self.sem[engine]
            self.cnt[id(s)] += 1
            tag = (id(s), self.cnt[id(s)], False)
            inc_sem, inc_by = s, 1
        for w in writes:
            self.res[w] = {"w": tag, "r": []}
        for r in reads:
            e = self.res.setdefault(r, {"w": None, "r": []})
            e["r"].append(tag)
        self.ops[engine].append((waits, fn, inc_sem, inc_by))
        self.n_ops += 1
        return tag

    def wait_all_streams(self, engine, streams=None):
        names = list(self.stream_sems) if streams is None else list(streams)
        targets = [(id(self.stream_sems[n]), self.cnt[id(self.stream_sems[n])])
                   for n in names if n in self.stream_sems]

        def fn(eng):
            for sid, val in targets:
                if val > 0:
                    eng.wait_ge(self.sem_by_id[sid], val)
            return eng.nop()

        self.ops[engine].append(([], fn, self.sem[engine], 1))
        self.cnt[id(self.sem[engine])] += 1
        self.n_ops += 1

    def build(self):
        nc = self.nc
        with nc.Block() as block:
            for engine in self.ENGINES:
                ops = self.ops[engine]
                if not ops:
                    continue
                dec = getattr(block, engine)

                def make(ops):
                    def body(eng):
                        for waits, fn, inc_sem, inc_by in ops:
                            for sid, val in waits:
                                eng.wait_ge(self.sem_by_id[sid], val)
                            inst = fn(eng)
                            if inst is not None:
                                inst.then_inc(inc_sem, inc_by)
                    return body

                dec(make(list(ops)))
        return nc


def build_program(n_tok=2048, tile_tok=512, n_heads=HEADS):
    assert tile_tok % C == 0 and n_tok % tile_tok == 0
    n_tiles = n_tok // tile_tok
    JC = tile_tok // C
    NH2 = tile_tok // 256

    nc = bass.Bass(trn_type="TRN2", target_bir_lowering=False, debug=False)
    din = lambda name, shp: nc.dram_tensor(name, shp, F32, kind="ExternalInput").ap()
    x_hbm = din("x", [n_tok, D])
    wq_l = din("wq_l", [8, 128, 1024])
    wk_l = din("wk_l", [8, 128, 1024])
    wv_l = din("wv_l", [8, 128, 2048])
    wg_l = din("wg_l", [8, 128, 2048])
    wo_l = din("wo_l", [8, 128, 2048])
    w1_l = din("w1_l", [16, 128, 2048])
    w2_l = din("w2_l", [16, 128, 2048])
    wgk1_l = din("wgk1_l", [128, 8 * R])
    dscr = lambda name, shp: nc.dram_tensor(name, shp, BF16, kind="Internal").ap()
    wq_hi = dscr("wq_hi", [8, 128, 1024]); wq_lo = dscr("wq_lo", [8, 128, 1024])
    wk_hi = dscr("wk_hi", [8, 128, 1024]); wk_lo = dscr("wk_lo", [8, 128, 1024])
    wv_hi = dscr("wv_hi", [8, 128, 2048]); wv_lo = dscr("wv_lo", [8, 128, 2048])
    wg_hi = dscr("wg_hi", [8, 128, 2048]); wg_lo = dscr("wg_lo", [8, 128, 2048])
    wo_hi = dscr("wo_hi", [8, 128, 2048]); wo_lo = dscr("wo_lo", [8, 128, 2048])
    w1_hi = dscr("w1_hi", [16, 128, 2048]); w1_lo = dscr("w1_lo", [16, 128, 2048])
    w2_hi = dscr("w2_hi", [16, 128, 2048]); w2_lo = dscr("w2_lo", [16, 128, 2048])
    wgk2_h = din("wgk2", [R, 1024])
    bgk2_l = din("bgk2_l", [128, 8])
    b1_l = din("b1_l", [128, 32])
    b2_l = din("b2_l", [1, D])
    gnw_l = din("gnw_l", [128, 4])
    out_hbm = nc.dram_tensor("out", [n_tok, D], F32, kind="ExternalOutput").ap()

    S_ = lambda name, shp, d=F32: nc.alloc_sbuf_tensor(name, shp, d)

    ident = S_("c_ident", [128, 128])
    triu = S_("c_triu", [128, 128])
    ones128 = S_("c_ones", [128, 128])
    onesrow = S_("c_onesrow", [1, 128])
    wgk2 = S_("r_wgk2", [16, 1024])
    bgk2n = S_("r_bgk2n", [128, 8])
    b1f = S_("r_b1f", [128, 32])
    b2row = S_("r_b2row", [1, D])
    b2bc = S_("r_b2bc", [128, D])
    gnwf = S_("r_gnwf", [128, 4])
    Sst = [[S_(f"st_S_{h}_{c2}", [128, HDV]) for c2 in range(2)] for h in range(n_heads)]

    x_c = [S_(f"t_x{i}", [128, D]) for i in range(2)]
    xnf_h = [S_(f"t_xnfh{k}", [128, tile_tok], BF16) for k in range(8)]
    xnf_l = [S_(f"t_xnfl{k}", [128, tile_tok], BF16) for k in range(8)]
    attn_c = [S_(f"t_attn{j}", [128, D]) for j in range(JC)]
    qf = S_("h_qf", [128, 2 * tile_tok])
    kf = S_("h_kf", [128, 2 * tile_tok])
    spf = S_("h_spf", [128, 2 * tile_tok])
    vth = [S_(f"h_v{j}", [128, HDV]) for j in range(JC)]
    gth = [S_(f"h_g{j}", [128, HDV]) for j in range(JC)]
    ogh_h = [S_(f"h_ogh{j}", [128, HDV], BF16) for j in range(JC)]
    ogh_l = [S_(f"h_ogl{j}", [128, HDV], BF16) for j in range(JC)]
    u_sb = S_("h_u", [16, tile_tok])
    at = [S_(f"k_at{i}", [128, 2 * C]) for i in range(2)]
    qt = [S_(f"k_qt{i}", [128, 2 * C]) for i in range(2)]
    kt = [S_(f"k_kt{i}", [128, 2 * C]) for i in range(2)]
    khf = [S_(f"k_khf{i}", [128, 2 * C]) for i in range(2)]
    khtm = S_("k_khtm", [128, 2 * C])
    alast = [S_(f"k_alast{i}", [128, 2]) for i in range(2)]
    p_sm = S_("k_p", [128, C])
    o_tm = [S_(f"k_otm{i}", [128, HDV]) for i in range(2)]
    sil_s = S_("k_sils", [128, HDV])
    gg = S_("k_gg", [128, HDV])
    bscr = S_("k_bscr", [128, D], BF16)
    xfull = S_("a_xfull", [128, 512])
    negml = S_("a_negml", [128, 2])
    ln_sx = [S_(f"k_sx{i}", [128, 1]) for i in range(2)]
    ln_sxx = [S_(f"k_sxx{i}", [128, 1]) for i in range(2)]
    ln_m = [S_(f"k_m{i}", [128, 1]) for i in range(2)]
    ln_v = [S_(f"k_v{i}", [128, 1]) for i in range(2)]
    ln_inv = [S_(f"k_inv{i}", [128, 1]) for i in range(2)]
    xn_tm = S_("k_xntm", [128, D])
    ssum = S_("k_ssum", [128, 1])
    srms = S_("k_srms", [128, 1])
    wqh_cs = [S_(f"w_wqh{i}", [128, 1024], BF16) for i in range(2)]
    wql_cs = [S_(f"w_wql{i}", [128, 1024], BF16) for i in range(2)]
    wkh_cs = [S_(f"w_wkh{i}", [128, 1024], BF16) for i in range(2)]
    wkl_cs = [S_(f"w_wkl{i}", [128, 1024], BF16) for i in range(2)]
    wvh_cs = [S_(f"w_wvh{i}", [128, 2048], BF16) for i in range(2)]
    wvl_cs = [S_(f"w_wvl{i}", [128, 2048], BF16) for i in range(2)]
    wgh_cs = [S_(f"w_wgh{i}", [128, 2048], BF16) for i in range(2)]
    wgl_cs = [S_(f"w_wgl{i}", [128, 2048], BF16) for i in range(2)]
    woh_cs = [S_(f"w_woh{i}", [128, 2048], BF16) for i in range(2)]
    wol_cs = [S_(f"w_wol{i}", [128, 2048], BF16) for i in range(2)]
    wgk1h = S_("r_wgk1h", [128, 8 * R], BF16)
    wgk1l = S_("r_wgk1l", [128, 8 * R], BF16)
    h1h = S_("f_h1h", [128, 4 * tile_tok], BF16)
    h1l = S_("f_h1l", [128, 4 * tile_tok], BF16)

    pb = [nc.alloc_psum_tensor(f"pb{i}", [128, 512], F32) for i in range(8)]
    PB = [f"pb{i}" for i in range(8)]

    p = Prog(nc)
    OP = p.op

    cbias_vals = [0.0, 1.0, EPS, -LN16]
    cbias = S_("c_bias2", [128, len(cbias_vals)])
    for i, v in enumerate(cbias_vals):
        OP("gpsimd", lambda g, i=i, v=v: g.memset(cbias[:, i:i + 1], float(v)),
           writes=["cbias"])
        nc.const_aps.register(F32, float(v), cbias[:, i:i + 1])

    OP("gpsimd", lambda g: g.memset(ident[:], 1.0), writes=["ident"])
    OP("gpsimd", lambda g: g.affine_select(out=ident[:], in_=ident[:],
        compare_op=ALU.is_equal, fill=0.0, base=0, pattern=[[1, 128]],
        channel_multiplier=-1), reads=["ident"], writes=["ident"])
    OP("gpsimd", lambda g: g.memset(triu[:], 1.0), writes=["triu"])
    OP("gpsimd", lambda g: g.affine_select(out=triu[:], in_=triu[:],
        compare_op=ALU.is_ge, fill=0.0, base=0, pattern=[[1, 128]],
        channel_multiplier=-1), reads=["triu"], writes=["triu"])
    OP("gpsimd", lambda g: g.memset(ones128[:], 1.0), writes=["ones128"])
    OP("gpsimd", lambda g: g.memset(onesrow[:], 1.0), writes=["onesrow"])
    for h in range(n_heads):
        for c2 in range(2):
            OP("gpsimd", lambda g, h=h, c2=c2: g.memset(Sst[h][c2][:], 0.0),
               writes=[f"S{h}{c2}"])
    OP("sync", lambda s: s.dma_start(out=wgk2[:], in_=wgk2_h),
       writes=["wgk2"], dma=True, stream="cst_wgk2")
    OP("sync", lambda s: s.dma_start(out=bgk2n[:], in_=bgk2_l),
       writes=["bgk2n"], dma=True, stream="cst_bgk2n")
    OP("vector", lambda v: v.tensor_scalar(out=bgk2n[:], in0=bgk2n[:],
       scalar1=-1.0, scalar2=None, op0=ALU.mult), reads=["bgk2n"], writes=["bgk2n"])
    OP("sync", lambda s: s.dma_start(out=b1f[:], in_=b1_l),
       writes=["b1f"], dma=True, stream="cst_b1f")
    OP("sync", lambda s: s.dma_start(out=b2row[:], in_=b2_l),
       writes=["b2row"], dma=True, stream="cst_b2row")
    OP("sync", lambda s: s.dma_start(out=gnwf[:], in_=gnw_l),
       writes=["gnwf"], dma=True, stream="cst_gnwf")

    def bcast_row(dst, dkey, srcrow, skey, width, bank):
        for g0 in range(0, width, 512):
            w = min(512, width - g0)

            def mmg(t, g0=g0, w=w):
                last = None
                for c0 in range(0, w, 256):
                    last = t.matmul(pb[bank][:, c0:c0 + 256], onesrow[0:1, :],
                                    srcrow[0:1, g0 + c0:g0 + c0 + 256],
                                    start=True, stop=True)
                return last
            OP("tensor", mmg, reads=[skey, "onesrow"], writes=[PB[bank]])
            OP("vector", lambda v, g0=g0, w=w: v.tensor_copy(
               dst[:, g0:g0 + w], pb[bank][:, 0:w]),
               reads=[PB[bank]], writes=[dkey])

    bcast_row(b2bc, "b2bc", b2row, "b2row", D, 6)

    split_jobs = []
    PWS = 512
    def add_split(matname, mat_fp32, hi, lo, nslab, W):
        for s in range(nslab):
            for c0 in range(0, W, PWS):
                dc = (c0 // 512) if matname == "wo" else None
                split_jobs.append((matname, mat_fp32[s, :, c0:c0 + PWS],
                                   hi[s, :, c0:c0 + PWS], lo[s, :, c0:c0 + PWS],
                                   dc))
    add_split("wq", wq_l, wq_hi, wq_lo, 8, 1024)
    add_split("wk", wk_l, wk_hi, wk_lo, 8, 1024)
    add_split("wv", wv_l, wv_hi, wv_lo, 8, 2048)
    add_split("wg", wg_l, wg_hi, wg_lo, 8, 2048)
    add_split("wo", wo_l, wo_hi, wo_lo, 8, 2048)
    add_split("w1", w1_l, w1_hi, w1_lo, 16, 2048)
    add_split("w2", w2_l, w2_hi, w2_lo, 16, 2048)
    _mat_order = ["wq", "wk", "wv", "wg", "wo", "w1", "w2"]
    SPLIT_PIECE_KEYS = [f"splh{i}" for i in range(12)] + [f"spll{i}" for i in range(12)]
    SCRK = lambda mn: ([f"scr_{mn}_h{s}" for s in range(12)]
                       + [f"scr_{mn}_l{s}" for s in range(12)])
    split_ctr = [0]

    def emit_split_until(matname):
        n = 0
        for i in range(split_ctr[0], len(split_jobs)):
            if split_jobs[i][0] == matname or _mat_order.index(split_jobs[i][0]) < _mat_order.index(matname):
                n += 1
            else:
                break
        emit_split(n)

    def emit_split(n):
        i0 = split_ctr[0]
        n_land = 8
        def landing(li):
            if li < 4:
                return vth[li][:], f"v{li}"
            return gth[li - 4][:], f"g{li - 4}"
        hi_all = [wvh_cs[0], wvh_cs[1], wgh_cs[0]]
        lo_all = [wvl_cs[0], wvl_cs[1], wgl_cs[0]]
        for i in range(i0, min(i0 + n, len(split_jobs))):
            mn, srcp, hip, lop, wodc = split_jobs[i]
            lb, lbk = landing(i % n_land)
            sslot = i % 12
            hcol = hi_all[sslot // 4][:, (sslot % 4) * 512:(sslot % 4) * 512 + 512]
            lcol = lo_all[sslot // 4][:, (sslot % 4) * 512:(sslot % 4) * 512 + 512]
            hkey = f"splh{sslot}"; lkey = f"spll{sslot}"
            lastp = (i == len(split_jobs) - 1)
            OP("sync", lambda s, srcp=srcp, lb=lb: s.dma_start(out=lb, in_=srcp),
               writes=[lbk], dma=True, stream=f"spl{i % n_land}")
            scl = 1.0 if wodc is None else gnwf[:, wodc:wodc + 1]
            srd = [] if wodc is None else ["gnwf"]
            OP("scalar", lambda sc, lb=lb, hcol=hcol, scl=scl: sc.activation(
               out=hcol, in_=lb, func=AF.Copy, bias=0.0, scale=scl),
               reads=[lbk] + srd, writes=[hkey])
            OP("vector", lambda v, lb=lb, hcol=hcol, lcol=lcol, scl=scl:
               v.scalar_tensor_tensor(out=lcol, in0=lb, scalar=scl, in1=hcol,
               op0=ALU.mult, op1=ALU.subtract),
               reads=[lbk, hkey] + srd, writes=[lkey])
            hwr = [f"scr_{mn}_h{sslot}"]
            lwr = [f"scr_{mn}_l{sslot}"]
            OP("scalar", lambda sc, hip=hip, hcol=hcol: sc.dma_start(
               out=hip, in_=hcol),
               reads=[hkey], writes=hwr, dma=True, stream=f"sch{sslot}")
            OP("gpsimd", lambda g, lop=lop, lcol=lcol: g.dma_start(
               out=lop, in_=lcol),
               reads=[lkey], writes=lwr, dma=True, stream=f"scl{sslot}")
        split_ctr[0] = min(i0 + n, len(split_jobs))

    def emit_wgk1_split():
        OP("sync", lambda s: s.dma_start(out=vth[0][:, 0:128], in_=wgk1_l),
           writes=["v0"], dma=True, stream="splgk")
        OP("scalar", lambda sc: sc.activation(out=wgk1h[:], in_=vth[0][:, 0:128],
           func=AF.Copy, bias=0.0, scale=1.0), reads=["v0"], writes=["wgk1h"])
        OP("vector", lambda v: v.scalar_tensor_tensor(out=wgk1l[:],
           in0=vth[0][:, 0:128], scalar=1.0, in1=wgk1h[:],
           op0=ALU.mult, op1=ALU.subtract), reads=["v0", "wgk1h"],
           writes=["wgk1l"])

    def load_qk(mat, h, c2):
        bh, bl, hi, lo, key = {
            "wq": (wqh_cs, wql_cs, wq_hi, wq_lo, "wq"),
            "wk": (wkh_cs, wkl_cs, wk_hi, wk_lo, "wk")}[mat]
        s = 2 * h + c2
        OP("sync", lambda ss, s=s: ss.dma_start(out=bh[c2][:], in_=hi[s]),
           reads=SCRK(key), writes=[f"{key}hcs{c2}"], dma=True, stream=f"{key}{c2}")
        OP("sync", lambda ss, s=s: ss.dma_start(out=bl[c2][:], in_=lo[s]),
           reads=SCRK(key), writes=[f"{key}lcs{c2}"], dma=True, stream=f"{key}{c2}")

    def load_vg(mat, h, half, ext=()):
        bh, bl, hi, lo, key = {
            "wv": (wvh_cs, wvl_cs, wv_hi, wv_lo, "wv"),
            "wg": (wgh_cs, wgl_cs, wg_hi, wg_lo, "wg")}[mat]
        s = 2 * h + half
        OP("gpsimd", lambda g, s=s: g.dma_start(out=bh[half][:], in_=hi[s]),
           reads=SCRK(key), writes=[f"{key}hcs{half}"] + list(ext), dma=True,
           stream=f"{key}{half}")
        OP("gpsimd", lambda g, s=s: g.dma_start(out=bl[half][:], in_=lo[s]),
           reads=SCRK(key), writes=[f"{key}lcs{half}"] + list(ext), dma=True,
           stream=f"{key}{half}")

    def load_w12(mat, jp):
        bh, bl, hi, lo, key, strm = {
            "w1": (wvh_cs, wvl_cs, w1_hi, w1_lo, "wv", "wv"),
            "w2": (wgh_cs, wgl_cs, w2_hi, w2_lo, "wg", "wg")}[mat]
        b = jp % 2
        OP("gpsimd", lambda g, jp=jp, b=b: g.dma_start(out=bh[b][:], in_=hi[jp]),
           reads=SCRK(mat), writes=[f"{key}hcs{b}"], dma=True, stream=f"{strm}{b}")
        OP("gpsimd", lambda g, jp=jp, b=b: g.dma_start(out=bl[b][:], in_=lo[jp]),
           reads=SCRK(mat), writes=[f"{key}lcs{b}"], dma=True, stream=f"{strm}{b}")

    def load_wo(h):
        for dpart in range(2):
            OP("sync", lambda s, h=h, dpart=dpart: s.dma_start(
               out=woh_cs[dpart][:], in_=wo_hi[2 * h + dpart]),
               reads=SCRK("wo"), writes=[f"wohcs{dpart}"], dma=True,
               stream=f"wo{dpart}")
            OP("sync", lambda s, h=h, dpart=dpart: s.dma_start(
               out=wol_cs[dpart][:], in_=wo_lo[2 * h + dpart]),
               reads=SCRK("wo"), writes=[f"wolcs{dpart}"], dma=True,
               stream=f"wo{dpart}")

    def emit_A0(a0tile):
        a0t0 = a0tile * tile_tok
        for j in range(JC):
            r0 = a0t0 + j * C
            xb = j % 2
            OP("sync", lambda s, xb=xb, r0=r0: s.dma_start(
               out=x_c[xb][:], in_=x_hbm[r0:r0 + C, :]),
               writes=[f"x{xb}"], dma=True, stream=f"x{xb}")
            OP("scalar", lambda sc, xb=xb: sc.activation(out=bscr[:], in_=x_c[xb][:],
               func=AF.Copy, bias=0.0, scale=1.0, accum_out=ln_sx[xb][:]),
               reads=[f"x{xb}", "cbias"], writes=["bscr", f"lsx{xb}"])
            OP("scalar", lambda sc, xb=xb: sc.activation(out=bscr[:], in_=x_c[xb][:],
               func=AF.Square, accum_out=ln_sxx[xb][:]),
               reads=[f"x{xb}", "cbias"], writes=["bscr", f"lsxx{xb}"])
            OP("gpsimd", lambda g, xb=xb: g.tensor_scalar(out=ln_m[xb][:],
               in0=ln_sx[xb][:], scalar1=1.0 / D, scalar2=None, op0=ALU.mult),
               reads=[f"lsx{xb}"], writes=[f"lm{xb}"])
            OP("gpsimd", lambda g, xb=xb: g.tensor_tensor(out=ln_v[xb][:],
               in0=ln_m[xb][:], in1=ln_m[xb][:], op=ALU.mult),
               reads=[f"lm{xb}"], writes=[f"lv{xb}"])
            OP("gpsimd", lambda g, xb=xb: g.tensor_scalar(out=ln_sxx[xb][:],
               in0=ln_sxx[xb][:], scalar1=1.0 / D, scalar2=None, op0=ALU.mult),
               reads=[f"lsxx{xb}"], writes=[f"lsxx{xb}"])
            OP("gpsimd", lambda g, xb=xb: g.tensor_tensor(out=ln_v[xb][:],
               in0=ln_sxx[xb][:], in1=ln_v[xb][:], op=ALU.subtract),
               reads=[f"lsxx{xb}", f"lv{xb}"], writes=[f"lv{xb}"])
            OP("scalar", lambda sc, xb=xb: sc.activation(out=ln_inv[xb][:],
               in_=ln_v[xb][:], func=AF.Ln, bias=EPS, scale=1.0),
               reads=[f"lv{xb}", "cbias"], writes=[f"linv{xb}"])
            OP("scalar", lambda sc, xb=xb: sc.activation(out=ln_inv[xb][:],
               in_=ln_inv[xb][:], func=AF.Exp, bias=0.0, scale=-0.5),
               reads=[f"linv{xb}", "cbias"], writes=[f"linv{xb}"])
            OP("gpsimd", lambda g, xb=xb: g.tensor_tensor(
               out=negml[:, xb:xb + 1], in0=ln_m[xb][:], in1=ln_inv[xb][:],
               op=ALU.mult),
               reads=[f"lm{xb}", f"linv{xb}"], writes=[f"negml{xb}"])
            OP("gpsimd", lambda g, xb=xb: g.tensor_scalar(
               out=negml[:, xb:xb + 1], in0=negml[:, xb:xb + 1],
               scalar1=-1.0, scalar2=None, op0=ALU.mult),
               reads=[f"negml{xb}"], writes=[f"negml{xb}"])
            OP("scalar", lambda sc, xb=xb: sc.activation(out=xn_tm[:],
               in_=x_c[xb][:], func=AF.Identity, bias=negml[:, xb:xb + 1],
               scale=ln_inv[xb][:]),
               reads=[f"x{xb}", f"negml{xb}", f"linv{xb}"], writes=["xntm"])
            for k in range(8):
                bank = k % 2
                OP("tensor", lambda t, k=k, bank=bank: t.transpose(
                   pb[bank][:, 0:128], xn_tm[:, k * 128:(k + 1) * 128], ident[:]),
                   reads=["xntm", "ident"], writes=[PB[bank]])
                k4 = k % 4
                OP("scalar", lambda sc, k4=k4, bank=bank: sc.activation(
                   out=xfull[:, k4 * 128:k4 * 128 + 128],
                   in_=pb[bank][:, 0:128],
                   func=AF.Copy, bias=0.0, scale=1.0),
                   reads=[PB[bank]], writes=[f"xfull{k4}"])
                OP("scalar", lambda sc, k=k, j=j, k4=k4: sc.activation(
                   out=xnf_h[k][:, j * C:(j + 1) * C],
                   in_=xfull[:, k4 * 128:k4 * 128 + 128],
                   func=AF.Copy, bias=0.0, scale=1.0),
                   reads=[f"xfull{k4}"], writes=[f"xnfh{k}_{j}"])
                OP("vector", lambda v, k=k, j=j, k4=k4: v.scalar_tensor_tensor(
                   out=xnf_l[k][:, j * C:(j + 1) * C],
                   in0=xfull[:, k4 * 128:k4 * 128 + 128],
                   scalar=1.0, in1=xnf_h[k][:, j * C:(j + 1) * C],
                   op0=ALU.mult, op1=ALU.subtract),
                   reads=[f"xfull{k4}", f"xnfh{k}_{j}"], writes=[f"xnf{k}_{j}"])

    emit_A0(0)
    emit_wgk1_split()
    emit_split(10 ** 6)

    for tile in range(n_tiles):
        t0 = tile * tile_tok
        if tile > 0:
            emit_A0(tile)

        xnf_tile = [f"xnf{k}_{j}" for k in range(8) for j in range(JC)]

        PW = min(512, tile_tok)
        for p5 in range(tile_tok // PW):
            bank = 4 + (p5 % 2)
            sl = slice(p5 * PW, p5 * PW + PW)

            def u_one(t, sl=sl, bank=bank):
                last = None
                first = True
                for k in range(8):
                    for (wgt, mov) in ((wgk1h, xnf_h), (wgk1h, xnf_l), (wgk1l, xnf_h)):
                        last = t.matmul(pb[bank][0:16, 0:PW],
                                        wgt[:, k * R:(k + 1) * R], mov[k][:, sl],
                                        start=first, stop=(k == 7 and wgt is wgk1l))
                        first = False
                return last
            OP("tensor", u_one, reads=["wgk1h", "wgk1l"] + xnf_tile,
               writes=[PB[bank]])
            OP("vector", lambda v, sl=sl, bank=bank: v.tensor_copy(
               u_sb[:, sl], pb[bank][0:16, 0:PW]),
               reads=[PB[bank]], writes=["u"])

        for h in range(n_heads):
            buf = h % 2
            if tile == 0 and h == 0:
                for c2 in range(2):
                    load_qk("wq", 0, c2)
                    load_qk("wk", 0, c2)
                for half in range(2):
                    load_vg("wv", 0, half, ext=SPLIT_PIECE_KEYS)
                    load_vg("wg", 0, half, ext=SPLIT_PIECE_KEYS)
                load_wo(0)

            for c2 in range(2):
                for (mat, wh, wl, dstf, dkey) in (
                        ("wq", wqh_cs[c2], wql_cs[c2], qf, "qf"),
                        ("wk", wkh_cs[c2], wkl_cs[c2], kf, "kf")):
                    whk = f"{mat}hcs{c2}"; wlk = f"{mat}lcs{c2}"
                    for p5 in range(tile_tok // PW):
                        bk = (0 if mat == "wq" else 2) + (p5 % 2)
                        sl = slice(p5 * PW, p5 * PW + PW)

                        def qk_one(t, wh=wh, wl=wl, sl=sl, bk=bk):
                            last = None
                            first = True
                            for k in range(8):
                                for (wgt, mov) in ((wh, xnf_h), (wh, xnf_l),
                                                   (wl, xnf_h)):
                                    last = t.matmul(
                                        pb[bk][:, 0:PW],
                                        wgt[:, k * 128:(k + 1) * 128],
                                        mov[k][:, sl],
                                        start=first,
                                        stop=(k == 7 and wgt is wl))
                                    first = False
                            return last
                        OP("tensor", qk_one, reads=[whk, wlk] + xnf_tile,
                           writes=[PB[bk]])
                        OP("vector", lambda v, dstf=dstf, c2=c2, sl=sl, bk=bk:
                           v.tensor_copy(
                           dstf[:, c2 * tile_tok + sl.start:
                                c2 * tile_tok + sl.stop], pb[bk][:, 0:PW]),
                           reads=[PB[bk]], writes=[f"{dkey}{c2}"])
                    if h + 1 < n_heads:
                        load_qk(mat, h + 1, c2)
                    elif tile + 1 < n_tiles:
                        load_qk(mat, 0, c2)

            for c2 in range(2):
                d8 = h * 2 + c2
                zb = 4 + c2

                def z_all(t, d8=d8, zb=zb):
                    last = None
                    for half in range(NH2):
                        last = t.matmul(
                            pb[zb][:, (half % 2) * 256:(half % 2) * 256 + 256],
                            wgk2[:, d8 * 128:(d8 + 1) * 128],
                            u_sb[:, half * 256:(half + 1) * 256],
                            start=True, stop=True)
                    return last
                OP("tensor", z_all, reads=["wgk2", "u"], writes=[PB[zb]])
                for half in range(NH2):
                    hb = half % 2
                    OP("scalar", lambda sc, c2=c2, d8=d8, half=half, hb=hb,
                       zb=zb: sc.activation(
                       out=spf[:, c2 * tile_tok + half * 256:
                               c2 * tile_tok + half * 256 + 256],
                       in_=pb[zb][:, hb * 256:hb * 256 + 256], func=AF.Exp,
                       bias=bgk2n[:, d8:d8 + 1], scale=-1.0),
                       reads=[PB[zb], "bgk2n"], writes=[f"sp{c2}"])
                    OP("scalar", lambda sc, c2=c2, half=half: sc.activation(
                       out=spf[:, c2 * tile_tok + half * 256:
                               c2 * tile_tok + half * 256 + 256],
                       in_=spf[:, c2 * tile_tok + half * 256:
                               c2 * tile_tok + half * 256 + 256],
                       func=AF.Ln, bias=1.0, scale=1.0),
                       reads=[f"sp{c2}", "cbias"], writes=[f"sp{c2}"])

            def emit_gates(j):
                jt = j * C
                x = j % 2
                for c2 in range(2):
                    OP("vector", lambda v, c2=c2, jt=jt, x=x: v.tensor_tensor_scan(
                       out=at[x][:, c2 * C:(c2 + 1) * C], data0=ones128[:],
                       data1=spf[:, c2 * tile_tok + jt:c2 * tile_tok + jt + C],
                       initial=0.0, op0=ALU.mult, op1=ALU.add),
                       reads=[f"sp{c2}", "ones128"], writes=[f"at{c2}_{x}"])
                    OP("scalar", lambda sc, c2=c2, x=x: sc.activation(
                       out=qt[x][:, c2 * C:(c2 + 1) * C],
                       in_=at[x][:, c2 * C:(c2 + 1) * C],
                       func=AF.Exp, bias=-LN16, scale=-1.0 / 16.0),
                       reads=[f"at{c2}_{x}", "cbias"], writes=[f"qt{c2}_{x}"])
                    OP("scalar", lambda sc, c2=c2, x=x: sc.activation(
                       out=kt[x][:, c2 * C:(c2 + 1) * C],
                       in_=at[x][:, c2 * C:(c2 + 1) * C],
                       func=AF.Exp, bias=0.0, scale=1.0 / 16.0),
                       reads=[f"at{c2}_{x}", "cbias"], writes=[f"kt{c2}_{x}"])
                    OP("scalar", lambda sc, c2=c2, x=x: sc.activation(
                       out=alast[x][:, c2:c2 + 1],
                       in_=at[x][:, (c2 + 1) * C - 1:(c2 + 1) * C],
                       func=AF.Exp, bias=0.0, scale=-1.0 / 16.0),
                       reads=[f"at{c2}_{x}", "cbias"], writes=[f"alast{c2}_{x}"])
                    OP("vector", lambda v, c2=c2, jt=jt, x=x: v.tensor_tensor(
                       out=qt[x][:, c2 * C:(c2 + 1) * C],
                       in0=qt[x][:, c2 * C:(c2 + 1) * C],
                       in1=qf[:, c2 * tile_tok + jt:c2 * tile_tok + jt + C],
                       op=ALU.mult),
                       reads=[f"qf{c2}", f"qt{c2}_{x}"], writes=[f"qt{c2}_{x}"])
                    OP("vector", lambda v, c2=c2, jt=jt, x=x: v.tensor_tensor(
                       out=kt[x][:, c2 * C:(c2 + 1) * C],
                       in0=kt[x][:, c2 * C:(c2 + 1) * C],
                       in1=kf[:, c2 * tile_tok + jt:c2 * tile_tok + jt + C],
                       op=ALU.mult),
                       reads=[f"kf{c2}", f"kt{c2}_{x}"], writes=[f"kt{c2}_{x}"])
                    OP("scalar", lambda sc, c2=c2, x=x: sc.activation(
                       out=khf[x][:, c2 * C:(c2 + 1) * C],
                       in_=kt[x][:, c2 * C:(c2 + 1) * C],
                       func=AF.Copy, bias=0.0, scale=alast[x][:, c2:c2 + 1]),
                       reads=[f"kt{c2}_{x}", f"alast{c2}_{x}"], writes=[f"khf{c2}_{x}"])

            emit_gates(0)

            for half in range(2):
                for j in range(JC):
                    bv = 2 + (j % 2)
                    bg = 4 + (j % 2)

                    def vg_one(t, j=j, half=half, bv=bv, bg=bg):
                        last = None
                        for k in range(8):
                            ksl = slice(k * 256, (k + 1) * 256)
                            xh = xnf_h[k][:, j * C:(j + 1) * C]
                            xl = xnf_l[k][:, j * C:(j + 1) * C]
                            last = t.matmul(pb[bv][:, 0:256], xh,
                                            wvh_cs[half][:, ksl],
                                            start=(k == 0), stop=False)
                            last = t.matmul(pb[bg][:, 0:256], xh,
                                            wgh_cs[half][:, ksl],
                                            start=(k == 0), stop=False)
                            last = t.matmul(pb[bv][:, 0:256], xh,
                                            wvl_cs[half][:, ksl],
                                            start=False, stop=False)
                            last = t.matmul(pb[bg][:, 0:256], xh,
                                            wgl_cs[half][:, ksl],
                                            start=False, stop=False)
                            last = t.matmul(pb[bv][:, 0:256], xl,
                                            wvh_cs[half][:, ksl],
                                            start=False, stop=(k == 7))
                            last = t.matmul(pb[bg][:, 0:256], xl,
                                            wgh_cs[half][:, ksl],
                                            start=False, stop=(k == 7))
                        return last
                    OP("tensor", vg_one,
                       reads=[f"wvhcs{half}", f"wvlcs{half}", f"wghcs{half}",
                              f"wglcs{half}"] + xnf_tile,
                       writes=[PB[bv], PB[bg]])
                    OP("vector", lambda v, j=j, half=half, bv=bv: v.tensor_copy(
                       vth[j][:, half * 256:half * 256 + 256], pb[bv][:, 0:256]),
                       reads=[PB[bv]], writes=[f"v{j}"])
                    OP("vector", lambda v, j=j, half=half, bg=bg: v.tensor_copy(
                       gth[j][:, half * 256:half * 256 + 256], pb[bg][:, 0:256]),
                       reads=[PB[bg]], writes=[f"g{j}"])
                if h + 1 < n_heads:
                    load_vg("wv", h + 1, half)
                    load_vg("wg", h + 1, half)
                elif half == 1:
                    for jp in range(2):
                        load_w12("w1", jp)
                        load_w12("w2", jp)

            def emit_otm_evac(jp):
                OP("scalar", lambda sc, jp=jp: sc.activation(
                   out=o_tm[jp % 2][:, 0:256], in_=pb[5][:, 0:256],
                   func=AF.Copy, bias=0.0, scale=1.0),
                   reads=[PB[5]], writes=[f"otm{jp % 2}"])
                OP("scalar", lambda sc, jp=jp: sc.activation(
                   out=o_tm[jp % 2][:, 256:512], in_=pb[6][:, 0:256],
                   func=AF.Copy, bias=0.0, scale=1.0),
                   reads=[PB[6]], writes=[f"otm{jp % 2}"])

            def emit_epilogue(jp):
                ot = o_tm[jp % 2]
                otk = f"otm{jp % 2}"
                OP("scalar", lambda sc, ot=ot: sc.activation(out=bscr[:, 0:HDV],
                   in_=ot[:], func=AF.Square, accum_out=ssum[:]),
                   reads=[otk], writes=["bscr", "ssum"])
                OP("scalar", lambda sc: sc.activation(out=srms[:], in_=ssum[:],
                   func=AF.Ln, bias=EPS, scale=1.0 / HDV),
                   reads=["ssum", "cbias"], writes=["srms"])
                OP("scalar", lambda sc: sc.activation(out=srms[:], in_=srms[:],
                   func=AF.Exp, bias=0.0, scale=-0.5),
                   reads=["srms", "cbias"], writes=["srms"])
                OP("scalar", lambda sc, jp=jp: sc.activation(out=sil_s[:],
                   in_=gth[jp][:], func=AF.Exp, bias=0.0, scale=-1.0),
                   reads=[f"g{jp}", "cbias"], writes=["sils"])
                OP("scalar", lambda sc: sc.activation(out=sil_s[:], in_=sil_s[:],
                   func=AF.Copy, bias=1.0, scale=1.0),
                   reads=["sils"], writes=["sils"])
                OP("vector", lambda v: v.reciprocal(sil_s[:], sil_s[:]),
                   reads=["sils"], writes=["sils"])
                OP("vector", lambda v, jp=jp: v.tensor_tensor(out=gg[:],
                   in0=gth[jp][:], in1=sil_s[:], op=ALU.mult),
                   reads=[f"g{jp}", "sils"], writes=["gg"])
                OP("vector", lambda v, ot=ot: v.scalar_tensor_tensor(out=ot[:],
                   in0=ot[:], scalar=srms[:], in1=gg[:],
                   op0=ALU.mult, op1=ALU.mult),
                   reads=[otk, "srms", "gg"], writes=[otk])

            def emit_og(jp):
                ot = o_tm[jp % 2]
                otk = f"otm{jp % 2}"

                def og_tr(t, ot=ot):
                    last = None
                    for dc in range(4):
                        last = t.transpose(pb[2][:, dc * 128:dc * 128 + 128],
                                           ot[:, dc * 128:(dc + 1) * 128], ident[:])
                    return last
                OP("tensor", og_tr, reads=[otk, "ident"], writes=[PB[2]])
                OP("scalar", lambda sc, jp=jp: sc.activation(
                   out=ogh_h[jp][:], in_=pb[2][:, 0:512],
                   func=AF.Copy, bias=0.0, scale=1.0),
                   reads=[PB[2]], writes=[f"ogh{jp}"])
                OP("vector", lambda v, jp=jp: v.scalar_tensor_tensor(
                   out=ogh_l[jp][:], in0=pb[2][:, 0:512], scalar=1.0,
                   in1=ogh_h[jp][:], op0=ALU.mult, op1=ALU.subtract),
                   reads=[PB[2], f"ogh{jp}"], writes=[f"og{jp}"])

            for j in range(JC):
                x = j % 2
                if j + 1 < JC:
                    emit_gates(j + 1)
                if j > 0:
                    emit_otm_evac(j - 1)

                def kh_tr(t, x=x):
                    last = None
                    for c2 in range(2):
                        last = t.transpose(pb[4][:, c2 * 128:c2 * 128 + 128],
                                           khf[x][:, c2 * C:(c2 + 1) * C], ident[:])
                    return last
                OP("tensor", kh_tr, reads=[f"khf0_{x}", f"khf1_{x}", "ident"],
                   writes=[PB[4]])
                OP("scalar", lambda sc: sc.activation(out=khtm[:, 0:256],
                   in_=pb[4][:, 0:256], func=AF.Copy, bias=0.0, scale=1.0),
                   reads=[PB[4]], writes=["khtm0", "khtm1"])

                def p_mms(t, x=x):
                    last = None
                    for c2 in range(2):
                        last = t.matmul(pb[7][:, 0:128],
                                        kt[x][:, c2 * C:(c2 + 1) * C],
                                        qt[x][:, c2 * C:(c2 + 1) * C],
                                        start=(c2 == 0), stop=(c2 == 1))
                    return last
                OP("tensor", p_mms,
                   reads=[f"qt0_{x}", f"qt1_{x}", f"kt0_{x}", f"kt1_{x}"],
                   writes=[PB[7]])
                OP("vector", lambda v: v.tensor_tensor(out=p_sm[:],
                   in0=pb[7][:, 0:128], in1=triu[:], op=ALU.mult),
                   reads=[PB[7], "triu"], writes=["psm"])

                def o_mms(t, j=j, h=h, x=x):
                    last = None
                    for half in range(2):
                        sl = slice(half * 256, half * 256 + 256)
                        last = t.matmul(pb[5 + half][:, 0:256], p_sm[:],
                                        vth[j][:, sl], start=True, stop=False)
                    for c2 in range(2):
                        for half in range(2):
                            sl = slice(half * 256, half * 256 + 256)
                            last = t.matmul(pb[5 + half][:, 0:256],
                                            qt[x][:, c2 * C:(c2 + 1) * C],
                                            Sst[h][c2][:, sl],
                                            start=False, stop=(c2 == 1))
                    return last
                OP("tensor", o_mms,
                   reads=["psm", f"v{j}", f"qt0_{x}", f"qt1_{x}",
                          f"S{h}0", f"S{h}1"],
                   writes=[PB[5], PB[6]])

                for c2 in range(2):
                    def s_mms(t, c2=c2, j=j):
                        last = None
                        for half in range(2):
                            sl = slice(half * 256, half * 256 + 256)
                            last = t.matmul(pb[7][:, sl],
                                            khtm[:, c2 * C:(c2 + 1) * C],
                                            vth[j][:, sl], start=True, stop=True)
                        return last
                    OP("tensor", s_mms, reads=[f"khtm{c2}", f"v{j}"],
                       writes=[PB[7]])
                    OP("vector", lambda v, h=h, c2=c2, x=x: v.scalar_tensor_tensor(
                       out=Sst[h][c2][:], in0=Sst[h][c2][:],
                       scalar=alast[x][:, c2:c2 + 1], in1=pb[7][:, 0:HDV],
                       op0=ALU.mult, op1=ALU.add),
                       reads=[f"S{h}{c2}", f"alast{c2}_{x}", PB[7]],
                       writes=[f"S{h}{c2}"])

                if j > 0:
                    emit_epilogue(j - 1)
                if j > 1:
                    emit_og(j - 2)

            emit_otm_evac(JC - 1)
            emit_epilogue(JC - 1)
            emit_og(JC - 2)
            emit_og(JC - 1)

            for j in range(JC):
                def wo_mms(t, j=j):
                    last = None
                    first = [True, True]
                    for dc in range(4):
                        oh = ogh_h[j][:, dc * 128:(dc + 1) * 128]
                        ol = ogh_l[j][:, dc * 128:(dc + 1) * 128]
                        for dpart in range(2):
                            bk = 2 + dpart
                            dsl = slice(dc * 512, (dc + 1) * 512)
                            last = t.matmul(pb[bk][:, 0:512], oh,
                                            woh_cs[dpart][:, dsl],
                                            start=first[dpart], stop=False)
                            first[dpart] = False
                            last = t.matmul(pb[bk][:, 0:512], oh,
                                            wol_cs[dpart][:, dsl],
                                            start=False, stop=False)
                            last = t.matmul(pb[bk][:, 0:512], ol,
                                            woh_cs[dpart][:, dsl],
                                            start=False, stop=(dc == 3))
                    return last
                OP("tensor", wo_mms,
                   reads=["wohcs0", "wolcs0", "wohcs1", "wolcs1",
                          f"ogh{j}", f"og{j}"],
                   writes=[PB[2], PB[3]])
                for dpart in range(2):
                    sl0 = dpart * 512
                    bank = 2 + dpart
                    if h == 0:
                        OP("vector", lambda v, j=j, sl0=sl0, bank=bank:
                           v.scalar_tensor_tensor(
                           out=attn_c[j][:, sl0:sl0 + 512],
                           in0=pb[bank][:, 0:512], scalar=1.0,
                           in1=b2bc[:, sl0:sl0 + 512],
                           op0=ALU.mult, op1=ALU.add),
                           reads=[PB[bank], "b2bc"], writes=[f"attn{j}"])
                    else:
                        OP("vector", lambda v, j=j, sl0=sl0, bank=bank:
                           v.scalar_tensor_tensor(
                           out=attn_c[j][:, sl0:sl0 + 512],
                           in0=pb[bank][:, 0:512], scalar=1.0,
                           in1=attn_c[j][:, sl0:sl0 + 512],
                           op0=ALU.mult, op1=ALU.add),
                           reads=[PB[bank], f"attn{j}"], writes=[f"attn{j}"])
            if h + 1 < n_heads:
                load_wo(h + 1)
            elif tile + 1 < n_tiles:
                load_wo(0)

        for grp in range(8):
            jp0 = grp * 2
            for jpi in range(2):
                jp = jp0 + jpi
                wb = jpi
                for b in range(2):
                    jw = 2 * jp + b
                    slot = jw % 4
                    bank = slot % 2

                    def f1_all(t, b=b, wb=wb, bank=bank):
                        last = None
                        first = True
                        for k in range(8):
                            co = k * 256 + b * 128
                            for (wgt, mov) in ((wvh_cs[wb], xnf_h),
                                               (wvh_cs[wb], xnf_l),
                                               (wvl_cs[wb], xnf_h)):
                                lo_pass = wgt is wvl_cs[wb]
                                w_ap = (wvl_cs[wb] if lo_pass else wvh_cs[wb])[:, co:co + 128]
                                last = t.matmul(pb[bank][:, 0:tile_tok], w_ap,
                                                mov[k][:, 0:tile_tok],
                                                start=first,
                                                stop=(k == 7 and lo_pass))
                                first = False
                        return last
                    OP("tensor", f1_all,
                       reads=[f"wvhcs{wb}", f"wvlcs{wb}"] + xnf_tile,
                       writes=[PB[bank]])
                    OP("vector", lambda v, jw=jw, bank=bank: v.tensor_scalar(
                       out=gg[:, 0:tile_tok], in0=pb[bank][:, 0:tile_tok],
                       scalar1=b1f[:, jw:jw + 1],
                       scalar2=0.0, op0=ALU.add, op1=ALU.max),
                       reads=[PB[bank], "b1f"], writes=["gg"])
                    OP("scalar", lambda sc, slot=slot: sc.activation(
                       out=h1h[:, slot * tile_tok:(slot + 1) * tile_tok],
                       in_=gg[:, 0:tile_tok],
                       func=AF.Copy, bias=0.0, scale=1.0),
                       reads=["gg"], writes=[f"h1h_{slot}"])
                    OP("vector", lambda v, slot=slot: v.scalar_tensor_tensor(
                       out=h1l[:, slot * tile_tok:(slot + 1) * tile_tok],
                       in0=gg[:, 0:tile_tok], scalar=1.0,
                       in1=h1h[:, slot * tile_tok:(slot + 1) * tile_tok],
                       op0=ALU.mult, op1=ALU.subtract),
                       reads=["gg", f"h1h_{slot}"], writes=[f"h1_{slot}"])
            for jpi in range(2):
                jp = jp0 + jpi
                if jp + 2 < 16:
                    load_w12("w1", jp + 2)
                elif tile + 1 < n_tiles:
                    load_vg("wv", 0, jp % 2)
            for slot in range(4):
                def f2_slot(t, slot=slot):
                    last = None
                    jpi, b = slot // 2, slot % 2
                    for j in range(JC):
                        hh = h1h[:, slot * tile_tok + j * C:slot * tile_tok + (j + 1) * C]
                        hl = h1l[:, slot * tile_tok + j * C:slot * tile_tok + (j + 1) * C]
                        for dh in range(2):
                            bk = 2 * j + dh
                            wsl = slice(b * 1024 + dh * 512, b * 1024 + dh * 512 + 512)
                            last = t.matmul(pb[bk][:, 0:512], hh,
                                            wgh_cs[jpi][:, wsl],
                                            start=(slot == 0), stop=False)
                            last = t.matmul(pb[bk][:, 0:512], hh,
                                            wgl_cs[jpi][:, wsl],
                                            start=False, stop=False)
                            last = t.matmul(pb[bk][:, 0:512], hl,
                                            wgh_cs[jpi][:, wsl],
                                            start=False, stop=(slot == 3))
                    return last
                OP("tensor", f2_slot,
                   reads=[f"wghcs{slot // 2}", f"wglcs{slot // 2}",
                          f"h1h_{slot}", f"h1_{slot}"],
                   writes=PB[0:8])
            for jpi in range(2):
                jp = jp0 + 2 + jpi
                if jp < 16:
                    load_w12("w2", jp)
                elif tile + 1 < n_tiles:
                    load_vg("wg", 0, jp % 2)
            if grp < 7:
                for j in range(JC):
                    for dh in range(2):
                        OP("vector", lambda v, j=j, dh=dh: v.scalar_tensor_tensor(
                           out=attn_c[j][:, dh * 512:dh * 512 + 512],
                           in0=pb[2 * j + dh][:, 0:512], scalar=1.0,
                           in1=attn_c[j][:, dh * 512:dh * 512 + 512],
                           op0=ALU.mult, op1=ALU.add),
                           reads=[PB[2 * j + dh], f"attn{j}"], writes=[f"attn{j}"])
                if grp == 6:
                    for j in range(JC):
                        ob = j % 2
                        r0 = t0 + j * C
                        OP("sync", lambda s, ob=ob, r0=r0: s.dma_start(
                           out=x_c[ob][:], in_=x_hbm[r0:r0 + C, :]),
                           writes=[f"x{ob}"], dma=True, stream=f"x{ob}")
                        OP("vector", lambda v, j=j, ob=ob: v.tensor_tensor(
                           out=attn_c[j][:], in0=attn_c[j][:], in1=x_c[ob][:],
                           op=ALU.add),
                           reads=[f"attn{j}", f"x{ob}"], writes=[f"attn{j}"])
            else:
                for j in range(JC):
                    r0 = t0 + j * C
                    for dh in range(2):
                        OP("vector", lambda v, j=j, dh=dh:
                           v.scalar_tensor_tensor(
                           out=attn_c[j][:, dh * 512:dh * 512 + 512],
                           in0=pb[2 * j + dh][:, 0:512], scalar=1.0,
                           in1=attn_c[j][:, dh * 512:dh * 512 + 512],
                           op0=ALU.mult, op1=ALU.add),
                           reads=[PB[2 * j + dh], f"attn{j}"], writes=[f"attn{j}"])
                    OP("sync", lambda s, j=j, r0=r0: s.dma_start(
                       out=out_hbm[r0:r0 + C, :], in_=attn_c[j][:]),
                       reads=[f"attn{j}"], dma=True, stream="out")
    p.wait_all_streams("sync")
    p.build()
    return nc, p


_CACHE = {}


def _get_program(n_tok):
    if n_tok not in _CACHE:
        _CACHE[n_tok] = build_program(n_tok)
    return _CACHE[n_tok]


def _layout_weights(Wq, Wk, Wv, Wg, Wgk1, Wgk2, bgk2, Wo, gnw, w1, b1, w2, b2):
    c = np.ascontiguousarray
    return {
        "wq_l": c(Wq.reshape(8, 128, 8, 128).transpose(2, 1, 0, 3).reshape(8, 128, 1024)),
        "wk_l": c(Wk.reshape(8, 128, 8, 128).transpose(2, 1, 0, 3).reshape(8, 128, 1024)),
        "wv_l": c(Wv.reshape(8, 128, 8, 256).transpose(2, 1, 0, 3).reshape(8, 128, 2048)),
        "wg_l": c(Wg.reshape(8, 128, 8, 256).transpose(2, 1, 0, 3).reshape(8, 128, 2048)),
        "wo_l": c(Wo.reshape(4, 4, 128, 2, 512).transpose(0, 3, 2, 1, 4).reshape(8, 128, 2048)),
        "w1_l": c(w1.reshape(8, 128, 16, 256).transpose(2, 1, 0, 3).reshape(16, 128, 2048)),
        "w2_l": c(w2.reshape(16, 2, 128, 1024).transpose(0, 2, 1, 3).reshape(16, 128, 2048)),
        "wgk1_l": c(Wgk1.reshape(8, 128, R).transpose(1, 0, 2).reshape(128, 8 * R)),
        "wgk2": c(Wgk2),
        "bgk2_l": c(bgk2.reshape(8, 128).T),
        "b1_l": c(b1.reshape(32, 128).T),
        "b2_l": c(b2.reshape(1, D)),
        "gnw_l": c(gnw.reshape(4, 128).T),
    }


def kernel(L0_attn_x, L0_attn_Wq, L0_attn_Wk, L0_attn_Wv, L0_attn_Wg,
           L0_attn_Wgk1, L0_attn_Wgk2, L0_attn_bgk2, L0_attn_Wo,
           L0_attn_g_norm_weight, L0_ffn_w1, L0_ffn_b1, L0_ffn_w2, L0_ffn_b2):
    x = np.ascontiguousarray(np.asarray(L0_attn_x, dtype=np.float32))
    B, N, _ = x.shape
    wmap = _layout_weights(
        np.asarray(L0_attn_Wq, np.float32), np.asarray(L0_attn_Wk, np.float32),
        np.asarray(L0_attn_Wv, np.float32), np.asarray(L0_attn_Wg, np.float32),
        np.asarray(L0_attn_Wgk1, np.float32), np.asarray(L0_attn_Wgk2, np.float32),
        np.asarray(L0_attn_bgk2, np.float32), np.asarray(L0_attn_Wo, np.float32),
        np.asarray(L0_attn_g_norm_weight, np.float32),
        np.asarray(L0_ffn_w1, np.float32), np.asarray(L0_ffn_b1, np.float32),
        np.asarray(L0_ffn_w2, np.float32), np.asarray(L0_ffn_b2, np.float32))
    nc, _ = _get_program(N)
    in_maps = [dict(wmap, x=x[b]) for b in range(B)]
    res = run_bass_kernel_spmd(nc, in_maps, core_ids=list(range(B)))
    out = np.stack([res.results[b]["out"] for b in range(B)], axis=0)
    return out
